# version 01
# baseline (speedup 1.0000x reference)
import contextlib
import math

import numpy as np

import concourse.bass as bass
import concourse.mybir as mybir
from concourse.bass_utils import run_bass_kernel_spmd

F32 = mybir.dt.float32
BF16 = mybir.dt.bfloat16
AF = mybir.ActivationFunctionType
AX = mybir.AxisListType.X
ALU = mybir.AluOpType

B, L, D = 2, 8192, 2048
NH, HD, RD, QR, KVR = 16, 128, 64, 768, 512
HF = 8192
NQ = 2048
SCALE = 1.0 / math.sqrt(HD + RD)
BSHIFT = 32.0
EPS = 1e-5
VS = 130
NKC = L // 128

ENGINES = ("tensor", "vector", "scalar", "sync", "gpsimd")


class Op:
    __slots__ = ("eng", "fn", "deps", "dma", "dkey", "needed", "ctr", "val", "tag")

    def __init__(self, eng, fn, deps, dma, dkey, tag):
        self.eng, self.fn, self.dma, self.tag = eng, fn, dma, tag
        self.dkey = dkey
        self.deps = [d for d in deps if d is not None]
        self.needed = False
        self.ctr = None
        self.val = None


class Graph:
    def __init__(self):
        self.streams = {e: [] for e in ENGINES}
        self.all = []

    def op(self, eng, fn, deps=(), dma=False, dkey=None, tag=""):
        if dma and dkey is None:
            dkey = "bulk"
        o = Op(eng, fn, deps, dma, dkey, tag)
        self.streams[eng].append(o)
        self.all.append(o)
        return o

    def check(self):
        done = set()
        idx = {e: 0 for e in ENGINES}
        while len(done) < len(self.all):
            progress = False
            for e in ENGINES:
                while idx[e] < len(self.streams[e]):
                    o = self.streams[e][idx[e]]
                    if all(d in done for d in o.deps):
                        done.add(o)
                        idx[e] += 1
                        progress = True
                    else:
                        break
            if not progress:
                stuck = []
                for e in ENGINES:
                    if idx[e] < len(self.streams[e]):
                        o = self.streams[e][idx[e]]
                        missing = [d.tag for d in o.deps if d not in done]
                        stuck.append(f"{e}[{idx[e]}] {o.tag} waits {missing[:3]}")
                raise RuntimeError("op graph deadlock: " + " | ".join(stuck))

    def emit(self, nc, es):
        cc, dc = {}, {}
        for e in ENGINES:
            cc[e] = [es.enter_context(nc.semaphore(f"c_{e}")), 0]
        for o in self.all:
            if o.dma:
                o.needed = True
            for d in o.deps:
                d.needed = True
        for e in ENGINES:
            for o in self.streams[e]:
                if o.needed:
                    if o.dma:
                        k = (e, o.dkey)
                        if k not in dc:
                            dc[k] = [es.enter_context(nc.semaphore(f"d_{e}_{o.dkey}")), 0]
                        c = dc[k]
                        c[1] += 16
                    else:
                        c = cc[e]
                        c[1] += 1
                    o.ctr = c
                    o.val = c[1]
        self.check()
        blk = es.enter_context(nc.Block("main"))
        graph = self

        def make_cb(e):
            def cb(eng):
                covered = {}
                for o in graph.streams[e]:
                    want = {}
                    for d in o.deps:
                        assert d.val is not None, f"dep {d.tag} of {o.tag} not inc'd"
                        key = id(d.ctr)
                        if want.get(key, (None, -1))[1] < d.val:
                            want[key] = (d.ctr, d.val)
                    for key, (ctr, val) in want.items():
                        if covered.get(key, -1) < val:
                            eng.wait_ge(ctr[0], val)
                            covered[key] = val
                    inst = o.fn(eng)
                    if o.needed:
                        assert inst is not None, f"op {o.tag} returned no instruction"
                        inst.then_inc(o.ctr[0], 16 if o.dma else 1)
            return cb

        blk.tensor(make_cb("tensor"))
        blk.vector(make_cb("vector"))
        blk.scalar(make_cb("scalar"))
        blk.sync(make_cb("sync"))
        blk.gpsimd(make_cb("gpsimd"))


def _r3(sb, lo, n, c):
    return sb[:, lo:lo + n * c].rearrange("p (a c) -> p a c", c=c)


def build_program(upto=4, dump=()):
    nc = bass.Bass("TRN2", target_bir_lowering=False)
    G = Graph()

    def dram(name, shape, dt, kind="Internal"):
        if name in dump and kind == "Internal":
            kind = "ExternalOutput"
        return nc.dram_tensor(name, shape, dt, kind=kind).ap()

    xb = dram("xb", [L, D], F32, "ExternalInput")
    xq = dram("xq", [NQ, D], F32, "ExternalInput")
    wq_down = dram("wq_down", [D, QR], F32, "ExternalInput")
    wq_up = dram("wq_up", [QR, NH * HD], F32, "ExternalInput")
    wq_rope = dram("wq_rope", [QR, NH * RD], F32, "ExternalInput")
    wkv_down = dram("wkv_down", [D, KVR], F32, "ExternalInput")
    wk_up = dram("wk_up", [KVR, NH * HD], F32, "ExternalInput")
    wv_up = dram("wv_up", [KVR, NH * HD], F32, "ExternalInput")
    wk_rope = dram("wk_rope", [D, RD], F32, "ExternalInput")
    wo = dram("wo", [NH * HD, D], F32, "ExternalInput")
    w1til = dram("w1til", [64, 16, 128, 128], F32, "ExternalInput")
    w3til = dram("w3til", [64, 16, 128, 128], F32, "ExternalInput")
    w2r = dram("w2r", [128, 64 * D], F32, "ExternalInput")
    h64i2 = dram("h64i2", [128, 128], F32, "ExternalInput")
    h128 = dram("h128", [128, 128], F32, "ExternalInput")
    idin = dram("idin", [128, 128], F32, "ExternalInput")
    cosq = dram("cosq", [64, NQ], F32, "ExternalInput")
    sinq = dram("sinq", [64, NQ], F32, "ExternalInput")
    cosk = dram("cosk", [64, L], F32, "ExternalInput")
    sink = dram("sink", [64, L], F32, "ExternalInput")
    maskp = dram("maskp", [128, 16 * 512], F32, "ExternalInput")

    out = dram("out", [NQ, D], F32, "ExternalOutput")

    w2i = dram("w2i", [HF, D], BF16)
    w2pp = dram("w2pp", [8, 64, 128, 256], BF16)
    qt_hbm = dram("qt_hbm", [NH, 192, NQ], BF16)
    kt_hbm = dram("kt_hbm", [NH, 128, L], BF16)
    kr_hbm = dram("kr_hbm", [64, L], BF16)
    v_hbm = dram("v_hbm", [NKC, 128, NH * VS], BF16)
    aot_hbm = dram("aot_hbm", [NH, 128, NQ], BF16)
    y1_hbm = dram("y1_hbm", [NQ, D], F32)
    hid_hbm = dram("hid_hbm", [16, 64, 128, 128], BF16)
    y2f_hbm = dram("y2f_hbm", [16, 128, D], F32)

    es = contextlib.ExitStack()
    with es:
        E = es.enter_context
        ident = E(nc.sbuf_tensor("ident", [128, 128], BF16))
        consts = E(nc.sbuf_tensor("consts", [128, 8], F32))
        CB, CID, CEPS, CSC, CNH, CHS = 0, 1, 2, 3, 4, 5
        stat = E(nc.sbuf_tensor("stat", [128, 8], F32))

        def ld(dst_ap, src_ap, eng="gpsimd", deps=(), dkey="bulk", tag="ld"):
            return G.op(eng, lambda g, d=dst_ap, s=src_ap: g.dma_start(d, s),
                        deps=deps, dma=True, dkey=dkey, tag=tag)

        p1 = contextlib.ExitStack()
        with p1:
            P = p1.enter_context
            qlat = P(nc.sbuf_tensor("qlat", [128, 6 * NQ], BF16))
            kvlat = P(nc.sbuf_tensor("kvlat", [128, 4 * L], BF16))

            def memset_consts(gp):
                gp.memset(consts[:, CB:CB + 1], -BSHIFT)
                gp.memset(consts[:, CID:CID + 1], 1.0 / 2048.0)
                gp.memset(consts[:, CEPS:CEPS + 1], EPS)
                gp.memset(consts[:, CSC:CSC + 1], SCALE)
                gp.memset(consts[:, CNH:CNH + 1], -0.5)
                return gp.memset(consts[:, CHS:CHS + 1], 1.0 / math.sqrt(8192.0))
            o_consts = G.op("gpsimd", memset_consts, tag="consts")

            s1ab = contextlib.ExitStack()
            with s1ab:
                Q = s1ab.enter_context
                w2in = Q(nc.sbuf_tensor("w2in", [128, 2 * 1024], BF16))
                w2st = Q(nc.sbuf_tensor("w2st", [128, 2 * 1024], BF16))
                h64_sb = Q(nc.sbuf_tensor("h64_sb", [128, 128], BF16))
                h128_sb = Q(nc.sbuf_tensor("h128_sb", [128, 128], BF16))
                xin = Q(nc.sbuf_tensor("xin", [128, 2 * D], BF16))
                xt = Q(nc.sbuf_tensor("xt", [128, 2 * 16 * 512], BF16))
                wqd_sb = Q(nc.sbuf_tensor("wqd_sb", [128, 16 * QR], BF16))
                wkvd_sb = Q(nc.sbuf_tensor("wkvd_sb", [128, 16 * KVR], BF16))
                wkr_sb = Q(nc.sbuf_tensor("wkr_sb", [128, 16 * RD], BF16))
                ckb = Q(nc.sbuf_tensor("ckb", [64, 2 * 512], F32))
                skb = Q(nc.sbuf_tensor("skb", [64, 2 * 512], F32))
                rtmpK = Q(nc.sbuf_tensor("rtmpK", [64, 3 * 512], F32))
                krst = Q(nc.sbuf_tensor("krst", [64, 2 * 512], BF16))

                o_h64 = ld(h64_sb[:, :], h64i2, dkey="setup", tag="h64")
                o_h128 = ld(h128_sb[:, :], h128, dkey="setup", tag="h128")
                o_id = ld(ident[:, :], idin, dkey="setup", tag="id")
                o_wqd = ld(_r3(wqd_sb, 0, 16, QR),
                           wq_down.rearrange("(n p) c -> p n c", p=128), dkey="setup", tag="wqd")
                o_wkvd = ld(_r3(wkvd_sb, 0, 16, KVR),
                            wkv_down.rearrange("(n p) c -> p n c", p=128), dkey="setup", tag="wkvd")
                o_wkr = ld(_r3(wkr_sb, 0, 16, RD),
                           wk_rope.rearrange("(n p) c -> p n c", p=128), dkey="setup", tag="wkr")
                o_setup = o_wkr

                ps0 = contextlib.ExitStack()
                with ps0:
                    w2ps = [ps0.enter_context(nc.psum_tensor(f"w2ps{i}", [128, 1024], F32))
                            for i in range(2)]
                    mm1, ev1, st1 = {}, {}, {}
                    st2 = {}
                    for it in range(128):
                        b1, hf = it // 2, it % 2
                        pi = it % 2
                        ld1 = ld(w2in[:, pi * 1024:(pi + 1) * 1024],
                                 w2r[:, b1 * D + 1024 * hf:b1 * D + 1024 * hf + 1024],
                                 deps=[mm1.get(it - 2)], dkey=f"w2ld1{it % 2}", tag=f"w2ld1.{it}")

                        def f_mm1(t, pi=pi):
                            inst = None
                            for dcn in range(2):
                                inst = t.matmul(
                                    w2ps[pi][:, 512 * dcn:512 * dcn + 512], h64_sb[:, :],
                                    w2in[:, pi * 1024 + 512 * dcn:pi * 1024 + 512 * dcn + 512],
                                    start=True, stop=True)
                            return inst
                        mm1[it] = G.op("tensor", f_mm1,
                                       deps=[ld1, o_setup, ev1.get(it - 2)], tag=f"w2mm1.{it}")
                        ev1[it] = G.op("vector",
                                       lambda v, pi=pi: v.tensor_scalar_mul(
                                           w2st[:, pi * 1024:(pi + 1) * 1024],
                                           w2ps[pi][:, :], consts[:, CHS:CHS + 1]),
                                       deps=[mm1[it], st1.get(it - 2), o_consts],
                                       tag=f"w2ev1.{it}")
                        st1[it] = ld(
                            w2i.rearrange("(p b) d -> p b d", b=64)[:, b1,
                                                                   1024 * hf:1024 * hf + 1024],
                            w2st[:, pi * 1024:(pi + 1) * 1024], eng="sync",
                            deps=[ev1[it]], dkey=f"w2st1{it % 2}", tag=f"w2st1.{it}")
                    mm2, ev2 = {}, {}
                    for it in range(128):
                        a, hf = it // 2, it % 2
                        pi = it % 2
                        ld2 = ld(w2in[:, pi * 1024:(pi + 1) * 1024],
                                 w2i[a * 128:(a + 1) * 128, 1024 * hf:1024 * hf + 1024],
                                 eng="sync", deps=[st1[127], st1[126], mm2.get(it - 2)],
                                 dkey=f"w2ld2{it % 2}", tag=f"w2ld2.{it}")

                        def f_mm2(t, pi=pi):
                            inst = None
                            for dcn in range(2):
                                inst = t.matmul(
                                    w2ps[pi][:, 512 * dcn:512 * dcn + 512], h128_sb[:, :],
                                    w2in[:, pi * 1024 + 512 * dcn:pi * 1024 + 512 * dcn + 512],
                                    start=True, stop=True)
                            return inst
                        mm2[it] = G.op("tensor", f_mm2,
                                       deps=[ld2, o_setup, ev2.get(it - 2)], tag=f"w2mm2.{it}")
                        ev2[it] = G.op("vector",
                                       lambda v, pi=pi: v.tensor_copy(
                                           w2st[:, pi * 1024:(pi + 1) * 1024], w2ps[pi][:, :]),
                                       deps=[mm2[it], st2.get((it - 2, 3))], tag=f"w2ev2.{it}")
                        for e4 in range(4):
                            e8 = 4 * hf + e4
                            st2[(it, e4)] = ld(
                                w2pp[e8, a, :, :],
                                w2st[:, pi * 1024 + 256 * e4:pi * 1024 + 256 * e4 + 256],
                                eng="scalar", deps=[ev2[it]], dkey=f"w2st2{it % 2}", tag=f"w2st2.{it}.{e4}")
                ph0_guard = ev2[127]
                w2pp_done = st2[(127, 3)]
                w2pp_done2 = st2[(126, 3)]

                ps1 = contextlib.ExitStack()
                with ps1:
                    tpsX = [ps1.enter_context(nc.psum_tensor(f"tps{i}", [128, 512], BF16))
                            for i in range(2)]
                    prjX = [ps1.enter_context(nc.psum_tensor(f"prj{i}", [128, 512], F32))
                            for i in range(2)]
                    prjK = ps1.enter_context(nc.psum_tensor("prjK", [64, 512], F32))

                    trops, evops, ldx = {}, {}, {}
                    ldck, ldsk = {}, {}
                    prj_users = {}
                    nprj = [0]
                    evql, evkv, ropeK, stkr = {}, {}, {}, {}

                    def qlat_consumer(ci, evdep):
                        n = ci
                        for mt in range(6):
                            pp = nprj[0] % 2
                            nprj[0] += 1

                            def f(t, mt=mt, pp=pp):
                                inst = None
                                for dd in range(16):
                                    inst = t.matmul(
                                        prjX[pp][:, :],
                                        wqd_sb[:, dd * QR + 128 * mt:dd * QR + 128 * mt + 128],
                                        xt[:, (ci % 2) * 8192 + 512 * dd:
                                           (ci % 2) * 8192 + 512 * dd + 512],
                                        start=(dd == 0), stop=(dd == 15))
                                return inst
                            mm = G.op("tensor", f,
                                      deps=[evdep, o_setup, prj_users.get(pp), ph0_guard],
                                      tag=f"qlmm.{n}.{mt}")
                            evql[(n, mt)] = G.op(
                                "vector",
                                lambda v, pp=pp, mt=mt, n=n: v.tensor_copy(
                                    qlat[:, 2048 * mt + 512 * n:2048 * mt + 512 * n + 512],
                                    prjX[pp][:, :]),
                                deps=[mm], tag=f"qlev.{n}.{mt}")
                            prj_users[pp] = evql[(n, mt)]

                    def kv_consumer(ci, evdep):
                        nb = ci - 4
                        for mt in range(4):
                            pp = nprj[0] % 2
                            nprj[0] += 1

                            def f(t, mt=mt, pp=pp, ci=ci):
                                inst = None
                                for dd in range(16):
                                    inst = t.matmul(
                                        prjX[pp][:, :],
                                        wkvd_sb[:, dd * KVR + 128 * mt:dd * KVR + 128 * mt + 128],
                                        xt[:, (ci % 2) * 8192 + 512 * dd:
                                           (ci % 2) * 8192 + 512 * dd + 512],
                                        start=(dd == 0), stop=(dd == 15))
                                return inst
                            mm = G.op("tensor", f,
                                      deps=[evdep, o_setup, prj_users.get(pp), ph0_guard],
                                      tag=f"kvmm.{nb}.{mt}")
                            evkv[(nb, mt)] = G.op(
                                "vector",
                                lambda v, pp=pp, mt=mt, nb=nb: v.tensor_copy(
                                    kvlat[:, 8192 * mt + 512 * nb:8192 * mt + 512 * nb + 512],
                                    prjX[pp][:, :]),
                                deps=[mm], tag=f"kvev.{nb}.{mt}")
                            prj_users[pp] = evkv[(nb, mt)]

                        ldck[nb] = ld(ckb[:, (nb % 2) * 512:(nb % 2) * 512 + 512],
                                      cosk[:, 512 * nb:512 * nb + 512],
                                      eng="sync", deps=[ropeK.get(("B", nb - 2))],
                                      dkey=f"ldck{nb % 2}", tag=f"ldck.{nb}")
                        ldsk[nb] = ld(skb[:, (nb % 2) * 512:(nb % 2) * 512 + 512],
                                      sink[:, 512 * nb:512 * nb + 512],
                                      eng="sync", deps=[ropeK.get(("D", nb - 2))],
                                      dkey=f"ldsk{nb % 2}", tag=f"ldsk.{nb}")

                        def f_kr(t, ci=ci):
                            inst = None
                            for dd in range(16):
                                inst = t.matmul(
                                    prjK[:, :],
                                    wkr_sb[:, dd * RD:dd * RD + RD],
                                    xt[:, (ci % 2) * 8192 + 512 * dd:
                                       (ci % 2) * 8192 + 512 * dd + 512],
                                    start=(dd == 0), stop=(dd == 15))
                            return inst
                        mmkr = G.op("tensor", f_kr,
                                    deps=[evdep, o_setup, ropeK.get(("D", nb - 1)), ph0_guard],
                                    tag=f"krmm.{nb}")
                        ropeK[("A", nb)] = G.op(
                            "vector",
                            lambda v: v.tensor_copy(rtmpK[:, 0:512], prjK[:, :]),
                            deps=[mmkr, ropeK.get(("D", nb - 1))], tag=f"krA.{nb}")

                        def f_rot(v):
                            v.tensor_copy(rtmpK[0:32, 512:1024], prjK[32:64, :])
                            return v.tensor_copy(rtmpK[32:64, 512:1024], prjK[0:32, :])
                        ropeK[("R", nb)] = G.op("vector", f_rot,
                                                deps=[mmkr, ropeK.get(("D", nb - 1))],
                                                tag=f"krR.{nb}")
                        ropeK[("B", nb)] = G.op(
                            "vector",
                            lambda v, nb=nb: v.tensor_mul(
                                rtmpK[:, 1024:1536], rtmpK[:, 0:512],
                                ckb[:, (nb % 2) * 512:(nb % 2) * 512 + 512]),
                            deps=[ropeK[("A", nb)], ldck[nb]], tag=f"krB.{nb}")

                        ropeK[("C", nb)] = G.op(
                            "vector",
                            lambda v, nb=nb: v.tensor_mul(
                                rtmpK[:, 512:1024], rtmpK[:, 512:1024],
                                skb[:, (nb % 2) * 512:(nb % 2) * 512 + 512]),
                            deps=[ropeK[("B", nb)], ropeK[("R", nb)], ldsk[nb]],
                            tag=f"krC.{nb}")
                        ropeK[("D", nb)] = G.op(
                            "vector",
                            lambda v, nb=nb: v.tensor_add(
                                krst[:, (nb % 2) * 512:(nb % 2) * 512 + 512],
                                rtmpK[:, 1024:1536], rtmpK[:, 512:1024]),
                            deps=[ropeK[("C", nb)], stkr.get(nb - 2)],
                            tag=f"krD.{nb}")
                        stkr[nb] = ld(kr_hbm[:, 512 * nb:512 * nb + 512],
                                      krst[:, (nb % 2) * 512:(nb % 2) * 512 + 512],
                                      eng="sync", deps=[ropeK[("D", nb)]],
                                      dkey=f"stkr{nb % 2}", tag=f"stkr.{nb}")

                    def x_chunk(ci, srcrows, consumers):
                        for mtl in range(4):
                            gt = 4 * ci + mtl
                            pig = gt % 2
                            ldx[gt] = ld(
                                xin[:, pig * D:(pig + 1) * D],
                                srcrows[128 * mtl:128 * mtl + 128, :],
                                deps=[trops.get(4 * (gt - 2) + 3)], dkey=f"ldx{gt % 2}", tag=f"ldx.{gt}")
                            for r in range(4):
                                R = 4 * gt + r
                                dd0 = 4 * r

                                def f_tr(t, pig=pig, dd0=dd0, R=R):
                                    inst = None
                                    for k in range(4):
                                        inst = t.transpose(
                                            tpsX[R % 2][:, 128 * k:128 * k + 128],
                                            xin[:, pig * D + 128 * (dd0 + k):
                                                pig * D + 128 * (dd0 + k) + 128],
                                            ident[:, :])
                                    return inst
                                trops[R] = G.op("tensor", f_tr,
                                                deps=[ldx[gt], o_setup, evops.get(R - 2),
                                                      ph0_guard],
                                                tag=f"tr.{R}")

                                def f_ev(v, ci=ci, mtl=mtl, dd0=dd0, R=R):
                                    dst = xt[:, (ci % 2) * 8192:(ci % 2) * 8192 + 8192]\
                                        .rearrange("p (a c) -> p a c", c=512)[
                                        :, dd0:dd0 + 4, 128 * mtl:128 * mtl + 128]
                                    return v.tensor_copy(
                                        dst, tpsX[R % 2].rearrange("p (a b) -> p a b", b=128))
                                evops[R] = G.op("vector", f_ev, deps=[trops[R]],
                                                tag=f"ev.{R}")
                        for c in consumers:
                            c(ci, evops[4 * (4 * ci + 3) + 3])

                    for n in range(4):
                        x_chunk(n, xq[512 * n:512 * (n + 1), :], [qlat_consumer])
                    for n in range(16):
                        x_chunk(4 + n, xb[512 * n:512 * (n + 1), :], [kv_consumer])

                    last_ql = evql[(3, 5)]
                    last_kv = evkv[(15, 3)]
                    last_kr = ropeK[("D", 15)]
                    ps1_guard = [last_ql, last_kv, last_kr, prj_users[0], prj_users[1],
                                 evops[4 * (4 * 19 + 3) + 3]]
                g1ab_last = [last_ql, last_kv, stkr[15], stkr[14], st1[127], st1[126],
                             w2pp_done, w2pp_done2,
                             trops[4 * (4 * 19 + 3) + 3], ps1_guard[-1]]

            s1cd = contextlib.ExitStack()
            with s1cd:
                Q = s1cd.enter_context
                wquh = Q(nc.sbuf_tensor("wquh", [128, 2 * 6 * 128], BF16))
                wqrh = Q(nc.sbuf_tensor("wqrh", [128, 2 * 6 * 64], BF16))
                wkuh = Q(nc.sbuf_tensor("wkuh", [128, 2 * 4 * 128], BF16))
                wvu_sb = Q(nc.sbuf_tensor("wvu_sb", [128, 4 * 2048], BF16))
                cq_sb = Q(nc.sbuf_tensor("cq_sb", [64, NQ], F32))
                sq_sb = Q(nc.sbuf_tensor("sq_sb", [64, NQ], F32))
                qtc_st = Q(nc.sbuf_tensor("qtc_st", [128, 2 * NQ], BF16))
                qtr_st = Q(nc.sbuf_tensor("qtr_st", [64, 2 * NQ], BF16))
                ktc_st = Q(nc.sbuf_tensor("ktc_st", [128, 2 * 2048], BF16))
                vstage = Q(nc.sbuf_tensor("vstage", [128, 2 * NH * VS], BF16))
                rtmp = Q(nc.sbuf_tensor("rtmp", [64, 3 * 2048], F32))

                o_cq = ld(cq_sb[:, :], cosq, deps=g1ab_last, dkey="g1cd", tag="cq")
                o_sq = ld(sq_sb[:, :], sinq, deps=g1ab_last, dkey="g1cd", tag="sq")
                o_wvu = ld(_r3(wvu_sb, 0, 4, 2048),
                           wv_up.rearrange("(n p) c -> p n c", p=128),
                           deps=g1ab_last, dkey="g1cd", tag="wvu")
                o_g1cd = o_wvu

                ps1c = contextlib.ExitStack()
                with ps1c:
                    qcps = [ps1c.enter_context(nc.psum_tensor(f"qcps{i}", [128, 512], F32))
                            for i in range(2)]
                    qrps = ps1c.enter_context(nc.psum_tensor("qrps", [64, 2048], F32))
                    stq = {}
                    ldqu, ldqr = {}, {}
                    qc_user = {}
                    rope_user = [None]
                    qc_lastmm, qr_lastmm = {}, {}
                    for h in range(NH):
                        pi = h % 2
                        ldqu[h] = ld(_r3(wquh, pi * 6 * 128, 6, 128),
                                     wq_up[:, 128 * h:128 * h + 128].rearrange(
                                         "(n p) c -> p n c", p=128),
                                     deps=[qc_lastmm.get(h - 2)] +
                                          (g1ab_last if h < 2 else []),
                                     dkey=f"ldqu{h % 2}", tag=f"ldqu.{h}")
                        ldqr[h] = ld(_r3(wqrh, pi * 6 * 64, 6, 64),
                                     wq_rope[:, 64 * h:64 * h + 64].rearrange(
                                         "(n p) c -> p n c", p=128),
                                     deps=[qr_lastmm.get(h - 2)] +
                                          (g1ab_last if h < 2 else []),
                                     dkey=f"ldqr{h % 2}", tag=f"ldqr.{h}")
                        evs = []
                        for cg in range(4):
                            pp = cg % 2

                            def f(t, h=h, cg=cg, pp=pp, pi=pi):
                                inst = None
                                for qq in range(6):
                                    inst = t.matmul(
                                        qcps[pp][:, :],
                                        wquh[:, pi * 768 + 128 * qq:pi * 768 + 128 * qq + 128],
                                        qlat[:, 2048 * qq + 512 * cg:2048 * qq + 512 * cg + 512],
                                        start=(qq == 0), stop=(qq == 5))
                                return inst
                            mm = G.op("tensor", f,
                                      deps=[last_ql, ldqu[h], qc_user.get(pp)] +
                                           (ps1_guard if h == 0 and cg == 0 else []),
                                      tag=f"qcmm.{h}.{cg}")
                            ev = G.op("scalar",
                                      lambda sc, cg=cg, pp=pp, pi=pi: sc.activation(
                                          qtc_st[:, pi * NQ + 512 * cg:pi * NQ + 512 * cg + 512],
                                          qcps[pp][:, :], AF.Copy, bias=0.0,
                                          scale=consts[:, CSC:CSC + 1]),
                                      deps=[mm, o_consts, stq.get((h - 2, 0))],
                                      tag=f"qcev.{h}.{cg}")
                            qc_user[pp] = ev
                            evs.append(ev)
                        qc_lastmm[h] = mm

                        def f_qr(t, h=h, pi=pi):
                            inst = None
                            for cg in range(4):
                                for qq in range(6):
                                    inst = t.matmul(
                                        qrps[:, 512 * cg:512 * cg + 512],
                                        wqrh[:, pi * 384 + 64 * qq:pi * 384 + 64 * qq + 64],
                                        qlat[:, 2048 * qq + 512 * cg:2048 * qq + 512 * cg + 512],
                                        start=(qq == 0), stop=(qq == 5))
                            return inst
                        mmqr = G.op("tensor", f_qr,
                                    deps=[last_ql, ldqr[h], rope_user[0]] +
                                         (ps1_guard if h == 0 else []),
                                    tag=f"qrmm.{h}")
                        oA_ = G.op("vector",
                                   lambda v: v.tensor_copy(rtmp[:, 0:2048], qrps[:, :]),
                                   deps=[mmqr, rope_user[0]] +
                                        (g1ab_last if h == 0 else []),
                                   tag=f"qrA.{h}")

                        def f_rot(v):
                            v.tensor_copy(rtmp[0:32, 2048:4096], qrps[32:64, :])
                            return v.tensor_copy(rtmp[32:64, 2048:4096], qrps[0:32, :])
                        oR_ = G.op("vector", f_rot, deps=[mmqr, rope_user[0]],
                                   tag=f"qrR.{h}")
                        oB_ = G.op("vector",
                                   lambda v: v.tensor_mul(rtmp[:, 4096:6144],
                                                          rtmp[:, 0:2048], cq_sb[:, :]),
                                   deps=[oA_, o_g1cd], tag=f"qrB.{h}")

                        oC_ = G.op("vector",
                                   lambda v: v.tensor_mul(rtmp[:, 2048:4096],
                                                          rtmp[:, 2048:4096], sq_sb[:, :]),
                                   deps=[oB_, oR_, o_g1cd], tag=f"qrC.{h}")
                        oD_ = G.op("vector",
                                   lambda v, pi=pi: v.tensor_add(
                                       qtr_st[:, pi * NQ:pi * NQ + NQ],
                                       rtmp[:, 4096:6144], rtmp[:, 2048:4096]),
                                   deps=[oC_, stq.get((h - 2, 1))], tag=f"qrD.{h}")
                        rope_user[0] = oD_
                        qr_lastmm[h] = mmqr
                        stq[(h, 0)] = ld(qt_hbm[h, 0:128, :],
                                         qtc_st[:, pi * NQ:(pi + 1) * NQ],
                                         eng="sync", deps=evs, dkey=f"stqc{h % 2}", tag=f"stqc.{h}")
                        stq[(h, 1)] = ld(qt_hbm[h, 128:192, :],
                                         qtr_st[:, pi * NQ:(pi + 1) * NQ],
                                         eng="sync", deps=[oD_], dkey=f"stqr{h % 2}", tag=f"stqr.{h}")
                    ps1c_guard = [qc_user[0], qc_user[1], rope_user[0]]

                ps1dk = contextlib.ExitStack()
                with ps1dk:
                    kcps = [ps1dk.enter_context(
                        nc.psum_tensor(f"kcps{i}", [128, 512], F32)) for i in range(2)]
                    stkc = {}
                    ldku = {}
                    kc_user = {}
                    kc_lastmm = {}
                    for h in range(NH):
                        ldku[h] = ld(_r3(wkuh, (h % 2) * 4 * 128, 4, 128),
                                     wk_up[:, 128 * h:128 * h + 128].rearrange(
                                         "(n p) c -> p n c", p=128),
                                     deps=[kc_lastmm.get(h - 2)] +
                                          (g1ab_last if h < 2 else []),
                                     dkey=f"ldku{h % 2}", tag=f"ldku.{h}")
                        for qq4 in range(4):
                            qg = 4 * h + qq4
                            sl = qg % 2
                            evs = []
                            for g2 in range(4):
                                g8 = 4 * qq4 + g2
                                pp = g8 % 2

                                def f(t, h=h, g8=g8, pp=pp):
                                    inst = None
                                    for kk in range(4):
                                        inst = t.matmul(
                                            kcps[pp][:, :],
                                            wkuh[:, (h % 2) * 512 + 128 * kk:
                                                 (h % 2) * 512 + 128 * kk + 128],
                                            kvlat[:, 8192 * kk + 512 * g8:
                                                  8192 * kk + 512 * g8 + 512],
                                            start=(kk == 0), stop=(kk == 3))
                                    return inst
                                mm = G.op("tensor", f,
                                          deps=[last_kv, ldku[h], kc_user.get(pp)] +
                                               (ps1c_guard if h == 0 and g8 == 0 else []),
                                          tag=f"kcmm.{h}.{g8}")
                                ev = G.op("vector",
                                          lambda v, g2=g2, pp=pp, sl=sl: v.tensor_copy(
                                              ktc_st[:, sl * 2048 + 512 * g2:
                                                     sl * 2048 + 512 * g2 + 512],
                                              kcps[pp][:, :]),
                                          deps=[mm, stkc.get(qg - 2)],
                                          tag=f"kcev.{h}.{g8}")
                                kc_user[pp] = ev
                                evs.append(ev)
                            stkc[qg] = ld(kt_hbm[h, 0:128, 2048 * qq4:2048 * qq4 + 2048]
                                          if False else
                                          kt_hbm[h, :, 2048 * qq4:2048 * qq4 + 2048],
                                          ktc_st[:, sl * 2048:(sl + 1) * 2048],
                                          eng="sync", deps=evs, dkey=f"stkc{qg % 2}", tag=f"stkc.{qg}")
                        kc_lastmm[h] = mm
                    ps1dk_guard = [kc_user[0], kc_user[1]]

                ps1dv = contextlib.ExitStack()
                with ps1dv:
                    vps = [ps1dv.enter_context(
                        nc.psum_tensor(f"vps{i}", [128, 2048], F32)) for i in range(2)]

                    def f_ones(gp):
                        for pi2 in range(2):
                            base = pi2 * NH * VS
                            inst = gp.memset(
                                _r3(vstage, base, NH, VS)[:, :, 128:130], 1.0)
                        return inst
                    o_ones = G.op("gpsimd", f_ones, deps=g1ab_last, tag="vones")

                    stv, evv, mmv = {}, {}, {}
                    for tt in range(NKC):
                        pi = tt % 2

                        def f(t, tt=tt, pi=pi):
                            inst = None
                            for kk in range(4):
                                for hg in range(4):
                                    inst = t.matmul(
                                        vps[pi][:, 512 * hg:512 * hg + 512],
                                        kvlat[:, 8192 * kk + 128 * tt:8192 * kk + 128 * tt + 128],
                                        wvu_sb[:, kk * 2048 + 512 * hg:kk * 2048 + 512 * hg + 512],
                                        start=(kk == 0), stop=(kk == 3))
                            return inst
                        mmv[tt] = G.op("tensor", f,
                                       deps=[last_kv, o_g1cd, evv.get(tt - 2)] +
                                            (ps1dk_guard if tt == 0 else []),
                                       tag=f"vmm.{tt}")

                        def f_ev(v, tt=tt, pi=pi):
                            inst = None
                            for hg in range(4):
                                dst = _r3(vstage, pi * NH * VS + VS * 4 * hg, 4, VS)[:, :, 0:128]
                                inst = v.tensor_copy(
                                    dst,
                                    vps[pi].rearrange("p (a b) -> p a b", b=128)[:, 4 * hg:4 * hg + 4, :])
                            return inst
                        evv[tt] = G.op("vector", f_ev,
                                       deps=[mmv[tt], o_ones, stv.get(tt - 2)],
                                       tag=f"vev.{tt}")
                        stv[tt] = ld(v_hbm[tt], vstage[:, pi * NH * VS:(pi + 1) * NH * VS],
                                     eng="sync", deps=[evv[tt]], dkey=f"stv{tt % 2}", tag=f"stv.{tt}")
                    ps1dv_guard = [evv[NKC - 1], evv[NKC - 2]]

                ph1_last = [stv[NKC - 1], stv[NKC - 2], stq[(NH - 1, 1)], stq[(NH - 2, 1)],
                            stq[(NH - 1, 0)], stq[(NH - 2, 0)],
                            stkc[4 * NH - 1], stkc[4 * NH - 2],
                            w2pp_done, w2pp_done2, stkr[15], stkr[14]] + ps1dv_guard

        if upto < 2:
            _finish_stub(G, nc, es, out, ph1_last)
            G.emit(nc, es)
            return nc

        p2 = contextlib.ExitStack()
        with p2:
            P = p2.enter_context
            qtc_b = P(nc.sbuf_tensor("qtc_b", [128, 2 * NQ], BF16))
            qtr_b = P(nc.sbuf_tensor("qtr_b", [64, 2 * NQ], BF16))
            ktc_b = P(nc.sbuf_tensor("ktc_b", [128, 2 * L], BF16))
            ktr_b = P(nc.sbuf_tensor("ktr_b", [64, L], BF16))
            v_b = P(nc.sbuf_tensor("v_b", [128, 2 * NKC * VS], BF16))
            pt_b = P(nc.sbuf_tensor("pt_b", [128, 2 * 1024], BF16))
            mask_sb = P(nc.sbuf_tensor("mask_sb", [128, 16 * 512], BF16))
            obf = P(nc.sbuf_tensor("obf", [128, 2 * NQ], BF16))
            rs_sb = P(nc.sbuf_tensor("rs_sb", [128, 8], F32))
            ssum_sb = P(nc.sbuf_tensor("ssum_sb", [128, 8], F32))
            aot_st = P(nc.sbuf_tensor("aot_st", [128, 2 * NQ], BF16))

            stp = [P(nc.psum_tensor(f"st{i}", [128, 1024], F32)) for i in range(2)]
            oA = P(nc.psum_tensor("oA", [128, 512], F32))
            oB = P(nc.psum_tensor("oB", [128, 512], F32))
            tp2 = [P(nc.psum_tensor(f"tp2{i}", [128, 512], BF16)) for i in range(2)]
            OFF = [0, 144]

            o_mask = ld(mask_sb[:, :], maskp, deps=ph1_last, dkey="g2", tag="maskld")
            o_ktr = ld(ktr_b[:, :], kr_hbm, deps=ph1_last, dkey="g2", tag="ktrld")
            o_g2 = o_ktr

            lq, lk, lv = {}, {}, {}
            sc_last, pv_last = {}, {}
            exop, pvop, scop = {}, {}, {}
            sfin, trh, evh, sta = {}, {}, {}, {}
            Pg = 0
            for h in range(NH):
                pi = h % 2
                dep_prev = [sc_last.get(h - 2), pv_last.get(h - 2),
                            trh.get((h - 2, 3))] + (ph1_last if h < 2 else [])
                lq[h] = ld(qtc_b[:, pi * NQ:(pi + 1) * NQ], qt_hbm[h, 0:128, :],
                           eng="sync", deps=dep_prev, dkey=f"l2h{h % 2}", tag=f"lqc.{h}")
                lq[(h, 1)] = ld(qtr_b[:, pi * NQ:(pi + 1) * NQ], qt_hbm[h, 128:192, :],
                                eng="sync", deps=dep_prev, dkey=f"l2h{h % 2}", tag=f"lqr.{h}")
                lk[h] = ld(ktc_b[:, pi * L:(pi + 1) * L], kt_hbm[h],
                           eng="sync", deps=dep_prev, dkey=f"l2h{h % 2}", tag=f"lkc.{h}")
                for part in range(8):
                    src = v_hbm[8 * part:8 * part + 8, :, VS * h:VS * h + VS]\
                        .rearrange("a p c -> p a c")
                    dst = _r3(v_b, pi * NKC * VS + 8 * part * VS, 8, VS)
                    lv[(h, part)] = ld(dst, src, eng="sync", deps=dep_prev,
                                       dkey=f"l2h{h % 2}", tag=f"lv.{h}.{part}")

                for j in range(4):
                    jg = 4 * h + j
                    nch = 16 * (j + 1)
                    for p in range(nch // 2):
                        pp = Pg % 2
                        masked = [c for c in (2 * p, 2 * p + 1) if c >= 16 * j]

                        def f_sc(t, p=p, j=j, pp=pp, pi=pi):
                            inst = None
                            for half in range(2):
                                c = 2 * p + half
                                cs = slice(512 * half, 512 * half + 512)
                                t.matmul(stp[pp][:, cs],
                                         ktc_b[:, pi * L + 128 * c:pi * L + 128 * c + 128],
                                         qtc_b[:, pi * NQ + 512 * j:pi * NQ + 512 * j + 512],
                                         start=True, stop=False)
                                inst = t.matmul(
                                    stp[pp][:, cs],
                                    ktr_b[:, 128 * c:128 * c + 128],
                                    qtr_b[:, pi * NQ + 512 * j:pi * NQ + 512 * j + 512],
                                    start=False, stop=True)
                            return inst
                        scop[Pg] = G.op("tensor", f_sc,
                                        deps=[lv[(h, 7)], o_g2,
                                              exop.get(Pg - 2)] +
                                             ([ph1_last[0]] if Pg == 0 else []),
                                        tag=f"sc.{Pg}")
                        exop[Pg] = G.op("scalar",
                                        lambda s, pp=pp: s.activation(
                                            pt_b[:, pp * 1024:pp * 1024 + 1024],
                                            stp[pp][:, :], AF.Exp,
                                            bias=consts[:, CB:CB + 1], scale=1.0),
                                        deps=[scop[Pg], o_consts, pvop.get(Pg - 2)],
                                        tag=f"ex.{Pg}")
                        mkop = None
                        if masked:
                            def f_mk(v, masked=tuple(masked), pp=pp, j=j, p=p):
                                inst = None
                                for c in masked:
                                    half = c - 2 * p
                                    cc = c - 16 * j
                                    inst = v.tensor_mul(
                                        pt_b[:, pp * 1024 + 512 * half:
                                             pp * 1024 + 512 * half + 512],
                                        pt_b[:, pp * 1024 + 512 * half:
                                             pp * 1024 + 512 * half + 512],
                                        mask_sb[:, 512 * cc:512 * cc + 512])
                                return inst
                            mkop = G.op("vector", f_mk, deps=[exop[Pg], o_g2],
                                        tag=f"mk.{Pg}")

                        def f_pv(t, p=p, pp=pp, pi=pi, nch=nch):
                            inst = None
                            for half in range(2):
                                c = 2 * p + half
                                for u in range(4):
                                    inst = t.matmul(
                                        (oA if u < 2 else oB)[:, OFF[u % 2]:OFF[u % 2] + 129],
                                        pt_b[:, pp * 1024 + 512 * half + 128 * u:
                                             pp * 1024 + 512 * half + 128 * u + 128],
                                        v_b[:, pi * NKC * VS + VS * c:
                                            pi * NKC * VS + VS * c + 129],
                                        start=(c == 0 and u % 2 == 0),
                                        stop=(c == nch - 1 and u % 2 == 1),
                                        skip_group_check=True)
                            return inst
                        pvdeps = [exop[Pg], mkop, lv[(h, 7)]]
                        if p == 0:
                            pvdeps.append(sfin.get(jg - 1))
                        pvop[Pg] = G.op("tensor", f_pv, deps=pvdeps, tag=f"pv.{Pg}")
                        Pg += 1

                    jp = jg % 2
                    vs1 = G.op("vector",
                               lambda v, jp=jp: v.tensor_copy(
                                   ssum_sb[:, 4 * jp:4 * jp + 2], oA[:, 128:273:144]),
                               deps=[pvop[Pg - 1], sfin.get(jg - 1)], tag=f"vs1.{jg}")
                    vs2 = G.op("vector",
                               lambda v, jp=jp: v.tensor_copy(
                                   ssum_sb[:, 4 * jp + 2:4 * jp + 4], oB[:, 128:273:144]),
                               deps=[pvop[Pg - 1], sfin.get(jg - 1)], tag=f"vs2.{jg}")
                    vrec = G.op("vector",
                                lambda v, jp=jp: v.reciprocal(
                                    rs_sb[:, 4 * jp:4 * jp + 4],
                                    ssum_sb[:, 4 * jp:4 * jp + 4]),
                                deps=[vs1, vs2], tag=f"vrec.{jg}")

                    def f_fin(s, j=j, pi=pi, jp=jp):
                        inst = None
                        for u in range(4):
                            ob = oA if u < 2 else oB
                            ui = (0 if u < 2 else 2) + u % 2
                            inst = s.activation(
                                obf[:, pi * NQ + 512 * j + 128 * u:
                                    pi * NQ + 512 * j + 128 * u + 128],
                                ob[:, OFF[u % 2]:OFF[u % 2] + 128],
                                AF.Copy, bias=0.0,
                                scale=rs_sb[:, 4 * jp + ui:4 * jp + ui + 1])
                        return inst
                    sfin[jg] = G.op("scalar", f_fin,
                                    deps=[vrec, trh.get((h - 2, 3))], tag=f"sfin.{jg}")
                for r in range(4):
                    def f_tr(t, r=r, pi=pi):
                        inst = None
                        for k in range(4):
                            u = 4 * r + k
                            inst = t.transpose(
                                tp2[r % 2][:, 128 * k:128 * k + 128],
                                obf[:, pi * NQ + 128 * u:pi * NQ + 128 * u + 128],
                                ident[:, :])
                        return inst
                    trh[(h, r)] = G.op("tensor", f_tr,
                                       deps=[sfin[4 * h + 3], evh.get((h, r - 2)), o_id],
                                       tag=f"trh.{h}.{r}")
                    evh[(h, r)] = G.op("vector",
                                       lambda v, r=r, pi=pi: v.tensor_copy(
                                           aot_st[:, pi * NQ + 512 * r:pi * NQ + 512 * r + 512],
                                           tp2[r % 2][:, :]),
                                       deps=[trh[(h, r)], sta.get(h - 2)],
                                       tag=f"evh.{h}.{r}")
                sta[h] = ld(aot_hbm[h], aot_st[:, pi * NQ:(pi + 1) * NQ], eng="scalar",
                            deps=[evh[(h, 3)]], dkey=f"sta{h % 2}", tag=f"sta.{h}")
                sc_last[h] = scop[Pg - 1]
                pv_last[h] = pvop[Pg - 1]
            ph2_last = [sta[NH - 1], sta[NH - 2], pvop[Pg - 1], sfin[4 * NH - 1],
                        trh[(NH - 1, 3)], evh[(NH - 1, 3)]]
            ph2_guard = [exop[Pg - 1], exop[Pg - 2], sfin[4 * NH - 1],
                         evh[(NH - 1, 3)], evh[(NH - 1, 2)]]

        if upto < 3:
            _finish_stub(G, nc, es, out, ph2_last)
            G.emit(nc, es)
            return nc

        p34 = contextlib.ExitStack()
        y1t_sb = p34.enter_context(nc.sbuf_tensor("y1t_sb", [128, 16 * NQ], BF16))
        p3 = contextlib.ExitStack()
        with p3:
            P = p3.enter_context
            wo_sb = P(nc.sbuf_tensor("wo_sb", [128, 16 * D], BF16))
            aotm = P(nc.sbuf_tensor("aotm", [128, 2 * D], BF16))
            xq_sb = P(nc.sbuf_tensor("xq_sb", [128, 2 * D], F32))
            ypre = P(nc.sbuf_tensor("ypre", [128, D], F32))
            sqd = P(nc.sbuf_tensor("sqd", [128, D], F32))
            y1f = P(nc.sbuf_tensor("y1f", [128, 2 * D], F32))
            y1b = P(nc.sbuf_tensor("y1b", [128, D], BF16))
            y1ps = P(nc.psum_tensor("y1ps", [128, 2048], F32))
            tp3 = [P(nc.psum_tensor(f"tp3{i}", [128, 512], BF16)) for i in range(2)]

            o_wo = ld(_r3(wo_sb, 0, 16, D),
                      wo.rearrange("(n p) c -> p n c", p=128),
                      deps=ph2_last, dkey="g3", tag="wold")
            xqv = xq.rearrange("(n p) d -> n p d", p=128)
            y1v = y1_hbm.rearrange("(n p) d -> n p d", p=128)
            lnx = {}
            sty1 = {}
            evt3 = {}
            prev_mm = {}
            for m in range(16):
                pi = m % 2
                lda = []
                for q4 in range(4):
                    src = aot_hbm[4 * q4:4 * q4 + 4, :, 128 * m:128 * m + 128]\
                        .rearrange("a p c -> p a c")
                    dst = _r3(aotm, pi * D + 512 * q4, 4, 128)
                    lda.append(ld(dst, src, eng="sync",
                                  deps=[prev_mm.get(m - 2)] + (ph2_last if m < 2 else []),
                                  dkey=f"lda{m % 2}", tag=f"lda.{m}.{q4}"))
                ldx3 = ld(xq_sb[:, pi * D:(pi + 1) * D], xqv[m],
                          eng="scalar", deps=[lnx.get((m - 2, "evy"))] +
                          (ph2_last if m < 2 else []), dkey=f"ldx3{m % 2}",
                          tag=f"ldx3.{m}")

                def f_mm(t, pi=pi):
                    inst = None
                    for hh in range(16):
                        for dg in range(4):
                            inst = t.matmul(
                                y1ps[:, 512 * dg:512 * dg + 512],
                                aotm[:, pi * D + 128 * hh:pi * D + 128 * hh + 128],
                                wo_sb[:, hh * D + 512 * dg:hh * D + 512 * dg + 512],
                                start=(hh == 0), stop=(hh == 15))
                    return inst
                mm = G.op("tensor", f_mm,
                          deps=[lda[3], o_wo, lnx.get((m - 1, "evy"))] +
                               (ph2_guard if m == 0 else []),
                          tag=f"womm.{m}")
                prev_mm[m] = mm

                def f_evy(v, pi=pi):
                    inst = None
                    for dg in range(4):
                        inst = v.tensor_add(
                            ypre[:, 512 * dg:512 * dg + 512],
                            y1ps[:, 512 * dg:512 * dg + 512],
                            xq_sb[:, pi * D + 512 * dg:pi * D + 512 * dg + 512])
                    return inst
                evy = G.op("vector", f_evy,
                           deps=[mm, ldx3, lnx.get((m - 1, "l9")),
                                 lnx.get((m - 1, "l3"))],
                           tag=f"evy.{m}")
                lnx[(m, "evy")] = evy
                _ln_chain(G, "p3", m, evy, ypre, sqd, stat, consts,
                          (CID, CEPS, CNH), y1f, pi, lnx, AXP=AX,
                          war=sty1.get(m - 2))
                l9 = G.op("vector",
                          lambda v, pi=pi: v.tensor_copy(
                              y1b[:, :], y1f[:, pi * D:(pi + 1) * D]),
                          deps=[lnx[(m, "l8")], evt3.get((m - 1, 3))],
                          tag=f"p3ln9.{m}")
                lnx[(m, "l9")] = l9
                sty1[m] = ld(y1v[m], y1f[:, pi * D:(pi + 1) * D], eng="scalar",
                             deps=[lnx[(m, "l8")]], dkey=f"sty1{m % 2}",
                             tag=f"sty1.{m}")
                for r in range(4):
                    def f_tr(t, r=r):
                        inst = None
                        for k in range(4):
                            dd = 4 * r + k
                            inst = t.transpose(tp3[r % 2][:, 128 * k:128 * k + 128],
                                               y1b[:, 128 * dd:128 * dd + 128],
                                               ident[:, :])
                        return inst
                    tr = G.op("tensor", f_tr,
                              deps=[l9, evt3.get((m, r - 2)), evt3.get((m - 1, r + 2)),
                                    o_id] + (ph2_guard if m == 0 and r == 0 else []),
                              tag=f"tr3.{m}.{r}")

                    def f_ev(v, r=r, m=m):
                        dst = y1t_sb.rearrange("p (a c) -> p a c", c=2048)[
                            :, 4 * r:4 * r + 4, 128 * m:128 * m + 128]
                        return v.tensor_copy(
                            dst, tp3[r % 2].rearrange("p (a b) -> p a b", b=128))
                    evt3[(m, r)] = G.op("vector", f_ev, deps=[tr], tag=f"evt3.{m}.{r}")
            ph3_last = [sty1[15], sty1[14]] + [evt3[(15, r)] for r in range(4)] + \
                [lnx[(15, "evy")], lnx[(15, "l9")]]
            ph3_guard = [lnx[(15, "evy")], lnx[(15, "l3")],
                         evt3[(15, 2)], evt3[(15, 3)]]

        if upto < 4:
            _finish_stub(G, nc, es, out, ph3_last)
            G.emit(nc, es)
            return nc

        hmuls, sth = {}, {}
        p4 = contextlib.ExitStack()
        with p4:
            P = p4.enter_context
            w1c = P(nc.sbuf_tensor("w1c", [128, 2 * D], BF16))
            w3c = P(nc.sbuf_tensor("w3c", [128, 2 * D], BF16))
            silu_sb = P(nc.sbuf_tensor("silu_sb", [128, 2 * 1024], F32))
            hidst = P(nc.sbuf_tensor("hidst", [128, 2 * D], BF16))
            gps = [P(nc.psum_tensor(f"gps{i}", [128, 1024], F32)) for i in range(2)]
            lps = [P(nc.psum_tensor(f"lps{i}", [128, 1024], F32)) for i in range(2)]

            for hh in range(64):
                pi = hh % 2
                wdep = [hmuls.get((hh - 2, 1))] + (ph3_last if hh < 2 else [])
                ldw1 = ld(_r3(w1c, pi * D, 16, 128),
                          w1til[hh].rearrange("n p c -> p n c"), deps=wdep, dkey=f"ldw1{hh % 2}", tag=f"ldw1.{hh}")
                ldw3 = ld(_r3(w3c, pi * D, 16, 128),
                          w3til[hh].rearrange("n p c -> p n c"), deps=wdep, dkey=f"ldw3{hh % 2}", tag=f"ldw3.{hh}")
                for half in range(2):
                    def f_g(t, pi=pi, half=half):
                        inst = None
                        for dd in range(16):
                            for tg in range(2):
                                inst = t.matmul(
                                    gps[half][:, 512 * tg:512 * tg + 512],
                                    w1c[:, pi * D + 128 * dd:pi * D + 128 * dd + 128],
                                    y1t_sb[:, 2048 * dd + 1024 * half + 512 * tg:
                                           2048 * dd + 1024 * half + 512 * tg + 512],
                                    start=(dd == 0), stop=(dd == 15))
                        return inst
                    depg = [ldw1, hmuls.get((hh - 1, half))]
                    if hh == 0 and half == 0:
                        depg += ph3_guard
                    mmg = G.op("tensor", f_g, deps=depg, tag=f"mmg.{hh}.{half}")

                    def f_l(t, pi=pi, half=half):
                        inst = None
                        for dd in range(16):
                            for tg in range(2):
                                inst = t.matmul(
                                    lps[half][:, 512 * tg:512 * tg + 512],
                                    w3c[:, pi * D + 128 * dd:pi * D + 128 * dd + 128],
                                    y1t_sb[:, 2048 * dd + 1024 * half + 512 * tg:
                                           2048 * dd + 1024 * half + 512 * tg + 512],
                                    start=(dd == 0), stop=(dd == 15))
                        return inst
                    mml = G.op("tensor", f_l,
                               deps=[ldw3, hmuls.get((hh - 1, half))],
                               tag=f"mml.{hh}.{half}")
                    sil = G.op("scalar",
                               lambda s, half=half: s.activation(
                                   silu_sb[:, 1024 * half:1024 * half + 1024],
                                   gps[half][:, :], AF.Silu),
                               deps=[mmg, hmuls.get((hh - 1, half))],
                               tag=f"sil.{hh}.{half}")
                    hmuls[(hh, half)] = G.op(
                        "vector",
                        lambda v, half=half, pi=pi: v.tensor_mul(
                            hidst[:, pi * D + 1024 * half:pi * D + 1024 * half + 1024],
                            lps[half][:, :],
                            silu_sb[:, 1024 * half:1024 * half + 1024]),
                        deps=[mml, sil, sth.get((hh - 2, 3))],
                        tag=f"hmul.{hh}.{half}")
                for r in range(4):
                    src = _r3(hidst, pi * D + 512 * r, 4, 128)
                    sth[(hh, r)] = ld(hid_hbm[4 * r:4 * r + 4, hh].rearrange(
                        "a p c -> p a c"), src, eng="sync",
                                      deps=[hmuls[(hh, r // 2)]],
                                      dkey=f"sth{hh % 2}", tag=f"sth.{hh}.{r}")
            up_last = [sth[(63, r)] for r in range(4)] + [sth[(62, 3)]]
            up_guard = [hmuls[(63, 0)], hmuls[(63, 1)], hmuls[(62, 0)], hmuls[(62, 1)]]

        p34.close()
        p4b = contextlib.ExitStack()
        with p4b:
            P = p4b.enter_context
            hidt = P(nc.sbuf_tensor("hidt", [128, 4 * 64 * 128], BF16))
            w2c = P(nc.sbuf_tensor("w2c", [128, 2 * 64 * 256], BF16))
            dstg = P(nc.sbuf_tensor("dstg", [128, 2 * 256], F32))
            y2ld = P(nc.sbuf_tensor("y2ld", [128, 2 * D], F32))
            y1ld = P(nc.sbuf_tensor("y1ld", [128, 2 * D], F32))
            ypre4 = P(nc.sbuf_tensor("ypre4", [128, D], F32))
            sqd4 = P(nc.sbuf_tensor("sqd4", [128, D], F32))
            outf = P(nc.sbuf_tensor("outf", [128, 2 * D], F32))
            dps = [P(nc.psum_tensor(f"dps{i}", [128, 256], F32)) for i in range(2)]

            ldh, ldw2, mmd, evd, std = {}, {}, {}, {}, {}
            for tg4 in range(4):
                for i in range(4):
                    tt = 4 * tg4 + i
                    ldh[tt] = ld(_r3(hidt, i * 64 * 128, 64, 128),
                                 hid_hbm[tt].rearrange("n p c -> p n c"), eng="sync",
                                 deps=[up_last[3], up_last[4],
                                       mmd.get((tg4 - 1, 7, 4 * (tg4 - 1) + i))],
                                 dkey=f"ldh{tg4 % 2}", tag=f"ldh.{tt}")
                for e8 in range(8):
                    pi = e8 % 2
                    ldw2[(tg4, e8)] = ld(
                        _r3(w2c, pi * 64 * 256, 64, 256),
                        w2pp[e8].rearrange("n p c -> p n c"), eng="scalar",
                        deps=[w2pp_done, w2pp_done2,
                              mmd.get((tg4, e8 - 2, 4 * tg4 + 3)),
                              mmd.get((tg4 - 1, e8 + 6, 4 * (tg4 - 1) + 3))] +
                             (up_guard + up_last if tg4 == 0 and e8 < 2 else []),
                        dkey=f"ldw2{e8 % 2}", tag=f"ldw2.{tg4}.{e8}")
                    for i in range(4):
                        tt = 4 * tg4 + i

                        def f_d(t, i=i, pi=pi, tt=tt):
                            inst = None
                            for hh in range(64):
                                inst = t.matmul(
                                    dps[tt % 2][:, :],
                                    hidt[:, i * 64 * 128 + 128 * hh:
                                         i * 64 * 128 + 128 * hh + 128],
                                    w2c[:, pi * 64 * 256 + 256 * hh:
                                        pi * 64 * 256 + 256 * hh + 256],
                                    start=(hh == 0), stop=(hh == 63))
                            return inst
                        depd = [ldh[4 * tg4 + 3], ldw2[(tg4, e8)],
                                evd.get((tg4, e8, tt - 2)),
                                evd.get((tg4, e8 - 1, tt + 2))]
                        if tg4 == 0 and e8 == 0 and i == 0:
                            depd += up_guard + [lnx[(15, "l3")]]
                        mmd[(tg4, e8, tt)] = G.op("tensor", f_d, deps=depd,
                                                  tag=f"mmd.{tg4}.{e8}.{tt}")
                        evd[(tg4, e8, tt)] = G.op(
                            "vector",
                            lambda v, tt=tt: v.tensor_copy(
                                dstg[:, 256 * (tt % 2):256 * (tt % 2) + 256],
                                dps[tt % 2][:, :]),
                            deps=[mmd[(tg4, e8, tt)], std.get((tg4, e8, tt - 2)),
                                  std.get((tg4, e8 - 1, tt + 2)),
                                  std.get((tg4 - 1, 7, 4 * (tg4 - 1) + 2 + tt % 2))
                                  if e8 == 0 else None],
                            tag=f"evd.{tg4}.{e8}.{tt}")
                        std[(tg4, e8, tt)] = ld(
                            y2f_hbm[tt, :, 256 * e8:256 * e8 + 256],
                            dstg[:, 256 * (tt % 2):256 * (tt % 2) + 256],
                            eng="sync", deps=[evd[(tg4, e8, tt)]],
                            dkey=f"std{tt % 2}", tag=f"std.{tg4}.{e8}.{tt}")

            outv = out.rearrange("(n p) d -> n p d", p=128)
            ln4 = {}
            sto = {}
            for tt in range(16):
                pi = tt % 2
                l2d = ld(y2ld[:, pi * D:(pi + 1) * D], y2f_hbm[tt],
                         eng="sync", deps=[std[(3, 7, 15)], std[(3, 7, 14)],
                                           ln4.get((tt - 2, "evy"))],
                         dkey=f"ly2{tt % 2}", tag=f"ly2.{tt}")
                l1d = ld(y1ld[:, pi * D:(pi + 1) * D], y1v[tt],
                         eng="scalar", deps=[sty1[15], sty1[14],
                                             ln4.get((tt - 2, "evy"))],
                         dkey=f"ly1{tt % 2}", tag=f"ly1.{tt}")

                def f_add(v, pi=pi):
                    inst = None
                    for dg in range(2):
                        inst = v.tensor_add(
                            ypre4[:, 1024 * dg:1024 * dg + 1024],
                            y2ld[:, pi * D + 1024 * dg:pi * D + 1024 * dg + 1024],
                            y1ld[:, pi * D + 1024 * dg:pi * D + 1024 * dg + 1024])
                    return inst
                evy4 = G.op("vector", f_add,
                            deps=[l2d, l1d, ln4.get((tt - 1, "l9")),
                                  ln4.get((tt - 1, "l3"))],
                            tag=f"evy4.{tt}")
                ln4[(tt, "evy")] = evy4
                _ln_chain(G, "p4", tt, evy4, ypre4, sqd4, stat, consts,
                          (CID, CEPS, CNH), outf, pi, ln4, AXP=AX,
                          war=sto.get(tt - 2))
                ln4[(tt, "l9")] = ln4[(tt, "l8")]
                sto[tt] = ld(outv[tt], outf[:, pi * D:(pi + 1) * D], eng="sync",
                             deps=[ln4[(tt, "l8")]], dkey=f"sto{tt % 2}",
                             tag=f"sto.{tt}")

            G.op("sync", lambda sy: sy.nop(), deps=[sto[i] for i in range(16)],
                 tag="fence")

        G.emit(nc, es)
    return nc


def _ln_chain(G, pfx, m, evy, ypre, sqd, stat, consts, cc, yout, pi, store, AXP,
              war=None):
    CID, CEPS, CNH = cc
    l1 = G.op("vector",
              lambda v: v.reduce_sum(stat[:, 0:1], ypre[:, :], axis=AXP, negate=True),
              deps=[evy, store.get((m - 1, "l8"))], tag=f"{pfx}ln1.{m}")
    l2 = G.op("vector",
              lambda v: v.tensor_scalar_mul(stat[:, 1:2], stat[:, 0:1],
                                            consts[:, CID:CID + 1]),
              deps=[l1], tag=f"{pfx}ln2.{m}")
    l3 = G.op("scalar",
              lambda s: s.activation(sqd[:, :], ypre[:, :], AF.Square,
                                     bias=stat[:, 1:2], scale=1.0,
                                     accum_out=stat[:, 2:3]),
              deps=[l2, store.get((m - 1, "l3"))], tag=f"{pfx}ln3.{m}")
    store[(m, "l3")] = l3
    l4 = G.op("vector",
              lambda v: v.tensor_scalar(stat[:, 3:4], stat[:, 2:3],
                                        consts[:, CID:CID + 1],
                                        consts[:, CEPS:CEPS + 1],
                                        ALU.mult, ALU.add),
              deps=[l3], tag=f"{pfx}ln4.{m}")
    l5 = G.op("scalar",
              lambda s: s.activation(stat[:, 4:5], stat[:, 3:4], AF.Ln),
              deps=[l4], tag=f"{pfx}ln5.{m}")
    l6 = G.op("scalar",
              lambda s: s.activation(stat[:, 5:6], stat[:, 4:5], AF.Exp,
                                     scale=consts[:, CNH:CNH + 1]),
              deps=[l5], tag=f"{pfx}ln6.{m}")
    l7 = G.op("vector",
              lambda v: v.tensor_mul(stat[:, 6:7], stat[:, 1:2], stat[:, 5:6]),
              deps=[l6, l2], tag=f"{pfx}ln7.{m}")
    l8 = G.op("vector",
              lambda v: v.tensor_scalar(yout[:, pi * 2048:(pi + 1) * 2048], ypre[:, :],
                                        stat[:, 5:6], stat[:, 6:7],
                                        ALU.mult, ALU.add),
              deps=[l7, l3, war], tag=f"{pfx}ln8.{m}")
    store[(m, "l8")] = l8
    return l8


def _finish_stub(G, nc, es, out, deps):
    z = es.enter_context(nc.sbuf_tensor("zstub", [128, D], F32))
    zo = G.op("gpsimd", lambda g: g.memset(z[:, :], 0.0), tag="zstub")
    outv = out.rearrange("(n p) d -> n p d", p=128)
    last = []
    for m in range(16):
        last.append(G.op("sync", lambda sy, m=m: sy.dma_start(outv[m], z[:, :]),
                         deps=[zo] + list(deps), dma=True, tag=f"zo.{m}"))
    G.op("sync", lambda sy: sy.nop(), deps=[last[-1]], tag="fence")


def _sylvester(n):
    H = np.array([[1.0]])
    while H.shape[0] < n:
        H = np.block([[H, H], [H, -H]])
    return H.astype(np.float32)


def make_core_inputs(x, W, core):
    b, g = core // 4, core % 4
    ins = dict(W)
    ins["xb"] = np.ascontiguousarray(x[b])
    ins["xq"] = np.ascontiguousarray(x[b, g::4, :])
    inv = (1.0 / 10000.0 ** (np.arange(0, RD, 2, dtype=np.float64) / RD))
    posq = (g + 4 * np.arange(NQ, dtype=np.float64))
    ang_q = np.outer(inv, posq)
    cq, sq = np.cos(ang_q), np.sin(ang_q)
    ins["cosq"] = (np.concatenate([cq, cq], 0) * SCALE).astype(np.float32)
    ins["sinq"] = (np.concatenate([-sq, sq], 0) * SCALE).astype(np.float32)
    posk = np.arange(L, dtype=np.float64)
    ang_k = np.outer(inv, posk)
    ck, sk = np.cos(ang_k), np.sin(ang_k)
    ins["cosk"] = np.concatenate([ck, ck], 0).astype(np.float32)
    ins["sink"] = np.concatenate([-sk, sk], 0).astype(np.float32)
    p_ = np.arange(128)[:, None]
    i_ = np.arange(512)[None, :]
    tiles = [(128 * cc + p_ <= g + 4 * i_).astype(np.float32) for cc in range(16)]
    ins["maskp"] = np.ascontiguousarray(np.concatenate(tiles, axis=1))
    return ins


def make_shared_weights(inputs):
    W = {}
    for k in ("wq_down", "wq_up", "wq_rope", "wkv_down", "wk_up", "wv_up",
              "wk_rope", "wo"):
        W[k] = np.ascontiguousarray(np.asarray(inputs["L0_attn_" + k], np.float32))
    W1 = np.asarray(inputs["L0_ffn_W1"], np.float32)
    W3 = np.asarray(inputs["L0_ffn_W3"], np.float32)
    W2 = np.asarray(inputs["L0_ffn_W2"], np.float32)
    W["w1til"] = np.ascontiguousarray(W1.reshape(64, 128, 16, 128).transpose(0, 2, 3, 1))
    W["w3til"] = np.ascontiguousarray(W3.reshape(64, 128, 16, 128).transpose(0, 2, 3, 1))
    W["w2r"] = np.ascontiguousarray(W2.T).reshape(128, 64 * D)
    H64 = _sylvester(64)
    W["h64i2"] = np.ascontiguousarray(np.kron(H64, np.eye(2, dtype=np.float32)))
    W["h128"] = _sylvester(128)
    W["idin"] = np.eye(128, dtype=np.float32)
    return W


def kernel(**inputs):
    x = np.asarray(inputs["L0_attn_x"], np.float32)
    W = make_shared_weights(inputs)
    nc = build_program(upto=4)
    in_maps = [make_core_inputs(x, W, c) for c in range(8)]
    res = run_bass_kernel_spmd(nc, in_maps, core_ids=list(range(8)))
    y = np.empty((B, L, D), np.float32)
    for c in range(8):
        y[c // 4, c % 4::4, :] = res.results[c]["out"]
    return y


# version 02
# speedup vs baseline: 1.2465x; 1.2465x over previous
import contextlib
import math

import numpy as np

import concourse.bass as bass
import concourse.mybir as mybir
from concourse.bass_utils import run_bass_kernel_spmd

F32 = mybir.dt.float32
BF16 = mybir.dt.bfloat16
AF = mybir.ActivationFunctionType
AX = mybir.AxisListType.X
ALU = mybir.AluOpType

B, L, D = 2, 8192, 2048
NH, HD, RD, QR, KVR = 16, 128, 64, 768, 512
HF = 8192
NQ = 2048
SCALE = 1.0 / math.sqrt(HD + RD)
BSHIFT = 32.0
EPS = 1e-5
VS = 130
NKC = L // 128

ENGINES = ("tensor", "vector", "scalar", "sync", "gpsimd")


class Op:
    __slots__ = ("eng", "fn", "deps", "dma", "dkey", "needed", "ctr", "val", "tag")

    def __init__(self, eng, fn, deps, dma, dkey, tag):
        self.eng, self.fn, self.dma, self.tag = eng, fn, dma, tag
        self.dkey = dkey
        self.deps = [d for d in deps if d is not None]
        self.needed = False
        self.ctr = None
        self.val = None


class Graph:
    def __init__(self):
        self.streams = {e: [] for e in ENGINES}
        self.all = []

    def op(self, eng, fn, deps=(), dma=False, dkey=None, tag=""):
        if dma and dkey is None:
            dkey = "bulk"
        o = Op(eng, fn, deps, dma, dkey, tag)
        self.streams[eng].append(o)
        self.all.append(o)
        return o

    def check(self):
        done = set()
        idx = {e: 0 for e in ENGINES}
        while len(done) < len(self.all):
            progress = False
            for e in ENGINES:
                while idx[e] < len(self.streams[e]):
                    o = self.streams[e][idx[e]]
                    if all(d in done for d in o.deps):
                        done.add(o)
                        idx[e] += 1
                        progress = True
                    else:
                        break
            if not progress:
                stuck = []
                for e in ENGINES:
                    if idx[e] < len(self.streams[e]):
                        o = self.streams[e][idx[e]]
                        missing = [d.tag for d in o.deps if d not in done]
                        stuck.append(f"{e}[{idx[e]}] {o.tag} waits {missing[:3]}")
                raise RuntimeError("op graph deadlock: " + " | ".join(stuck))

    def emit(self, nc, es):
        cc, dc = {}, {}
        for e in ENGINES:
            cc[e] = [es.enter_context(nc.semaphore(f"c_{e}")), 0]
        for o in self.all:
            if o.dma:
                o.needed = True
            for d in o.deps:
                d.needed = True
        for e in ENGINES:
            for o in self.streams[e]:
                if o.needed:
                    if o.dma:
                        k = (e, o.dkey)
                        if k not in dc:
                            dc[k] = [es.enter_context(nc.semaphore(f"d_{e}_{o.dkey}")), 0]
                        c = dc[k]
                        c[1] += 16
                    else:
                        c = cc[e]
                        c[1] += 1
                    o.ctr = c
                    o.val = c[1]
        self.check()
        blk = es.enter_context(nc.Block("main"))
        graph = self

        def make_cb(e):
            def cb(eng):
                covered = {}
                for o in graph.streams[e]:
                    want = {}
                    for d in o.deps:
                        assert d.val is not None, f"dep {d.tag} of {o.tag} not inc'd"
                        key = id(d.ctr)
                        if want.get(key, (None, -1))[1] < d.val:
                            want[key] = (d.ctr, d.val)
                    for key, (ctr, val) in want.items():
                        if covered.get(key, -1) < val:
                            eng.wait_ge(ctr[0], val)
                            covered[key] = val
                    inst = o.fn(eng)
                    if o.needed:
                        assert inst is not None, f"op {o.tag} returned no instruction"
                        inst.then_inc(o.ctr[0], 16 if o.dma else 1)
            return cb

        blk.tensor(make_cb("tensor"))
        blk.vector(make_cb("vector"))
        blk.scalar(make_cb("scalar"))
        blk.sync(make_cb("sync"))
        blk.gpsimd(make_cb("gpsimd"))


def _r3(sb, lo, n, c):
    return sb[:, lo:lo + n * c].rearrange("p (a c) -> p a c", c=c)


def build_program(upto=4, dump=()):
    nc = bass.Bass("TRN2", target_bir_lowering=False)
    G = Graph()

    def dram(name, shape, dt, kind="Internal"):
        if name in dump and kind == "Internal":
            kind = "ExternalOutput"
        return nc.dram_tensor(name, shape, dt, kind=kind).ap()

    xb = dram("xb", [L, D], F32, "ExternalInput")
    xq = dram("xq", [NQ, D], F32, "ExternalInput")
    wq_down = dram("wq_down", [D, QR], F32, "ExternalInput")
    wq_up = dram("wq_up", [QR, NH * HD], F32, "ExternalInput")
    wq_rope = dram("wq_rope", [QR, NH * RD], F32, "ExternalInput")
    wkv_down = dram("wkv_down", [D, KVR], F32, "ExternalInput")
    wk_up = dram("wk_up", [KVR, NH * HD], F32, "ExternalInput")
    wv_up = dram("wv_up", [KVR, NH * HD], F32, "ExternalInput")
    wk_rope = dram("wk_rope", [D, RD], F32, "ExternalInput")
    wo = dram("wo", [NH * HD, D], F32, "ExternalInput")
    w1til = dram("w1til", [64, 16, 128, 128], F32, "ExternalInput")
    w3til = dram("w3til", [64, 16, 128, 128], F32, "ExternalInput")
    w2r = dram("w2r", [128, 64 * D], F32, "ExternalInput")
    h64i2 = dram("h64i2", [128, 128], F32, "ExternalInput")
    h128 = dram("h128", [128, 128], F32, "ExternalInput")
    idin = dram("idin", [128, 128], F32, "ExternalInput")
    cosq = dram("cosq", [64, NQ], F32, "ExternalInput")
    sinq = dram("sinq", [64, NQ], F32, "ExternalInput")
    cosk = dram("cosk", [64, L], F32, "ExternalInput")
    sink = dram("sink", [64, L], F32, "ExternalInput")
    maskp = dram("maskp", [128, 16 * 512], F32, "ExternalInput")

    out = dram("out", [NQ, D], F32, "ExternalOutput")

    w2i = dram("w2i", [HF, D], BF16)
    w2pp = dram("w2pp", [8, 64, 128, 256], BF16)
    qt_hbm = dram("qt_hbm", [NH, 192, NQ], BF16)
    kt_hbm = dram("kt_hbm", [NH, 128, L], BF16)
    kr_hbm = dram("kr_hbm", [64, L], BF16)
    v_hbm = dram("v_hbm", [NKC, 128, NH * VS], BF16)
    aot_hbm = dram("aot_hbm", [NH, 128, NQ], BF16)
    y1_hbm = dram("y1_hbm", [NQ, D], F32)
    hid_hbm = dram("hid_hbm", [16, 64, 128, 128], BF16)
    y2f_hbm = dram("y2f_hbm", [16, 128, D], F32)

    es = contextlib.ExitStack()
    with es:
        E = es.enter_context
        ident = E(nc.sbuf_tensor("ident", [128, 128], BF16))
        consts = E(nc.sbuf_tensor("consts", [128, 8], F32))
        CB, CID, CEPS, CSC, CNH, CHS = 0, 1, 2, 3, 4, 5
        stat = E(nc.sbuf_tensor("stat", [128, 8], F32))

        def ld(dst_ap, src_ap, eng="gpsimd", deps=(), dkey="bulk", tag="ld"):
            return G.op(eng, lambda g, d=dst_ap, s=src_ap: g.dma_start(d, s),
                        deps=deps, dma=True, dkey=dkey, tag=tag)

        p1 = contextlib.ExitStack()
        with p1:
            P = p1.enter_context
            qlat = P(nc.sbuf_tensor("qlat", [128, 6 * NQ], BF16))
            kvlat = P(nc.sbuf_tensor("kvlat", [128, 4 * L], BF16))

            def memset_consts(gp):
                gp.memset(consts[:, CB:CB + 1], -BSHIFT)
                gp.memset(consts[:, CID:CID + 1], 1.0 / 2048.0)
                gp.memset(consts[:, CEPS:CEPS + 1], EPS)
                gp.memset(consts[:, CSC:CSC + 1], SCALE)
                gp.memset(consts[:, CNH:CNH + 1], -0.5)
                return gp.memset(consts[:, CHS:CHS + 1], 1.0 / math.sqrt(8192.0))
            o_consts = G.op("gpsimd", memset_consts, tag="consts")

            s1ab = contextlib.ExitStack()
            with s1ab:
                Q = s1ab.enter_context
                w2in = Q(nc.sbuf_tensor("w2in", [128, 2 * 1024], BF16))
                w2st = Q(nc.sbuf_tensor("w2st", [128, 2 * 1024], BF16))
                h64_sb = Q(nc.sbuf_tensor("h64_sb", [128, 128], BF16))
                h128_sb = Q(nc.sbuf_tensor("h128_sb", [128, 128], BF16))
                xin = Q(nc.sbuf_tensor("xin", [128, 2 * D], BF16))
                xt = Q(nc.sbuf_tensor("xt", [128, 2 * 16 * 512], BF16))
                wqd_sb = Q(nc.sbuf_tensor("wqd_sb", [128, 16 * QR], BF16))
                wkvd_sb = Q(nc.sbuf_tensor("wkvd_sb", [128, 16 * KVR], BF16))
                wkr_sb = Q(nc.sbuf_tensor("wkr_sb", [128, 16 * RD], BF16))
                ckb = Q(nc.sbuf_tensor("ckb", [64, 2 * 512], F32))
                skb = Q(nc.sbuf_tensor("skb", [64, 2 * 512], F32))
                rtmpK = Q(nc.sbuf_tensor("rtmpK", [64, 3 * 512], F32))
                krst = Q(nc.sbuf_tensor("krst", [64, 2 * 512], BF16))

                o_h64 = ld(h64_sb[:, :], h64i2, dkey="setup", tag="h64")
                o_h128 = ld(h128_sb[:, :], h128, dkey="setup", tag="h128")
                o_id = ld(ident[:, :], idin, dkey="setup", tag="id")
                o_wqd = ld(_r3(wqd_sb, 0, 16, QR),
                           wq_down.rearrange("(n p) c -> p n c", p=128), dkey="setup", tag="wqd")
                o_wkvd = ld(_r3(wkvd_sb, 0, 16, KVR),
                            wkv_down.rearrange("(n p) c -> p n c", p=128), dkey="setup", tag="wkvd")
                o_wkr = ld(_r3(wkr_sb, 0, 16, RD),
                           wk_rope.rearrange("(n p) c -> p n c", p=128), dkey="setup", tag="wkr")
                o_setup = o_wkr

                ps0 = contextlib.ExitStack()
                with ps0:
                    w2ps = [ps0.enter_context(nc.psum_tensor(f"w2ps{i}", [128, 1024], F32))
                            for i in range(2)]
                    mm1, ev1, st1 = {}, {}, {}
                    st2 = {}
                    for it in range(128):
                        b1, hf = it // 2, it % 2
                        pi = it % 2
                        ld1 = ld(w2in[:, pi * 1024:(pi + 1) * 1024],
                                 w2r[:, b1 * D + 1024 * hf:b1 * D + 1024 * hf + 1024],
                                 deps=[mm1.get(it - 2)], dkey=f"w2ld1{it % 2}", tag=f"w2ld1.{it}")

                        def f_mm1(t, pi=pi):
                            inst = None
                            for dcn in range(2):
                                inst = t.matmul(
                                    w2ps[pi][:, 512 * dcn:512 * dcn + 512], h64_sb[:, :],
                                    w2in[:, pi * 1024 + 512 * dcn:pi * 1024 + 512 * dcn + 512],
                                    start=True, stop=True)
                            return inst
                        mm1[it] = G.op("tensor", f_mm1,
                                       deps=[ld1, o_setup, ev1.get(it - 2)], tag=f"w2mm1.{it}")
                        ev1[it] = G.op("vector",
                                       lambda v, pi=pi: v.tensor_scalar_mul(
                                           w2st[:, pi * 1024:(pi + 1) * 1024],
                                           w2ps[pi][:, :], consts[:, CHS:CHS + 1]),
                                       deps=[mm1[it], st1.get(it - 2), o_consts],
                                       tag=f"w2ev1.{it}")
                        st1[it] = ld(
                            w2i.rearrange("(p b) d -> p b d", b=64)[:, b1,
                                                                   1024 * hf:1024 * hf + 1024],
                            w2st[:, pi * 1024:(pi + 1) * 1024], eng="sync",
                            deps=[ev1[it]], dkey=f"w2st1{it % 2}", tag=f"w2st1.{it}")
                    mm2, ev2 = {}, {}
                    for it in range(128):
                        a, hf = it // 2, it % 2
                        pi = it % 2
                        ld2 = ld(w2in[:, pi * 1024:(pi + 1) * 1024],
                                 w2i[a * 128:(a + 1) * 128, 1024 * hf:1024 * hf + 1024],
                                 eng="sync", deps=[st1[127], st1[126], mm2.get(it - 2)],
                                 dkey=f"w2ld2{it % 2}", tag=f"w2ld2.{it}")

                        def f_mm2(t, pi=pi):
                            inst = None
                            for dcn in range(2):
                                inst = t.matmul(
                                    w2ps[pi][:, 512 * dcn:512 * dcn + 512], h128_sb[:, :],
                                    w2in[:, pi * 1024 + 512 * dcn:pi * 1024 + 512 * dcn + 512],
                                    start=True, stop=True)
                            return inst
                        mm2[it] = G.op("tensor", f_mm2,
                                       deps=[ld2, o_setup, ev2.get(it - 2)], tag=f"w2mm2.{it}")
                        ev2[it] = G.op("vector",
                                       lambda v, pi=pi: v.tensor_copy(
                                           w2st[:, pi * 1024:(pi + 1) * 1024], w2ps[pi][:, :]),
                                       deps=[mm2[it], st2.get((it - 2, 3))], tag=f"w2ev2.{it}")
                        for e4 in range(4):
                            e8 = 4 * hf + e4
                            st2[(it, e4)] = ld(
                                w2pp[e8, a, :, :],
                                w2st[:, pi * 1024 + 256 * e4:pi * 1024 + 256 * e4 + 256],
                                eng="scalar", deps=[ev2[it]], dkey=f"w2st2{it % 2}", tag=f"w2st2.{it}.{e4}")
                ph0_guard = ev2[127]
                w2pp_done = st2[(127, 3)]
                w2pp_done2 = st2[(126, 3)]

                ps1 = contextlib.ExitStack()
                with ps1:
                    tpsX = [ps1.enter_context(nc.psum_tensor(f"tps{i}", [128, 512], BF16))
                            for i in range(2)]
                    prjX = [ps1.enter_context(nc.psum_tensor(f"prj{i}", [128, 512], F32))
                            for i in range(2)]
                    prjK = ps1.enter_context(nc.psum_tensor("prjK", [64, 512], F32))

                    trops, evops, ldx = {}, {}, {}
                    ldck, ldsk = {}, {}
                    prj_users = {}
                    nprj = [0]
                    evql, evkv, ropeK, stkr = {}, {}, {}, {}

                    def qlat_consumer(ci, evdep):
                        n = ci
                        for mt in range(6):
                            pp = nprj[0] % 2
                            nprj[0] += 1

                            def f(t, mt=mt, pp=pp):
                                inst = None
                                for dd in range(16):
                                    inst = t.matmul(
                                        prjX[pp][:, :],
                                        wqd_sb[:, dd * QR + 128 * mt:dd * QR + 128 * mt + 128],
                                        xt[:, (ci % 2) * 8192 + 512 * dd:
                                           (ci % 2) * 8192 + 512 * dd + 512],
                                        start=(dd == 0), stop=(dd == 15))
                                return inst
                            mm = G.op("tensor", f,
                                      deps=[evdep, o_setup, prj_users.get(pp), ph0_guard],
                                      tag=f"qlmm.{n}.{mt}")
                            evql[(n, mt)] = G.op(
                                "vector",
                                lambda v, pp=pp, mt=mt, n=n: v.tensor_copy(
                                    qlat[:, 2048 * mt + 512 * n:2048 * mt + 512 * n + 512],
                                    prjX[pp][:, :]),
                                deps=[mm], tag=f"qlev.{n}.{mt}")
                            prj_users[pp] = evql[(n, mt)]

                    def kv_consumer(ci, evdep):
                        nb = ci - 4
                        for mt in range(4):
                            pp = nprj[0] % 2
                            nprj[0] += 1

                            def f(t, mt=mt, pp=pp, ci=ci):
                                inst = None
                                for dd in range(16):
                                    inst = t.matmul(
                                        prjX[pp][:, :],
                                        wkvd_sb[:, dd * KVR + 128 * mt:dd * KVR + 128 * mt + 128],
                                        xt[:, (ci % 2) * 8192 + 512 * dd:
                                           (ci % 2) * 8192 + 512 * dd + 512],
                                        start=(dd == 0), stop=(dd == 15))
                                return inst
                            mm = G.op("tensor", f,
                                      deps=[evdep, o_setup, prj_users.get(pp), ph0_guard],
                                      tag=f"kvmm.{nb}.{mt}")
                            evkv[(nb, mt)] = G.op(
                                "vector",
                                lambda v, pp=pp, mt=mt, nb=nb: v.tensor_copy(
                                    kvlat[:, 8192 * mt + 512 * nb:8192 * mt + 512 * nb + 512],
                                    prjX[pp][:, :]),
                                deps=[mm], tag=f"kvev.{nb}.{mt}")
                            prj_users[pp] = evkv[(nb, mt)]

                        ldck[nb] = ld(ckb[:, (nb % 2) * 512:(nb % 2) * 512 + 512],
                                      cosk[:, 512 * nb:512 * nb + 512],
                                      eng="sync", deps=[ropeK.get(("B", nb - 2))],
                                      dkey=f"ldck{nb % 2}", tag=f"ldck.{nb}")
                        ldsk[nb] = ld(skb[:, (nb % 2) * 512:(nb % 2) * 512 + 512],
                                      sink[:, 512 * nb:512 * nb + 512],
                                      eng="sync", deps=[ropeK.get(("D", nb - 2))],
                                      dkey=f"ldsk{nb % 2}", tag=f"ldsk.{nb}")

                        def f_kr(t, ci=ci):
                            inst = None
                            for dd in range(16):
                                inst = t.matmul(
                                    prjK[:, :],
                                    wkr_sb[:, dd * RD:dd * RD + RD],
                                    xt[:, (ci % 2) * 8192 + 512 * dd:
                                       (ci % 2) * 8192 + 512 * dd + 512],
                                    start=(dd == 0), stop=(dd == 15))
                            return inst
                        mmkr = G.op("tensor", f_kr,
                                    deps=[evdep, o_setup, ropeK.get(("D", nb - 1)), ph0_guard],
                                    tag=f"krmm.{nb}")
                        ropeK[("A", nb)] = G.op(
                            "vector",
                            lambda v: v.tensor_copy(rtmpK[:, 0:512], prjK[:, :]),
                            deps=[mmkr, ropeK.get(("D", nb - 1))], tag=f"krA.{nb}")

                        def f_rot(v):
                            v.tensor_copy(rtmpK[0:32, 512:1024], prjK[32:64, :])
                            return v.tensor_copy(rtmpK[32:64, 512:1024], prjK[0:32, :])
                        ropeK[("R", nb)] = G.op("vector", f_rot,
                                                deps=[mmkr, ropeK.get(("D", nb - 1))],
                                                tag=f"krR.{nb}")
                        ropeK[("B", nb)] = G.op(
                            "vector",
                            lambda v, nb=nb: v.tensor_mul(
                                rtmpK[:, 1024:1536], rtmpK[:, 0:512],
                                ckb[:, (nb % 2) * 512:(nb % 2) * 512 + 512]),
                            deps=[ropeK[("A", nb)], ldck[nb]], tag=f"krB.{nb}")

                        ropeK[("C", nb)] = G.op(
                            "vector",
                            lambda v, nb=nb: v.tensor_mul(
                                rtmpK[:, 512:1024], rtmpK[:, 512:1024],
                                skb[:, (nb % 2) * 512:(nb % 2) * 512 + 512]),
                            deps=[ropeK[("B", nb)], ropeK[("R", nb)], ldsk[nb]],
                            tag=f"krC.{nb}")
                        ropeK[("D", nb)] = G.op(
                            "vector",
                            lambda v, nb=nb: v.tensor_add(
                                krst[:, (nb % 2) * 512:(nb % 2) * 512 + 512],
                                rtmpK[:, 1024:1536], rtmpK[:, 512:1024]),
                            deps=[ropeK[("C", nb)], stkr.get(nb - 2)],
                            tag=f"krD.{nb}")
                        stkr[nb] = ld(kr_hbm[:, 512 * nb:512 * nb + 512],
                                      krst[:, (nb % 2) * 512:(nb % 2) * 512 + 512],
                                      eng="sync", deps=[ropeK[("D", nb)]],
                                      dkey=f"stkr{nb % 2}", tag=f"stkr.{nb}")

                    def x_chunk(ci, srcrows, consumers):
                        for mtl in range(4):
                            gt = 4 * ci + mtl
                            pig = gt % 2
                            ldx[gt] = ld(
                                xin[:, pig * D:(pig + 1) * D],
                                srcrows[128 * mtl:128 * mtl + 128, :],
                                deps=[trops.get(4 * (gt - 2) + 3)], dkey=f"ldx{gt % 2}", tag=f"ldx.{gt}")
                            for r in range(4):
                                R = 4 * gt + r
                                dd0 = 4 * r

                                def f_tr(t, pig=pig, dd0=dd0, R=R):
                                    inst = None
                                    for k in range(4):
                                        inst = t.transpose(
                                            tpsX[R % 2][:, 128 * k:128 * k + 128],
                                            xin[:, pig * D + 128 * (dd0 + k):
                                                pig * D + 128 * (dd0 + k) + 128],
                                            ident[:, :])
                                    return inst
                                trops[R] = G.op("tensor", f_tr,
                                                deps=[ldx[gt], o_setup, evops.get(R - 2),
                                                      ph0_guard],
                                                tag=f"tr.{R}")

                                def f_ev(v, ci=ci, mtl=mtl, dd0=dd0, R=R):
                                    dst = xt[:, (ci % 2) * 8192:(ci % 2) * 8192 + 8192]\
                                        .rearrange("p (a c) -> p a c", c=512)[
                                        :, dd0:dd0 + 4, 128 * mtl:128 * mtl + 128]
                                    return v.tensor_copy(
                                        dst, tpsX[R % 2].rearrange("p (a b) -> p a b", b=128))
                                evops[R] = G.op("vector", f_ev, deps=[trops[R]],
                                                tag=f"ev.{R}")
                        for c in consumers:
                            c(ci, evops[4 * (4 * ci + 3) + 3])

                    for n in range(4):
                        x_chunk(n, xq[512 * n:512 * (n + 1), :], [qlat_consumer])
                    for n in range(16):
                        x_chunk(4 + n, xb[512 * n:512 * (n + 1), :], [kv_consumer])

                    last_ql = evql[(3, 5)]
                    last_kv = evkv[(15, 3)]
                    last_kr = ropeK[("D", 15)]
                    ps1_guard = [last_ql, last_kv, last_kr, prj_users[0], prj_users[1],
                                 evops[4 * (4 * 19 + 3) + 3]]
                g1ab_last = [last_ql, last_kv, stkr[15], stkr[14], st1[127], st1[126],
                             w2pp_done, w2pp_done2,
                             trops[4 * (4 * 19 + 3) + 3], ps1_guard[-1]]

            s1cd = contextlib.ExitStack()
            with s1cd:
                Q = s1cd.enter_context
                wquh = Q(nc.sbuf_tensor("wquh", [128, 2 * 6 * 128], BF16))
                wqrh = Q(nc.sbuf_tensor("wqrh", [128, 2 * 6 * 64], BF16))
                wkuh = Q(nc.sbuf_tensor("wkuh", [128, 2 * 4 * 128], BF16))
                wvu_sb = Q(nc.sbuf_tensor("wvu_sb", [128, 4 * 2048], BF16))
                cq_sb = Q(nc.sbuf_tensor("cq_sb", [64, NQ], F32))
                sq_sb = Q(nc.sbuf_tensor("sq_sb", [64, NQ], F32))
                qtc_st = Q(nc.sbuf_tensor("qtc_st", [128, 2 * NQ], BF16))
                qtr_st = Q(nc.sbuf_tensor("qtr_st", [64, 2 * NQ], BF16))
                ktc_st = Q(nc.sbuf_tensor("ktc_st", [128, 2 * 2048], BF16))
                vstage = Q(nc.sbuf_tensor("vstage", [128, 2 * NH * VS], BF16))
                rtmp = Q(nc.sbuf_tensor("rtmp", [64, 3 * 2048], F32))

                o_cq = ld(cq_sb[:, :], cosq, deps=g1ab_last, dkey="g1cd", tag="cq")
                o_sq = ld(sq_sb[:, :], sinq, deps=g1ab_last, dkey="g1cd", tag="sq")
                o_wvu = ld(_r3(wvu_sb, 0, 4, 2048),
                           wv_up.rearrange("(n p) c -> p n c", p=128),
                           deps=g1ab_last, dkey="g1cd", tag="wvu")
                o_g1cd = o_wvu

                ps1c = contextlib.ExitStack()
                with ps1c:
                    qcps = [ps1c.enter_context(nc.psum_tensor(f"qcps{i}", [128, 512], F32))
                            for i in range(2)]
                    qrps = ps1c.enter_context(nc.psum_tensor("qrps", [64, 2048], F32))
                    stq = {}
                    ldqu, ldqr = {}, {}
                    qc_user = {}
                    rope_user = [None]
                    qc_lastmm, qr_lastmm = {}, {}
                    for h in range(NH):
                        pi = h % 2
                        ldqu[h] = ld(_r3(wquh, pi * 6 * 128, 6, 128),
                                     wq_up[:, 128 * h:128 * h + 128].rearrange(
                                         "(n p) c -> p n c", p=128),
                                     deps=[qc_lastmm.get(h - 2)] +
                                          (g1ab_last if h < 2 else []),
                                     dkey=f"ldqu{h % 2}", tag=f"ldqu.{h}")
                        ldqr[h] = ld(_r3(wqrh, pi * 6 * 64, 6, 64),
                                     wq_rope[:, 64 * h:64 * h + 64].rearrange(
                                         "(n p) c -> p n c", p=128),
                                     deps=[qr_lastmm.get(h - 2)] +
                                          (g1ab_last if h < 2 else []),
                                     dkey=f"ldqr{h % 2}", tag=f"ldqr.{h}")
                        evs = []
                        for cg in range(4):
                            pp = cg % 2

                            def f(t, h=h, cg=cg, pp=pp, pi=pi):
                                inst = None
                                for qq in range(6):
                                    inst = t.matmul(
                                        qcps[pp][:, :],
                                        wquh[:, pi * 768 + 128 * qq:pi * 768 + 128 * qq + 128],
                                        qlat[:, 2048 * qq + 512 * cg:2048 * qq + 512 * cg + 512],
                                        start=(qq == 0), stop=(qq == 5))
                                return inst
                            mm = G.op("tensor", f,
                                      deps=[last_ql, ldqu[h], qc_user.get(pp)] +
                                           (ps1_guard if h == 0 and cg == 0 else []),
                                      tag=f"qcmm.{h}.{cg}")
                            ev = G.op("scalar",
                                      lambda sc, cg=cg, pp=pp, pi=pi: sc.activation(
                                          qtc_st[:, pi * NQ + 512 * cg:pi * NQ + 512 * cg + 512],
                                          qcps[pp][:, :], AF.Copy, bias=0.0,
                                          scale=consts[:, CSC:CSC + 1]),
                                      deps=[mm, o_consts, stq.get((h - 2, 0))],
                                      tag=f"qcev.{h}.{cg}")
                            qc_user[pp] = ev
                            evs.append(ev)
                        qc_lastmm[h] = mm

                        def f_qr(t, h=h, pi=pi):
                            inst = None
                            for cg in range(4):
                                for qq in range(6):
                                    inst = t.matmul(
                                        qrps[:, 512 * cg:512 * cg + 512],
                                        wqrh[:, pi * 384 + 64 * qq:pi * 384 + 64 * qq + 64],
                                        qlat[:, 2048 * qq + 512 * cg:2048 * qq + 512 * cg + 512],
                                        start=(qq == 0), stop=(qq == 5))
                            return inst
                        mmqr = G.op("tensor", f_qr,
                                    deps=[last_ql, ldqr[h], rope_user[0]] +
                                         (ps1_guard if h == 0 else []),
                                    tag=f"qrmm.{h}")
                        oA_ = G.op("vector",
                                   lambda v: v.tensor_copy(rtmp[:, 0:2048], qrps[:, :]),
                                   deps=[mmqr, rope_user[0]] +
                                        (g1ab_last if h == 0 else []),
                                   tag=f"qrA.{h}")

                        def f_rot(v):
                            v.tensor_copy(rtmp[0:32, 2048:4096], qrps[32:64, :])
                            return v.tensor_copy(rtmp[32:64, 2048:4096], qrps[0:32, :])
                        oR_ = G.op("vector", f_rot, deps=[mmqr, rope_user[0]],
                                   tag=f"qrR.{h}")
                        oB_ = G.op("vector",
                                   lambda v: v.tensor_mul(rtmp[:, 4096:6144],
                                                          rtmp[:, 0:2048], cq_sb[:, :]),
                                   deps=[oA_, o_g1cd], tag=f"qrB.{h}")

                        oC_ = G.op("vector",
                                   lambda v: v.tensor_mul(rtmp[:, 2048:4096],
                                                          rtmp[:, 2048:4096], sq_sb[:, :]),
                                   deps=[oB_, oR_, o_g1cd], tag=f"qrC.{h}")
                        oD_ = G.op("vector",
                                   lambda v, pi=pi: v.tensor_add(
                                       qtr_st[:, pi * NQ:pi * NQ + NQ],
                                       rtmp[:, 4096:6144], rtmp[:, 2048:4096]),
                                   deps=[oC_, stq.get((h - 2, 1))], tag=f"qrD.{h}")
                        rope_user[0] = oD_
                        qr_lastmm[h] = mmqr
                        stq[(h, 0)] = ld(qt_hbm[h, 0:128, :],
                                         qtc_st[:, pi * NQ:(pi + 1) * NQ],
                                         eng="sync", deps=evs, dkey=f"stqc{h % 2}", tag=f"stqc.{h}")
                        stq[(h, 1)] = ld(qt_hbm[h, 128:192, :],
                                         qtr_st[:, pi * NQ:(pi + 1) * NQ],
                                         eng="sync", deps=[oD_], dkey=f"stqr{h % 2}", tag=f"stqr.{h}")
                    ps1c_guard = [qc_user[0], qc_user[1], rope_user[0]]

                ps1dk = contextlib.ExitStack()
                with ps1dk:
                    kcps = [ps1dk.enter_context(
                        nc.psum_tensor(f"kcps{i}", [128, 512], F32)) for i in range(2)]
                    stkc = {}
                    ldku = {}
                    kc_user = {}
                    kc_lastmm = {}
                    for h in range(NH):
                        ldku[h] = ld(_r3(wkuh, (h % 2) * 4 * 128, 4, 128),
                                     wk_up[:, 128 * h:128 * h + 128].rearrange(
                                         "(n p) c -> p n c", p=128),
                                     deps=[kc_lastmm.get(h - 2)] +
                                          (g1ab_last if h < 2 else []),
                                     dkey=f"ldku{h % 2}", tag=f"ldku.{h}")
                        for qq4 in range(4):
                            qg = 4 * h + qq4
                            sl = qg % 2
                            evs = []
                            for g2 in range(4):
                                g8 = 4 * qq4 + g2
                                pp = g8 % 2

                                def f(t, h=h, g8=g8, pp=pp):
                                    inst = None
                                    for kk in range(4):
                                        inst = t.matmul(
                                            kcps[pp][:, :],
                                            wkuh[:, (h % 2) * 512 + 128 * kk:
                                                 (h % 2) * 512 + 128 * kk + 128],
                                            kvlat[:, 8192 * kk + 512 * g8:
                                                  8192 * kk + 512 * g8 + 512],
                                            start=(kk == 0), stop=(kk == 3))
                                    return inst
                                mm = G.op("tensor", f,
                                          deps=[last_kv, ldku[h], kc_user.get(pp)] +
                                               (ps1c_guard if h == 0 and g8 == 0 else []),
                                          tag=f"kcmm.{h}.{g8}")
                                ev = G.op("vector",
                                          lambda v, g2=g2, pp=pp, sl=sl: v.tensor_copy(
                                              ktc_st[:, sl * 2048 + 512 * g2:
                                                     sl * 2048 + 512 * g2 + 512],
                                              kcps[pp][:, :]),
                                          deps=[mm, stkc.get(qg - 2)],
                                          tag=f"kcev.{h}.{g8}")
                                kc_user[pp] = ev
                                evs.append(ev)
                            stkc[qg] = ld(kt_hbm[h, 0:128, 2048 * qq4:2048 * qq4 + 2048]
                                          if False else
                                          kt_hbm[h, :, 2048 * qq4:2048 * qq4 + 2048],
                                          ktc_st[:, sl * 2048:(sl + 1) * 2048],
                                          eng="sync", deps=evs, dkey=f"stkc{qg % 2}", tag=f"stkc.{qg}")
                        kc_lastmm[h] = mm
                    ps1dk_guard = [kc_user[0], kc_user[1]]

                ps1dv = contextlib.ExitStack()
                with ps1dv:
                    vps = [ps1dv.enter_context(
                        nc.psum_tensor(f"vps{i}", [128, 2048], F32)) for i in range(2)]

                    def f_ones(gp):
                        for pi2 in range(2):
                            base = pi2 * NH * VS
                            inst = gp.memset(
                                _r3(vstage, base, NH, VS)[:, :, 128:130], 1.0)
                        return inst
                    o_ones = G.op("gpsimd", f_ones, deps=g1ab_last, tag="vones")

                    stv, evv, mmv = {}, {}, {}
                    for tt in range(NKC):
                        pi = tt % 2

                        def f(t, tt=tt, pi=pi):
                            inst = None
                            for kk in range(4):
                                for hg in range(4):
                                    inst = t.matmul(
                                        vps[pi][:, 512 * hg:512 * hg + 512],
                                        kvlat[:, 8192 * kk + 128 * tt:8192 * kk + 128 * tt + 128],
                                        wvu_sb[:, kk * 2048 + 512 * hg:kk * 2048 + 512 * hg + 512],
                                        start=(kk == 0), stop=(kk == 3))
                            return inst
                        mmv[tt] = G.op("tensor", f,
                                       deps=[last_kv, o_g1cd, evv.get(tt - 2)] +
                                            (ps1dk_guard if tt == 0 else []),
                                       tag=f"vmm.{tt}")

                        def f_ev(v, tt=tt, pi=pi):
                            inst = None
                            for hg in range(4):
                                dst = _r3(vstage, pi * NH * VS + VS * 4 * hg, 4, VS)[:, :, 0:128]
                                inst = v.tensor_copy(
                                    dst,
                                    vps[pi].rearrange("p (a b) -> p a b", b=128)[:, 4 * hg:4 * hg + 4, :])
                            return inst
                        evv[tt] = G.op("vector", f_ev,
                                       deps=[mmv[tt], o_ones, stv.get(tt - 2)],
                                       tag=f"vev.{tt}")
                        stv[tt] = ld(v_hbm[tt], vstage[:, pi * NH * VS:(pi + 1) * NH * VS],
                                     eng="sync", deps=[evv[tt]], dkey=f"stv{tt % 2}", tag=f"stv.{tt}")
                    ps1dv_guard = [evv[NKC - 1], evv[NKC - 2]]

                ph1_last = [stv[NKC - 1], stv[NKC - 2], stq[(NH - 1, 1)], stq[(NH - 2, 1)],
                            stq[(NH - 1, 0)], stq[(NH - 2, 0)],
                            stkc[4 * NH - 1], stkc[4 * NH - 2],
                            w2pp_done, w2pp_done2, stkr[15], stkr[14]] + ps1dv_guard

        if upto < 2:
            _finish_stub(G, nc, es, out, ph1_last)
            G.emit(nc, es)
            return nc

        p2 = contextlib.ExitStack()
        with p2:
            P = p2.enter_context
            qtc_b = P(nc.sbuf_tensor("qtc_b", [128, 2 * NQ], BF16))
            qtr_b = P(nc.sbuf_tensor("qtr_b", [64, 2 * NQ], BF16))
            ktc_b = P(nc.sbuf_tensor("ktc_b", [128, 2 * L], BF16))
            ktr_b = P(nc.sbuf_tensor("ktr_b", [64, L], BF16))
            v_b = P(nc.sbuf_tensor("v_b", [128, 2 * NKC * VS], BF16))
            pt_b = P(nc.sbuf_tensor("pt_b", [128, 3 * 1024], BF16))
            mask_sb = P(nc.sbuf_tensor("mask_sb", [128, 16 * 512], BF16))
            obf = P(nc.sbuf_tensor("obf", [128, 2 * NQ], BF16))
            rs_sb = P(nc.sbuf_tensor("rs_sb", [128, 8], F32))
            ssum_sb = P(nc.sbuf_tensor("ssum_sb", [128, 8], F32))
            aot_st = P(nc.sbuf_tensor("aot_st", [128, 2 * NQ], BF16))

            stp = [P(nc.psum_tensor(f"st{i}", [128, 1024], F32)) for i in range(2)]
            oA = P(nc.psum_tensor("oA", [128, 512], F32))
            oB = P(nc.psum_tensor("oB", [128, 512], F32))
            tp2 = [P(nc.psum_tensor(f"tp2{i}", [128, 512], BF16)) for i in range(2)]
            OFF = [0, 144]

            o_mask = ld(mask_sb[:, :], maskp, deps=ph1_last, dkey="g2", tag="maskld")
            o_ktr = ld(ktr_b[:, :], kr_hbm, deps=ph1_last, dkey="g2", tag="ktrld")
            o_g2 = o_ktr

            lq, lk, lv = {}, {}, {}
            sc_last, pv_last = {}, {}
            exop, pvop, scop, mkop = {}, {}, {}, {}
            sfin, trh, evh, sta = {}, {}, {}, {}
            tasks = []
            for h in range(NH):
                for j in range(4):
                    nch = 16 * (j + 1)
                    for p in range(nch // 2):
                        tasks.append((h, j, p, nch))
            NT = len(tasks)
            DEPTH = 2

            def make_loads(h):
                pi = h % 2
                dep_prev = [sc_last.get(h - 2), pv_last.get(h - 2),
                            trh.get((h - 2, 3))] + (ph1_last if h < 2 else [])
                lq[h] = ld(qtc_b[:, pi * NQ:(pi + 1) * NQ], qt_hbm[h, 0:128, :],
                           eng="sync", deps=dep_prev, dkey=f"l2h{h % 2}", tag=f"lqc.{h}")
                lq[(h, 1)] = ld(qtr_b[:, pi * NQ:(pi + 1) * NQ], qt_hbm[h, 128:192, :],
                                eng="sync", deps=dep_prev, dkey=f"l2h{h % 2}", tag=f"lqr.{h}")
                lk[h] = ld(ktc_b[:, pi * L:(pi + 1) * L], kt_hbm[h],
                           eng="sync", deps=dep_prev, dkey=f"l2h{h % 2}", tag=f"lkc.{h}")
                for part in range(8):
                    src = v_hbm[8 * part:8 * part + 8, :, VS * h:VS * h + VS]\
                        .rearrange("a p c -> p a c")
                    dst = _r3(v_b, pi * NKC * VS + 8 * part * VS, 8, VS)
                    lv[(h, part)] = ld(dst, src, eng="sync", deps=dep_prev,
                                       dkey=f"l2h{h % 2}", tag=f"lv.{h}.{part}")

            def make_sc(Pg):
                h, j, p, nch = tasks[Pg]
                pi = h % 2
                pp = Pg % 2
                pp3 = Pg % 3
                masked = [c for c in (2 * p, 2 * p + 1) if c >= 16 * j]

                def f_sc(t, p=p, j=j, pp=pp, pi=pi):
                    inst = None
                    for half in range(2):
                        c = 2 * p + half
                        cs = slice(512 * half, 512 * half + 512)
                        t.matmul(stp[pp][:, cs],
                                 ktc_b[:, pi * L + 128 * c:pi * L + 128 * c + 128],
                                 qtc_b[:, pi * NQ + 512 * j:pi * NQ + 512 * j + 512],
                                 start=True, stop=False)
                        inst = t.matmul(
                            stp[pp][:, cs],
                            ktr_b[:, 128 * c:128 * c + 128],
                            qtr_b[:, pi * NQ + 512 * j:pi * NQ + 512 * j + 512],
                            start=False, stop=True)
                    return inst
                scop[Pg] = G.op("tensor", f_sc,
                                deps=[lv[(h, 7)], o_g2,
                                      exop.get(Pg - 2)] +
                                     ([ph1_last[0]] if Pg == 0 else []),
                                tag=f"sc.{Pg}")

            def make_ex(Pg):
                h, j, p, nch = tasks[Pg]
                pp = Pg % 2
                pp3 = Pg % 3
                masked = [c for c in (2 * p, 2 * p + 1) if c >= 16 * j]
                exop[Pg] = G.op("scalar",
                                lambda s, pp=pp, pp3=pp3: s.activation(
                                    pt_b[:, pp3 * 1024:pp3 * 1024 + 1024],
                                    stp[pp][:, :], AF.Exp,
                                    bias=consts[:, CB:CB + 1], scale=1.0),
                                deps=[scop[Pg], o_consts, pvop.get(Pg - 3)],
                                tag=f"ex.{Pg}")
                mkop[Pg] = None
                if masked:
                    def f_mk(v, masked=tuple(masked), pp3=pp3, j=j, p=p):
                        inst = None
                        for c in masked:
                            half = c - 2 * p
                            inst = v.tensor_mul(
                                pt_b[:, pp3 * 1024 + 512 * half:
                                     pp3 * 1024 + 512 * half + 512],
                                pt_b[:, pp3 * 1024 + 512 * half:
                                     pp3 * 1024 + 512 * half + 512],
                                mask_sb[:, 512 * (c - 16 * j):512 * (c - 16 * j) + 512])
                        return inst
                    mkop[Pg] = G.op("vector", f_mk, deps=[exop[Pg], o_g2],
                                    tag=f"mk.{Pg}")

            def make_pv(Pg):
                h, j, p, nch = tasks[Pg]
                pi = h % 2
                pp3 = Pg % 3
                jg = 4 * h + j
                jp = jg % 2

                def f_pv(t, p=p, pp3=pp3, pi=pi, nch=nch):
                    inst = None
                    for half in range(2):
                        c = 2 * p + half
                        for u in range(4):
                            inst = t.matmul(
                                (oA if u < 2 else oB)[:, OFF[u % 2]:OFF[u % 2] + 129],
                                pt_b[:, pp3 * 1024 + 512 * half + 128 * u:
                                     pp3 * 1024 + 512 * half + 128 * u + 128],
                                v_b[:, pi * NKC * VS + VS * c:
                                    pi * NKC * VS + VS * c + 129],
                                start=(c == 0 and u % 2 == 0),
                                stop=(c == nch - 1 and u % 2 == 1),
                                skip_group_check=True)
                    return inst
                pvdeps = [exop[Pg], mkop[Pg], lv[(h, 7)]]
                if p == 0:
                    pvdeps.append(sfin.get(jg - 1))
                pvop[Pg] = G.op("tensor", f_pv, deps=pvdeps, tag=f"pv.{Pg}")
                if p != nch // 2 - 1:
                    return
                vs1 = G.op("vector",
                           lambda v, jp=jp: v.tensor_copy(
                               ssum_sb[:, 4 * jp:4 * jp + 2], oA[:, 128:273:144]),
                           deps=[pvop[Pg], sfin.get(jg - 1)], tag=f"vs1.{jg}")
                vs2 = G.op("vector",
                           lambda v, jp=jp: v.tensor_copy(
                               ssum_sb[:, 4 * jp + 2:4 * jp + 4], oB[:, 128:273:144]),
                           deps=[pvop[Pg], sfin.get(jg - 1)], tag=f"vs2.{jg}")
                vrec = G.op("vector",
                            lambda v, jp=jp: v.reciprocal(
                                rs_sb[:, 4 * jp:4 * jp + 4],
                                ssum_sb[:, 4 * jp:4 * jp + 4]),
                            deps=[vs1, vs2], tag=f"vrec.{jg}")

                def f_fin(s, j=j, pi=pi, jp=jp):
                    inst = None
                    for u in range(4):
                        ob = oA if u < 2 else oB
                        ui = (0 if u < 2 else 2) + u % 2
                        inst = s.activation(
                            obf[:, pi * NQ + 512 * j + 128 * u:
                                pi * NQ + 512 * j + 128 * u + 128],
                            ob[:, OFF[u % 2]:OFF[u % 2] + 128],
                            AF.Copy, bias=0.0,
                            scale=rs_sb[:, 4 * jp + ui:4 * jp + ui + 1])
                    return inst
                sfin[jg] = G.op("scalar", f_fin,
                                deps=[vrec, trh.get((h - 2, 3))], tag=f"sfin.{jg}")
                if j != 3:
                    return
                for r in range(4):
                    def f_tr(t, r=r, pi=pi):
                        inst = None
                        for k in range(4):
                            u = 4 * r + k
                            inst = t.transpose(
                                tp2[r % 2][:, 128 * k:128 * k + 128],
                                obf[:, pi * NQ + 128 * u:pi * NQ + 128 * u + 128],
                                ident[:, :])
                        return inst
                    trh[(h, r)] = G.op("tensor", f_tr,
                                       deps=[sfin[4 * h + 3], evh.get((h, r - 2)), o_id],
                                       tag=f"trh.{h}.{r}")
                    evh[(h, r)] = G.op("vector",
                                       lambda v, r=r, pi=pi: v.tensor_copy(
                                           aot_st[:, pi * NQ + 512 * r:pi * NQ + 512 * r + 512],
                                           tp2[r % 2][:, :]),
                                       deps=[trh[(h, r)], sta.get(h - 2)],
                                       tag=f"evh.{h}.{r}")
                sta[h] = ld(aot_hbm[h], aot_st[:, pi * NQ:(pi + 1) * NQ], eng="scalar",
                            deps=[evh[(h, 3)]], dkey=f"sta{h % 2}", tag=f"sta.{h}")

            for Pg in range(NT):
                h = tasks[Pg][0]
                if Pg == 0 or tasks[Pg - 1][0] != h:
                    make_loads(h)
                make_sc(Pg)
                if Pg >= DEPTH:
                    make_pv(Pg - DEPTH)
                make_ex(Pg)
                if tasks[Pg][1] == 3 and tasks[Pg][2] == tasks[Pg][3] // 2 - 1:
                    sc_last[h] = scop[Pg]
                if Pg >= DEPTH:
                    hd = tasks[Pg - DEPTH][0]
                    td = tasks[Pg - DEPTH]
                    if td[1] == 3 and td[2] == td[3] // 2 - 1:
                        pv_last[hd] = pvop[Pg - DEPTH]
            for Pg in range(NT - DEPTH, NT):
                make_pv(Pg)
            pv_last[NH - 1] = pvop[NT - 1]
            ph2_last = [sta[NH - 1], sta[NH - 2], pvop[NT - 1], sfin[4 * NH - 1],
                        trh[(NH - 1, 3)], evh[(NH - 1, 3)]]
            ph2_guard = [exop[NT - 1], exop[NT - 2], sfin[4 * NH - 1],
                         evh[(NH - 1, 3)], evh[(NH - 1, 2)]]

        if upto < 3:
            _finish_stub(G, nc, es, out, ph2_last)
            G.emit(nc, es)
            return nc

        p34 = contextlib.ExitStack()
        y1t_sb = p34.enter_context(nc.sbuf_tensor("y1t_sb", [128, 16 * NQ], BF16))
        p3 = contextlib.ExitStack()
        with p3:
            P = p3.enter_context
            wo_sb = P(nc.sbuf_tensor("wo_sb", [128, 16 * D], BF16))
            aotm = P(nc.sbuf_tensor("aotm", [128, 2 * D], BF16))
            xq_sb = P(nc.sbuf_tensor("xq_sb", [128, 2 * D], F32))
            ypre = P(nc.sbuf_tensor("ypre", [128, D], F32))
            sqd = P(nc.sbuf_tensor("sqd", [128, D], F32))
            y1f = P(nc.sbuf_tensor("y1f", [128, 2 * D], F32))
            y1b = P(nc.sbuf_tensor("y1b", [128, D], BF16))
            y1ps = P(nc.psum_tensor("y1ps", [128, 2048], F32))
            tp3 = [P(nc.psum_tensor(f"tp3{i}", [128, 512], BF16)) for i in range(2)]

            o_wo = ld(_r3(wo_sb, 0, 16, D),
                      wo.rearrange("(n p) c -> p n c", p=128),
                      deps=ph2_last, dkey="g3", tag="wold")
            xqv = xq.rearrange("(n p) d -> n p d", p=128)
            y1v = y1_hbm.rearrange("(n p) d -> n p d", p=128)
            lnx = {}
            sty1 = {}
            evt3 = {}
            prev_mm = {}

            def make_tr3(m):
                pi = m % 2
                l9 = lnx[(m, "l9")]
                for r in range(4):
                    def f_tr(t, r=r):
                        inst = None
                        for k in range(4):
                            dd = 4 * r + k
                            inst = t.transpose(tp3[r % 2][:, 128 * k:128 * k + 128],
                                               y1b[:, 128 * dd:128 * dd + 128],
                                               ident[:, :])
                        return inst
                    tr = G.op("tensor", f_tr,
                              deps=[l9, evt3.get((m, r - 2)), evt3.get((m - 1, r + 2)),
                                    o_id] + (ph2_guard if m == 0 and r == 0 else []),
                              tag=f"tr3.{m}.{r}")

                    def f_ev(v, r=r, m=m):
                        dst = y1t_sb.rearrange("p (a c) -> p a c", c=2048)[
                            :, 4 * r:4 * r + 4, 128 * m:128 * m + 128]
                        return v.tensor_copy(
                            dst, tp3[r % 2].rearrange("p (a b) -> p a b", b=128))
                    evt3[(m, r)] = G.op("vector", f_ev, deps=[tr], tag=f"evt3.{m}.{r}")

            for m in range(16):
                pi = m % 2
                lda = []
                for q4 in range(4):
                    src = aot_hbm[4 * q4:4 * q4 + 4, :, 128 * m:128 * m + 128]\
                        .rearrange("a p c -> p a c")
                    dst = _r3(aotm, pi * D + 512 * q4, 4, 128)
                    lda.append(ld(dst, src, eng="sync",
                                  deps=[prev_mm.get(m - 2)] + (ph2_last if m < 2 else []),
                                  dkey=f"lda{m % 2}", tag=f"lda.{m}.{q4}"))
                ldx3 = ld(xq_sb[:, pi * D:(pi + 1) * D], xqv[m],
                          eng="scalar", deps=[lnx.get((m - 2, "evy"))] +
                          (ph2_last if m < 2 else []), dkey=f"ldx3{m % 2}",
                          tag=f"ldx3.{m}")

                def f_mm(t, pi=pi):
                    inst = None
                    for hh in range(16):
                        for dg in range(4):
                            inst = t.matmul(
                                y1ps[:, 512 * dg:512 * dg + 512],
                                aotm[:, pi * D + 128 * hh:pi * D + 128 * hh + 128],
                                wo_sb[:, hh * D + 512 * dg:hh * D + 512 * dg + 512],
                                start=(hh == 0), stop=(hh == 15))
                    return inst
                mm = G.op("tensor", f_mm,
                          deps=[lda[3], o_wo, lnx.get((m - 1, "evy"))] +
                               (ph2_guard if m == 0 else []),
                          tag=f"womm.{m}")
                prev_mm[m] = mm
                if m >= 1:
                    make_tr3(m - 1)

                def f_evy(v, pi=pi):
                    inst = None
                    for dg in range(4):
                        inst = v.tensor_add(
                            ypre[:, 512 * dg:512 * dg + 512],
                            y1ps[:, 512 * dg:512 * dg + 512],
                            xq_sb[:, pi * D + 512 * dg:pi * D + 512 * dg + 512])
                    return inst
                evy = G.op("vector", f_evy,
                           deps=[mm, ldx3, lnx.get((m - 1, "l9")),
                                 lnx.get((m - 1, "l3"))],
                           tag=f"evy.{m}")
                lnx[(m, "evy")] = evy
                _ln_chain(G, "p3", m, evy, ypre, sqd, stat, consts,
                          (CID, CEPS, CNH), y1f, pi, lnx, AXP=AX,
                          war=sty1.get(m - 2))
                l9 = G.op("vector",
                          lambda v, pi=pi: v.tensor_copy(
                              y1b[:, :], y1f[:, pi * D:(pi + 1) * D]),
                          deps=[lnx[(m, "l8")], evt3.get((m - 1, 3))],
                          tag=f"p3ln9.{m}")
                lnx[(m, "l9")] = l9
                sty1[m] = ld(y1v[m], y1f[:, pi * D:(pi + 1) * D], eng="scalar",
                             deps=[lnx[(m, "l8")]], dkey=f"sty1{m % 2}",
                             tag=f"sty1.{m}")
            make_tr3(15)
            ph3_last = [sty1[15], sty1[14]] + [evt3[(15, r)] for r in range(4)] + \
                [lnx[(15, "evy")], lnx[(15, "l9")]]
            ph3_guard = [lnx[(15, "evy")], lnx[(15, "l3")],
                         evt3[(15, 2)], evt3[(15, 3)]]

        if upto < 4:
            _finish_stub(G, nc, es, out, ph3_last)
            G.emit(nc, es)
            return nc

        hmuls, sth = {}, {}
        p4 = contextlib.ExitStack()
        with p4:
            P = p4.enter_context
            w1c = P(nc.sbuf_tensor("w1c", [128, 2 * D], BF16))
            w3c = P(nc.sbuf_tensor("w3c", [128, 2 * D], BF16))
            silu_sb = P(nc.sbuf_tensor("silu_sb", [128, 2 * 1024], F32))
            hidst = P(nc.sbuf_tensor("hidst", [128, 2 * D], BF16))
            gps = [P(nc.psum_tensor(f"gps{i}", [128, 1024], F32)) for i in range(2)]
            lps = [P(nc.psum_tensor(f"lps{i}", [128, 1024], F32)) for i in range(2)]

            for hh in range(64):
                pi = hh % 2
                wdep = [hmuls.get((hh - 2, 1))] + (ph3_last if hh < 2 else [])
                ldw1 = ld(_r3(w1c, pi * D, 16, 128),
                          w1til[hh].rearrange("n p c -> p n c"), deps=wdep, dkey=f"ldw1{hh % 2}", tag=f"ldw1.{hh}")
                ldw3 = ld(_r3(w3c, pi * D, 16, 128),
                          w3til[hh].rearrange("n p c -> p n c"), deps=wdep, dkey=f"ldw3{hh % 2}", tag=f"ldw3.{hh}")
                for half in range(2):
                    def f_g(t, pi=pi, half=half):
                        inst = None
                        for dd in range(16):
                            for tg in range(2):
                                inst = t.matmul(
                                    gps[half][:, 512 * tg:512 * tg + 512],
                                    w1c[:, pi * D + 128 * dd:pi * D + 128 * dd + 128],
                                    y1t_sb[:, 2048 * dd + 1024 * half + 512 * tg:
                                           2048 * dd + 1024 * half + 512 * tg + 512],
                                    start=(dd == 0), stop=(dd == 15))
                        return inst
                    depg = [ldw1, hmuls.get((hh - 1, half))]
                    if hh == 0 and half == 0:
                        depg += ph3_guard
                    mmg = G.op("tensor", f_g, deps=depg, tag=f"mmg.{hh}.{half}")

                    def f_l(t, pi=pi, half=half):
                        inst = None
                        for dd in range(16):
                            for tg in range(2):
                                inst = t.matmul(
                                    lps[half][:, 512 * tg:512 * tg + 512],
                                    w3c[:, pi * D + 128 * dd:pi * D + 128 * dd + 128],
                                    y1t_sb[:, 2048 * dd + 1024 * half + 512 * tg:
                                           2048 * dd + 1024 * half + 512 * tg + 512],
                                    start=(dd == 0), stop=(dd == 15))
                        return inst
                    mml = G.op("tensor", f_l,
                               deps=[ldw3, hmuls.get((hh - 1, half))],
                               tag=f"mml.{hh}.{half}")
                    sil = G.op("scalar",
                               lambda s, half=half: s.activation(
                                   silu_sb[:, 1024 * half:1024 * half + 1024],
                                   gps[half][:, :], AF.Silu),
                               deps=[mmg, hmuls.get((hh - 1, half))],
                               tag=f"sil.{hh}.{half}")
                    hmuls[(hh, half)] = G.op(
                        "vector",
                        lambda v, half=half, pi=pi: v.tensor_mul(
                            hidst[:, pi * D + 1024 * half:pi * D + 1024 * half + 1024],
                            lps[half][:, :],
                            silu_sb[:, 1024 * half:1024 * half + 1024]),
                        deps=[mml, sil, sth.get((hh - 2, 3))],
                        tag=f"hmul.{hh}.{half}")
                for r in range(4):
                    src = _r3(hidst, pi * D + 512 * r, 4, 128)
                    sth[(hh, r)] = ld(hid_hbm[4 * r:4 * r + 4, hh].rearrange(
                        "a p c -> p a c"), src, eng="sync",
                                      deps=[hmuls[(hh, r // 2)]],
                                      dkey=f"sth{hh % 2}", tag=f"sth.{hh}.{r}")
            up_last = [sth[(63, r)] for r in range(4)] + [sth[(62, 3)]]
            up_guard = [hmuls[(63, 0)], hmuls[(63, 1)], hmuls[(62, 0)], hmuls[(62, 1)]]

        p34.close()
        p4b = contextlib.ExitStack()
        with p4b:
            P = p4b.enter_context
            hidt = P(nc.sbuf_tensor("hidt", [128, 4 * 64 * 128], BF16))
            w2c = P(nc.sbuf_tensor("w2c", [128, 2 * 64 * 256], BF16))
            dstg = P(nc.sbuf_tensor("dstg", [128, 2 * 256], F32))
            y2ld = P(nc.sbuf_tensor("y2ld", [128, 2 * D], F32))
            y1ld = P(nc.sbuf_tensor("y1ld", [128, 2 * D], F32))
            ypre4 = P(nc.sbuf_tensor("ypre4", [128, D], F32))
            sqd4 = P(nc.sbuf_tensor("sqd4", [128, D], F32))
            outf = P(nc.sbuf_tensor("outf", [128, 2 * D], F32))
            dps = [P(nc.psum_tensor(f"dps{i}", [128, 256], F32)) for i in range(2)]

            ldh, ldw2, mmd, evd, std = {}, {}, {}, {}, {}
            outv = out.rearrange("(n p) d -> n p d", p=128)
            ln4 = {}
            sto = {}

            def emit_ln4_group(g):
                for tt in range(4 * g, 4 * g + 4):
                    pi = tt % 2
                    l2d = ld(y2ld[:, pi * D:(pi + 1) * D], y2f_hbm[tt],
                             eng="gpsimd", deps=[std[(g, 7, tt)],
                                                 ln4.get((tt - 2, "evy"))],
                             dkey=f"ly2{tt % 2}", tag=f"ly2.{tt}")
                    l1d = ld(y1ld[:, pi * D:(pi + 1) * D], y1v[tt],
                             eng="gpsimd", deps=[sty1[15], sty1[14],
                                                 ln4.get((tt - 2, "evy"))],
                             dkey=f"ly1{tt % 2}", tag=f"ly1.{tt}")

                    def f_add(v, pi=pi):
                        inst = None
                        for dg in range(2):
                            inst = v.tensor_add(
                                ypre4[:, 1024 * dg:1024 * dg + 1024],
                                y2ld[:, pi * D + 1024 * dg:pi * D + 1024 * dg + 1024],
                                y1ld[:, pi * D + 1024 * dg:pi * D + 1024 * dg + 1024])
                        return inst
                    evy4 = G.op("vector", f_add,
                                deps=[l2d, l1d, ln4.get((tt - 1, "l9")),
                                      ln4.get((tt - 1, "l3"))],
                                tag=f"evy4.{tt}")
                    ln4[(tt, "evy")] = evy4
                    _ln_chain(G, "p4", tt, evy4, ypre4, sqd4, stat, consts,
                              (CID, CEPS, CNH), outf, pi, ln4, AXP=AX,
                              war=sto.get(tt - 2))
                    ln4[(tt, "l9")] = ln4[(tt, "l8")]
                    sto[tt] = ld(outv[tt], outf[:, pi * D:(pi + 1) * D],
                                 eng="gpsimd", deps=[ln4[(tt, "l8")]],
                                 dkey=f"sto{tt % 2}", tag=f"sto.{tt}")

            for tg4 in range(4):
                if tg4 >= 2:
                    emit_ln4_group(tg4 - 2)
                for i in range(4):
                    tt = 4 * tg4 + i
                    ldh[tt] = ld(_r3(hidt, i * 64 * 128, 64, 128),
                                 hid_hbm[tt].rearrange("n p c -> p n c"), eng="sync",
                                 deps=[up_last[3], up_last[4],
                                       mmd.get((tg4 - 1, 7, 4 * (tg4 - 1) + i))],
                                 dkey=f"ldh{tg4 % 2}", tag=f"ldh.{tt}")
                for e8 in range(8):
                    pi = e8 % 2
                    ldw2[(tg4, e8)] = ld(
                        _r3(w2c, pi * 64 * 256, 64, 256),
                        w2pp[e8].rearrange("n p c -> p n c"), eng="scalar",
                        deps=[w2pp_done, w2pp_done2,
                              mmd.get((tg4, e8 - 2, 4 * tg4 + 3)),
                              mmd.get((tg4 - 1, e8 + 6, 4 * (tg4 - 1) + 3))] +
                             (up_guard + up_last if tg4 == 0 and e8 < 2 else []),
                        dkey=f"ldw2{e8 % 2}", tag=f"ldw2.{tg4}.{e8}")
                    for i in range(4):
                        tt = 4 * tg4 + i

                        def f_d(t, i=i, pi=pi, tt=tt):
                            inst = None
                            for hh in range(64):
                                inst = t.matmul(
                                    dps[tt % 2][:, :],
                                    hidt[:, i * 64 * 128 + 128 * hh:
                                         i * 64 * 128 + 128 * hh + 128],
                                    w2c[:, pi * 64 * 256 + 256 * hh:
                                        pi * 64 * 256 + 256 * hh + 256],
                                    start=(hh == 0), stop=(hh == 63))
                            return inst
                        depd = [ldh[tt], ldw2[(tg4, e8)],
                                evd.get((tg4, e8, tt - 2)),
                                evd.get((tg4, e8 - 1, tt + 2))]
                        if tg4 == 0 and e8 == 0 and i == 0:
                            depd += up_guard + [lnx[(15, "l3")]]
                        mmd[(tg4, e8, tt)] = G.op("tensor", f_d, deps=depd,
                                                  tag=f"mmd.{tg4}.{e8}.{tt}")
                        evd[(tg4, e8, tt)] = G.op(
                            "vector",
                            lambda v, tt=tt: v.tensor_copy(
                                dstg[:, 256 * (tt % 2):256 * (tt % 2) + 256],
                                dps[tt % 2][:, :]),
                            deps=[mmd[(tg4, e8, tt)], std.get((tg4, e8, tt - 2)),
                                  std.get((tg4, e8 - 1, tt + 2)),
                                  std.get((tg4 - 1, 7, 4 * (tg4 - 1) + 2 + tt % 2))
                                  if e8 == 0 else None],
                            tag=f"evd.{tg4}.{e8}.{tt}")
                        std[(tg4, e8, tt)] = ld(
                            y2f_hbm[tt, :, 256 * e8:256 * e8 + 256],
                            dstg[:, 256 * (tt % 2):256 * (tt % 2) + 256],
                            eng="sync", deps=[evd[(tg4, e8, tt)]],
                            dkey=f"std{tt % 2}", tag=f"std.{tg4}.{e8}.{tt}")

            emit_ln4_group(2)
            emit_ln4_group(3)

            G.op("sync", lambda sy: sy.nop(), deps=[sto[i] for i in range(16)],
                 tag="fence")

        G.emit(nc, es)
    return nc


def _ln_chain(G, pfx, m, evy, ypre, sqd, stat, consts, cc, yout, pi, store, AXP,
              war=None):
    CID, CEPS, CNH = cc
    l1 = G.op("vector",
              lambda v: v.reduce_sum(stat[:, 0:1], ypre[:, :], axis=AXP, negate=True),
              deps=[evy, store.get((m - 1, "l8"))], tag=f"{pfx}ln1.{m}")
    l2 = G.op("vector",
              lambda v: v.tensor_scalar_mul(stat[:, 1:2], stat[:, 0:1],
                                            consts[:, CID:CID + 1]),
              deps=[l1], tag=f"{pfx}ln2.{m}")
    l3 = G.op("scalar",
              lambda s: s.activation(sqd[:, :], ypre[:, :], AF.Square,
                                     bias=stat[:, 1:2], scale=1.0,
                                     accum_out=stat[:, 2:3]),
              deps=[l2, store.get((m - 1, "l3"))], tag=f"{pfx}ln3.{m}")
    store[(m, "l3")] = l3
    l4 = G.op("vector",
              lambda v: v.tensor_scalar(stat[:, 3:4], stat[:, 2:3],
                                        consts[:, CID:CID + 1],
                                        consts[:, CEPS:CEPS + 1],
                                        ALU.mult, ALU.add),
              deps=[l3], tag=f"{pfx}ln4.{m}")
    l5 = G.op("scalar",
              lambda s: s.activation(stat[:, 4:5], stat[:, 3:4], AF.Ln),
              deps=[l4], tag=f"{pfx}ln5.{m}")
    l6 = G.op("scalar",
              lambda s: s.activation(stat[:, 5:6], stat[:, 4:5], AF.Exp,
                                     scale=consts[:, CNH:CNH + 1]),
              deps=[l5], tag=f"{pfx}ln6.{m}")
    l7 = G.op("vector",
              lambda v: v.tensor_mul(stat[:, 6:7], stat[:, 1:2], stat[:, 5:6]),
              deps=[l6, l2], tag=f"{pfx}ln7.{m}")
    l8 = G.op("vector",
              lambda v: v.tensor_scalar(yout[:, pi * 2048:(pi + 1) * 2048], ypre[:, :],
                                        stat[:, 5:6], stat[:, 6:7],
                                        ALU.mult, ALU.add),
              deps=[l7, l3, war], tag=f"{pfx}ln8.{m}")
    store[(m, "l8")] = l8
    return l8


def _finish_stub(G, nc, es, out, deps):
    z = es.enter_context(nc.sbuf_tensor("zstub", [128, D], F32))
    zo = G.op("gpsimd", lambda g: g.memset(z[:, :], 0.0), tag="zstub")
    outv = out.rearrange("(n p) d -> n p d", p=128)
    last = []
    for m in range(16):
        last.append(G.op("sync", lambda sy, m=m: sy.dma_start(outv[m], z[:, :]),
                         deps=[zo] + list(deps), dma=True, tag=f"zo.{m}"))
    G.op("sync", lambda sy: sy.nop(), deps=[last[-1]], tag="fence")


def _sylvester(n):
    H = np.array([[1.0]])
    while H.shape[0] < n:
        H = np.block([[H, H], [H, -H]])
    return H.astype(np.float32)


def make_core_inputs(x, W, core):
    b, g = core // 4, core % 4
    ins = dict(W)
    ins["xb"] = np.ascontiguousarray(x[b])
    ins["xq"] = np.ascontiguousarray(x[b, g::4, :])
    inv = (1.0 / 10000.0 ** (np.arange(0, RD, 2, dtype=np.float64) / RD))
    posq = (g + 4 * np.arange(NQ, dtype=np.float64))
    ang_q = np.outer(inv, posq)
    cq, sq = np.cos(ang_q), np.sin(ang_q)
    ins["cosq"] = (np.concatenate([cq, cq], 0) * SCALE).astype(np.float32)
    ins["sinq"] = (np.concatenate([-sq, sq], 0) * SCALE).astype(np.float32)
    posk = np.arange(L, dtype=np.float64)
    ang_k = np.outer(inv, posk)
    ck, sk = np.cos(ang_k), np.sin(ang_k)
    ins["cosk"] = np.concatenate([ck, ck], 0).astype(np.float32)
    ins["sink"] = np.concatenate([-sk, sk], 0).astype(np.float32)
    p_ = np.arange(128)[:, None]
    i_ = np.arange(512)[None, :]
    tiles = [(128 * cc + p_ <= g + 4 * i_).astype(np.float32) for cc in range(16)]
    ins["maskp"] = np.ascontiguousarray(np.concatenate(tiles, axis=1))
    return ins


def make_shared_weights(inputs):
    W = {}
    for k in ("wq_down", "wq_up", "wq_rope", "wkv_down", "wk_up", "wv_up",
              "wk_rope", "wo"):
        W[k] = np.ascontiguousarray(np.asarray(inputs["L0_attn_" + k], np.float32))
    W1 = np.asarray(inputs["L0_ffn_W1"], np.float32)
    W3 = np.asarray(inputs["L0_ffn_W3"], np.float32)
    W2 = np.asarray(inputs["L0_ffn_W2"], np.float32)
    W["w1til"] = np.ascontiguousarray(W1.reshape(64, 128, 16, 128).transpose(0, 2, 3, 1))
    W["w3til"] = np.ascontiguousarray(W3.reshape(64, 128, 16, 128).transpose(0, 2, 3, 1))
    W["w2r"] = np.ascontiguousarray(W2.T).reshape(128, 64 * D)
    H64 = _sylvester(64)
    W["h64i2"] = np.ascontiguousarray(np.kron(H64, np.eye(2, dtype=np.float32)))
    W["h128"] = _sylvester(128)
    W["idin"] = np.eye(128, dtype=np.float32)
    return W


def kernel(**inputs):
    x = np.asarray(inputs["L0_attn_x"], np.float32)
    W = make_shared_weights(inputs)
    nc = build_program(upto=4)
    in_maps = [make_core_inputs(x, W, c) for c in range(8)]
    res = run_bass_kernel_spmd(nc, in_maps, core_ids=list(range(8)))
    y = np.empty((B, L, D), np.float32)
    for c in range(8):
        y[c // 4, c % 4::4, :] = res.results[c]["out"]
    return y


# version 06 (final)
# speedup vs baseline: 1.4916x; 1.1967x over previous
import contextlib
import math

import numpy as np

import concourse.bass as bass
import concourse.mybir as mybir
from concourse.bass_utils import run_bass_kernel_spmd

F32 = mybir.dt.float32
BF16 = mybir.dt.bfloat16
AF = mybir.ActivationFunctionType
AX = mybir.AxisListType.X
ALU = mybir.AluOpType

B, L, D = 2, 8192, 2048
NH, HD, RD, QR, KVR = 16, 128, 64, 768, 512
HF = 8192
NQ = 2048
SCALE = 1.0 / math.sqrt(HD + RD)
BSHIFT = 32.0
EPS = 1e-5
VS = 130
NKC = L // 128

ENGINES = ("tensor", "vector", "scalar", "sync", "gpsimd")


class Op:
    __slots__ = ("eng", "fn", "deps", "dma", "dkey", "needed", "ctr", "val", "tag")

    def __init__(self, eng, fn, deps, dma, dkey, tag):
        self.eng, self.fn, self.dma, self.tag = eng, fn, dma, tag
        self.dkey = dkey
        self.deps = [d for d in deps if d is not None]
        self.needed = False
        self.ctr = None
        self.val = None


class Graph:
    def __init__(self):
        self.streams = {e: [] for e in ENGINES}
        self.all = []

    def op(self, eng, fn, deps=(), dma=False, dkey=None, tag=""):
        if dma and dkey is None:
            dkey = "bulk"
        o = Op(eng, fn, deps, dma, dkey, tag)
        self.streams[eng].append(o)
        self.all.append(o)
        return o

    def check(self):
        done = set()
        idx = {e: 0 for e in ENGINES}
        while len(done) < len(self.all):
            progress = False
            for e in ENGINES:
                while idx[e] < len(self.streams[e]):
                    o = self.streams[e][idx[e]]
                    if all(d in done for d in o.deps):
                        done.add(o)
                        idx[e] += 1
                        progress = True
                    else:
                        break
            if not progress:
                stuck = []
                for e in ENGINES:
                    if idx[e] < len(self.streams[e]):
                        o = self.streams[e][idx[e]]
                        missing = [d.tag for d in o.deps if d not in done]
                        stuck.append(f"{e}[{idx[e]}] {o.tag} waits {missing[:3]}")
                raise RuntimeError("op graph deadlock: " + " | ".join(stuck))

    def emit(self, nc, es):
        cc, dc = {}, {}
        for e in ENGINES:
            cc[e] = [es.enter_context(nc.semaphore(f"c_{e}")), 0]
        for o in self.all:
            if o.dma:
                o.needed = True
            for d in o.deps:
                d.needed = True
        for e in ENGINES:
            for o in self.streams[e]:
                if o.needed:
                    if o.dma:
                        k = (e, o.dkey)
                        if k not in dc:
                            dc[k] = [es.enter_context(nc.semaphore(f"d_{e}_{o.dkey}")), 0]
                        c = dc[k]
                        c[1] += 16
                    else:
                        c = cc[e]
                        c[1] += 1
                    o.ctr = c
                    o.val = c[1]
        self.check()
        blk = es.enter_context(nc.Block("main"))
        graph = self

        def make_cb(e):
            def cb(eng):
                covered = {}
                for o in graph.streams[e]:
                    want = {}
                    for d in o.deps:
                        assert d.val is not None, f"dep {d.tag} of {o.tag} not inc'd"
                        key = id(d.ctr)
                        if want.get(key, (None, -1))[1] < d.val:
                            want[key] = (d.ctr, d.val)
                    for key, (ctr, val) in want.items():
                        if covered.get(key, -1) < val:
                            eng.wait_ge(ctr[0], val)
                            covered[key] = val
                    inst = o.fn(eng)
                    if o.needed:
                        assert inst is not None, f"op {o.tag} returned no instruction"
                        inst.then_inc(o.ctr[0], 16 if o.dma else 1)
            return cb

        blk.tensor(make_cb("tensor"))
        blk.vector(make_cb("vector"))
        blk.scalar(make_cb("scalar"))
        blk.sync(make_cb("sync"))
        blk.gpsimd(make_cb("gpsimd"))


def _r3(sb, lo, n, c):
    return sb[:, lo:lo + n * c].rearrange("p (a c) -> p a c", c=c)


def build_program(upto=4, dump=()):
    nc = bass.Bass("TRN2", target_bir_lowering=False)
    G = Graph()

    def dram(name, shape, dt, kind="Internal"):
        if name in dump and kind == "Internal":
            kind = "ExternalOutput"
        return nc.dram_tensor(name, shape, dt, kind=kind).ap()

    xb = dram("xb", [L, D], F32, "ExternalInput")
    xq = dram("xq", [NQ, D], F32, "ExternalInput")
    wq_down = dram("wq_down", [D, QR], F32, "ExternalInput")
    wq_up = dram("wq_up", [QR, NH * HD], F32, "ExternalInput")
    wq_rope = dram("wq_rope", [QR, NH * RD], F32, "ExternalInput")
    wkv_down = dram("wkv_down", [D, KVR], F32, "ExternalInput")
    wk_up = dram("wk_up", [KVR, NH * HD], F32, "ExternalInput")
    wv_up = dram("wv_up", [KVR, NH * HD], F32, "ExternalInput")
    wk_rope = dram("wk_rope", [D, RD], F32, "ExternalInput")
    wo = dram("wo", [NH * HD, D], F32, "ExternalInput")
    w1til = dram("w1til", [64, 16, 128, 128], F32, "ExternalInput")
    w3til = dram("w3til", [64, 16, 128, 128], F32, "ExternalInput")
    w2r = dram("w2r", [128, 64 * D], F32, "ExternalInput")
    h64i2 = dram("h64i2", [128, 128], F32, "ExternalInput")
    h128 = dram("h128", [128, 128], F32, "ExternalInput")
    idin = dram("idin", [128, 128], F32, "ExternalInput")
    cosq = dram("cosq", [64, NQ], F32, "ExternalInput")
    sinq = dram("sinq", [64, NQ], F32, "ExternalInput")
    cosk = dram("cosk", [64, L], F32, "ExternalInput")
    sink = dram("sink", [64, L], F32, "ExternalInput")
    maskp = dram("maskp", [128, 16 * 512], F32, "ExternalInput")

    out = dram("out", [NQ, D], F32, "ExternalOutput")

    w2i = dram("w2i", [HF, D], BF16)
    w2pp = dram("w2pp", [8, 64, 128, 256], BF16)
    qt_hbm = dram("qt_hbm", [NH, 192, NQ], BF16)
    kt_hbm = dram("kt_hbm", [NH, 128, L], BF16)
    kr_hbm = dram("kr_hbm", [64, L], BF16)
    v_hbm = dram("v_hbm", [NKC, 128, NH * VS], BF16)
    aot_hbm = dram("aot_hbm", [NH, 128, NQ], BF16)
    y1_hbm = dram("y1_hbm", [NQ, D], F32)
    hid_hbm = dram("hid_hbm", [16, 64, 128, 128], BF16)
    y2f_hbm = dram("y2f_hbm", [16, 128, D], F32)

    es = contextlib.ExitStack()
    with es:
        E = es.enter_context
        ident = E(nc.sbuf_tensor("ident", [128, 128], BF16))
        consts = E(nc.sbuf_tensor("consts", [128, 8], F32))
        CB, CID, CEPS, CSC, CNH, CHS = 0, 1, 2, 3, 4, 5
        stat = E(nc.sbuf_tensor("stat", [128, 8], F32))

        def ld(dst_ap, src_ap, eng="gpsimd", deps=(), dkey="bulk", tag="ld"):
            return G.op(eng, lambda g, d=dst_ap, s=src_ap: g.dma_start(d, s),
                        deps=deps, dma=True, dkey=dkey, tag=tag)

        p1 = contextlib.ExitStack()
        with p1:
            P = p1.enter_context
            qlat = P(nc.sbuf_tensor("qlat", [128, 6 * NQ], BF16))
            kvlat = P(nc.sbuf_tensor("kvlat", [128, 4 * L], BF16))

            def memset_consts(gp):
                gp.memset(consts[:, CB:CB + 1], -BSHIFT)
                gp.memset(consts[:, CID:CID + 1], 1.0 / 2048.0)
                gp.memset(consts[:, CEPS:CEPS + 1], EPS)
                gp.memset(consts[:, CSC:CSC + 1], SCALE)
                gp.memset(consts[:, CNH:CNH + 1], -0.5)
                return gp.memset(consts[:, CHS:CHS + 1], 1.0 / math.sqrt(8192.0))
            o_consts = G.op("gpsimd", memset_consts, tag="consts")

            s1ab = contextlib.ExitStack()
            with s1ab:
                Q = s1ab.enter_context
                w2in = Q(nc.sbuf_tensor("w2in", [128, 2 * 1024], BF16))
                w2st = Q(nc.sbuf_tensor("w2st", [128, 2 * 1024], BF16))
                h64_sb = Q(nc.sbuf_tensor("h64_sb", [128, 128], BF16))
                h128_sb = Q(nc.sbuf_tensor("h128_sb", [128, 128], BF16))
                xin = Q(nc.sbuf_tensor("xin", [128, 2 * D], BF16))
                xt = Q(nc.sbuf_tensor("xt", [128, 2 * 16 * 512], BF16))
                wqd_sb = Q(nc.sbuf_tensor("wqd_sb", [128, 16 * QR], BF16))
                wkvd_sb = Q(nc.sbuf_tensor("wkvd_sb", [128, 16 * KVR], BF16))
                wkr_sb = Q(nc.sbuf_tensor("wkr_sb", [128, 16 * RD], BF16))
                ckb = Q(nc.sbuf_tensor("ckb", [64, 2 * 512], F32))
                skb = Q(nc.sbuf_tensor("skb", [64, 2 * 512], F32))
                rtmpK = Q(nc.sbuf_tensor("rtmpK", [64, 3 * 512], F32))
                krst = Q(nc.sbuf_tensor("krst", [64, 2 * 512], BF16))

                o_h64 = ld(h64_sb[:, :], h64i2, dkey="setup", tag="h64")
                o_h128 = ld(h128_sb[:, :], h128, dkey="setup", tag="h128")
                o_id = ld(ident[:, :], idin, dkey="setup", tag="id")
                o_wqd = ld(_r3(wqd_sb, 0, 16, QR),
                           wq_down.rearrange("(n p) c -> p n c", p=128), dkey="setup", tag="wqd")
                o_wkvd = ld(_r3(wkvd_sb, 0, 16, KVR),
                            wkv_down.rearrange("(n p) c -> p n c", p=128), dkey="setup", tag="wkvd")
                o_wkr = ld(_r3(wkr_sb, 0, 16, RD),
                           wk_rope.rearrange("(n p) c -> p n c", p=128), dkey="setup", tag="wkr")
                o_setup = o_wkr

                ps0 = contextlib.ExitStack()
                with ps0:
                    w2ps = [ps0.enter_context(nc.psum_tensor(f"w2ps{i}", [128, 1024], F32))
                            for i in range(2)]
                    mm1, ev1, st1 = {}, {}, {}
                    st2 = {}
                    for it in range(128):
                        b1, hf = it // 2, it % 2
                        pi = it % 2
                        ld1 = ld(w2in[:, pi * 1024:(pi + 1) * 1024],
                                 w2r[:, b1 * D + 1024 * hf:b1 * D + 1024 * hf + 1024],
                                 deps=[mm1.get(it - 2)], dkey=f"w2ld1{it % 2}", tag=f"w2ld1.{it}")

                        def f_mm1(t, pi=pi):
                            inst = None
                            for dcn in range(2):
                                inst = t.matmul(
                                    w2ps[pi][:, 512 * dcn:512 * dcn + 512], h64_sb[:, :],
                                    w2in[:, pi * 1024 + 512 * dcn:pi * 1024 + 512 * dcn + 512],
                                    start=True, stop=True)
                            return inst
                        mm1[it] = G.op("tensor", f_mm1,
                                       deps=[ld1, o_setup, ev1.get(it - 2)], tag=f"w2mm1.{it}")
                        ev1[it] = G.op("vector",
                                       lambda v, pi=pi: v.tensor_scalar_mul(
                                           w2st[:, pi * 1024:(pi + 1) * 1024],
                                           w2ps[pi][:, :], consts[:, CHS:CHS + 1]),
                                       deps=[mm1[it], st1.get(it - 2), o_consts],
                                       tag=f"w2ev1.{it}")
                        st1[it] = ld(
                            w2i.rearrange("(p b) d -> p b d", b=64)[:, b1,
                                                                   1024 * hf:1024 * hf + 1024],
                            w2st[:, pi * 1024:(pi + 1) * 1024], eng="sync",
                            deps=[ev1[it]], dkey=f"w2st1{it % 2}", tag=f"w2st1.{it}")
                    mm2, ev2 = {}, {}
                    for it in range(128):
                        a, hf = it // 2, it % 2
                        pi = it % 2
                        ld2 = ld(w2in[:, pi * 1024:(pi + 1) * 1024],
                                 w2i[a * 128:(a + 1) * 128, 1024 * hf:1024 * hf + 1024],
                                 eng="sync", deps=[st1[127], st1[126], mm2.get(it - 2)],
                                 dkey=f"w2ld2{it % 2}", tag=f"w2ld2.{it}")

                        def f_mm2(t, pi=pi):
                            inst = None
                            for dcn in range(2):
                                inst = t.matmul(
                                    w2ps[pi][:, 512 * dcn:512 * dcn + 512], h128_sb[:, :],
                                    w2in[:, pi * 1024 + 512 * dcn:pi * 1024 + 512 * dcn + 512],
                                    start=True, stop=True)
                            return inst
                        mm2[it] = G.op("tensor", f_mm2,
                                       deps=[ld2, o_setup, ev2.get(it - 2)], tag=f"w2mm2.{it}")
                        ev2[it] = G.op("vector",
                                       lambda v, pi=pi: v.tensor_copy(
                                           w2st[:, pi * 1024:(pi + 1) * 1024], w2ps[pi][:, :]),
                                       deps=[mm2[it], st2.get((it - 2, 3))], tag=f"w2ev2.{it}")
                        for e4 in range(4):
                            e8 = 4 * hf + e4
                            st2[(it, e4)] = ld(
                                w2pp[e8, a, :, :],
                                w2st[:, pi * 1024 + 256 * e4:pi * 1024 + 256 * e4 + 256],
                                eng="scalar", deps=[ev2[it]], dkey=f"w2st2{it % 2}", tag=f"w2st2.{it}.{e4}")
                ph0_guard = ev2[127]
                w2pp_done = st2[(127, 3)]
                w2pp_done2 = st2[(126, 3)]

                ps1 = contextlib.ExitStack()
                with ps1:
                    tpsX = [ps1.enter_context(nc.psum_tensor(f"tps{i}", [128, 512], BF16))
                            for i in range(2)]
                    prjX = [ps1.enter_context(nc.psum_tensor(f"prj{i}", [128, 512], F32))
                            for i in range(2)]
                    prjK = ps1.enter_context(nc.psum_tensor("prjK", [64, 512], F32))

                    trops, evops, ldx = {}, {}, {}
                    ldck, ldsk = {}, {}
                    prj_users = {}
                    nprj = [0]
                    evql, evkv, ropeK, stkr = {}, {}, {}, {}

                    def qlat_consumer(ci, evdep):
                        n = ci
                        for mt in range(6):
                            pp = nprj[0] % 2
                            nprj[0] += 1

                            def f(t, mt=mt, pp=pp):
                                inst = None
                                for dd in range(16):
                                    inst = t.matmul(
                                        prjX[pp][:, :],
                                        wqd_sb[:, dd * QR + 128 * mt:dd * QR + 128 * mt + 128],
                                        xt[:, (ci % 2) * 8192 + 512 * dd:
                                           (ci % 2) * 8192 + 512 * dd + 512],
                                        start=(dd == 0), stop=(dd == 15))
                                return inst
                            mm = G.op("tensor", f,
                                      deps=[evdep, o_setup, prj_users.get(pp), ph0_guard],
                                      tag=f"qlmm.{n}.{mt}")
                            evql[(n, mt)] = G.op(
                                "vector",
                                lambda v, pp=pp, mt=mt, n=n: v.tensor_copy(
                                    qlat[:, 2048 * mt + 512 * n:2048 * mt + 512 * n + 512],
                                    prjX[pp][:, :]),
                                deps=[mm], tag=f"qlev.{n}.{mt}")
                            prj_users[pp] = evql[(n, mt)]

                    def kv_consumer(ci, evdep):
                        nb = ci - 4
                        for mt in range(4):
                            pp = nprj[0] % 2
                            nprj[0] += 1

                            def f(t, mt=mt, pp=pp, ci=ci):
                                inst = None
                                for dd in range(16):
                                    inst = t.matmul(
                                        prjX[pp][:, :],
                                        wkvd_sb[:, dd * KVR + 128 * mt:dd * KVR + 128 * mt + 128],
                                        xt[:, (ci % 2) * 8192 + 512 * dd:
                                           (ci % 2) * 8192 + 512 * dd + 512],
                                        start=(dd == 0), stop=(dd == 15))
                                return inst
                            mm = G.op("tensor", f,
                                      deps=[evdep, o_setup, prj_users.get(pp), ph0_guard],
                                      tag=f"kvmm.{nb}.{mt}")
                            evkv[(nb, mt)] = G.op(
                                "vector",
                                lambda v, pp=pp, mt=mt, nb=nb: v.tensor_copy(
                                    kvlat[:, 8192 * mt + 512 * nb:8192 * mt + 512 * nb + 512],
                                    prjX[pp][:, :]),
                                deps=[mm], tag=f"kvev.{nb}.{mt}")
                            prj_users[pp] = evkv[(nb, mt)]

                        ldck[nb] = ld(ckb[:, (nb % 2) * 512:(nb % 2) * 512 + 512],
                                      cosk[:, 512 * nb:512 * nb + 512],
                                      eng="sync", deps=[ropeK.get(("B", nb - 2))],
                                      dkey=f"ldck{nb % 2}", tag=f"ldck.{nb}")
                        ldsk[nb] = ld(skb[:, (nb % 2) * 512:(nb % 2) * 512 + 512],
                                      sink[:, 512 * nb:512 * nb + 512],
                                      eng="sync", deps=[ropeK.get(("D", nb - 2))],
                                      dkey=f"ldsk{nb % 2}", tag=f"ldsk.{nb}")

                        def f_kr(t, ci=ci):
                            inst = None
                            for dd in range(16):
                                inst = t.matmul(
                                    prjK[:, :],
                                    wkr_sb[:, dd * RD:dd * RD + RD],
                                    xt[:, (ci % 2) * 8192 + 512 * dd:
                                       (ci % 2) * 8192 + 512 * dd + 512],
                                    start=(dd == 0), stop=(dd == 15))
                            return inst
                        mmkr = G.op("tensor", f_kr,
                                    deps=[evdep, o_setup, ropeK.get(("D", nb - 1)), ph0_guard],
                                    tag=f"krmm.{nb}")
                        ropeK[("A", nb)] = G.op(
                            "vector",
                            lambda v: v.tensor_copy(rtmpK[:, 0:512], prjK[:, :]),
                            deps=[mmkr, ropeK.get(("D", nb - 1))], tag=f"krA.{nb}")

                        def f_rot(v):
                            v.tensor_copy(rtmpK[0:32, 512:1024], prjK[32:64, :])
                            return v.tensor_copy(rtmpK[32:64, 512:1024], prjK[0:32, :])
                        ropeK[("R", nb)] = G.op("vector", f_rot,
                                                deps=[mmkr, ropeK.get(("D", nb - 1))],
                                                tag=f"krR.{nb}")
                        ropeK[("B", nb)] = G.op(
                            "vector",
                            lambda v, nb=nb: v.tensor_mul(
                                rtmpK[:, 1024:1536], rtmpK[:, 0:512],
                                ckb[:, (nb % 2) * 512:(nb % 2) * 512 + 512]),
                            deps=[ropeK[("A", nb)], ldck[nb]], tag=f"krB.{nb}")

                        ropeK[("C", nb)] = G.op(
                            "vector",
                            lambda v, nb=nb: v.tensor_mul(
                                rtmpK[:, 512:1024], rtmpK[:, 512:1024],
                                skb[:, (nb % 2) * 512:(nb % 2) * 512 + 512]),
                            deps=[ropeK[("B", nb)], ropeK[("R", nb)], ldsk[nb]],
                            tag=f"krC.{nb}")
                        ropeK[("D", nb)] = G.op(
                            "vector",
                            lambda v, nb=nb: v.tensor_add(
                                krst[:, (nb % 2) * 512:(nb % 2) * 512 + 512],
                                rtmpK[:, 1024:1536], rtmpK[:, 512:1024]),
                            deps=[ropeK[("C", nb)], stkr.get(nb - 2)],
                            tag=f"krD.{nb}")
                        stkr[nb] = ld(kr_hbm[:, 512 * nb:512 * nb + 512],
                                      krst[:, (nb % 2) * 512:(nb % 2) * 512 + 512],
                                      eng="sync", deps=[ropeK[("D", nb)]],
                                      dkey=f"stkr{nb % 2}", tag=f"stkr.{nb}")

                    def x_chunk(ci, srcrows, consumers):
                        for mtl in range(4):
                            gt = 4 * ci + mtl
                            pig = gt % 2
                            ldx[gt] = ld(
                                xin[:, pig * D:(pig + 1) * D],
                                srcrows[128 * mtl:128 * mtl + 128, :],
                                deps=[trops.get(4 * (gt - 2) + 3)], dkey=f"ldx{gt % 2}", tag=f"ldx.{gt}")
                            for r in range(4):
                                R = 4 * gt + r
                                dd0 = 4 * r

                                def f_tr(t, pig=pig, dd0=dd0, R=R):
                                    inst = None
                                    for k in range(4):
                                        inst = t.transpose(
                                            tpsX[R % 2][:, 128 * k:128 * k + 128],
                                            xin[:, pig * D + 128 * (dd0 + k):
                                                pig * D + 128 * (dd0 + k) + 128],
                                            ident[:, :])
                                    return inst
                                trops[R] = G.op("tensor", f_tr,
                                                deps=[ldx[gt], o_setup, evops.get(R - 2),
                                                      ph0_guard],
                                                tag=f"tr.{R}")

                                def f_ev(v, ci=ci, mtl=mtl, dd0=dd0, R=R):
                                    dst = xt[:, (ci % 2) * 8192:(ci % 2) * 8192 + 8192]\
                                        .rearrange("p (a c) -> p a c", c=512)[
                                        :, dd0:dd0 + 4, 128 * mtl:128 * mtl + 128]
                                    return v.tensor_copy(
                                        dst, tpsX[R % 2].rearrange("p (a b) -> p a b", b=128))
                                evops[R] = G.op("vector", f_ev, deps=[trops[R]],
                                                tag=f"ev.{R}")
                        for c in consumers:
                            c(ci, evops[4 * (4 * ci + 3) + 3])

                    for n in range(4):
                        x_chunk(n, xq[512 * n:512 * (n + 1), :], [qlat_consumer])
                    for n in range(16):
                        x_chunk(4 + n, xb[512 * n:512 * (n + 1), :], [kv_consumer])

                    last_ql = evql[(3, 5)]
                    last_kv = evkv[(15, 3)]
                    last_kr = ropeK[("D", 15)]
                    ps1_guard = [last_ql, last_kv, last_kr, prj_users[0], prj_users[1],
                                 evops[4 * (4 * 19 + 3) + 3]]
                g1ab_last = [last_ql, last_kv, stkr[15], stkr[14], st1[127], st1[126],
                             w2pp_done, w2pp_done2,
                             trops[4 * (4 * 19 + 3) + 3], ps1_guard[-1]]

            s1cd = contextlib.ExitStack()
            with s1cd:
                Q = s1cd.enter_context
                wquh = Q(nc.sbuf_tensor("wquh", [128, 2 * 6 * 128], BF16))
                wqrh = Q(nc.sbuf_tensor("wqrh", [128, 2 * 6 * 64], BF16))
                wkuh = Q(nc.sbuf_tensor("wkuh", [128, 2 * 4 * 128], BF16))
                wvu_sb = Q(nc.sbuf_tensor("wvu_sb", [128, 4 * 2048], BF16))
                cq_sb = Q(nc.sbuf_tensor("cq_sb", [64, NQ], F32))
                sq_sb = Q(nc.sbuf_tensor("sq_sb", [64, NQ], F32))
                qtc_st = Q(nc.sbuf_tensor("qtc_st", [128, 2 * NQ], BF16))
                qtr_st = Q(nc.sbuf_tensor("qtr_st", [64, 2 * NQ], BF16))
                ktc_st = Q(nc.sbuf_tensor("ktc_st", [128, 2 * 2048], BF16))
                vstage = Q(nc.sbuf_tensor("vstage", [128, 2 * NH * VS], BF16))
                rtmp = Q(nc.sbuf_tensor("rtmp", [64, 3 * 2048], F32))

                o_cq = ld(cq_sb[:, :], cosq, deps=g1ab_last, dkey="g1cd", tag="cq")
                o_sq = ld(sq_sb[:, :], sinq, deps=g1ab_last, dkey="g1cd", tag="sq")
                o_wvu = ld(_r3(wvu_sb, 0, 4, 2048),
                           wv_up.rearrange("(n p) c -> p n c", p=128),
                           deps=g1ab_last, dkey="g1cd", tag="wvu")
                o_g1cd = o_wvu

                ps1c = contextlib.ExitStack()
                with ps1c:
                    qcps = [ps1c.enter_context(nc.psum_tensor(f"qcps{i}", [128, 512], F32))
                            for i in range(2)]
                    qrps = ps1c.enter_context(nc.psum_tensor("qrps", [64, 2048], F32))
                    stq = {}
                    ldqu, ldqr = {}, {}
                    qc_user = {}
                    rope_user = [None]
                    qc_lastmm, qr_lastmm = {}, {}
                    for h in range(NH):
                        pi = h % 2
                        ldqu[h] = ld(_r3(wquh, pi * 6 * 128, 6, 128),
                                     wq_up[:, 128 * h:128 * h + 128].rearrange(
                                         "(n p) c -> p n c", p=128),
                                     deps=[qc_lastmm.get(h - 2)] +
                                          (g1ab_last if h < 2 else []),
                                     dkey=f"ldqu{h % 2}", tag=f"ldqu.{h}")
                        ldqr[h] = ld(_r3(wqrh, pi * 6 * 64, 6, 64),
                                     wq_rope[:, 64 * h:64 * h + 64].rearrange(
                                         "(n p) c -> p n c", p=128),
                                     deps=[qr_lastmm.get(h - 2)] +
                                          (g1ab_last if h < 2 else []),
                                     dkey=f"ldqr{h % 2}", tag=f"ldqr.{h}")
                        evs = []
                        for cg in range(4):
                            pp = cg % 2

                            def f(t, h=h, cg=cg, pp=pp, pi=pi):
                                inst = None
                                for qq in range(6):
                                    inst = t.matmul(
                                        qcps[pp][:, :],
                                        wquh[:, pi * 768 + 128 * qq:pi * 768 + 128 * qq + 128],
                                        qlat[:, 2048 * qq + 512 * cg:2048 * qq + 512 * cg + 512],
                                        start=(qq == 0), stop=(qq == 5))
                                return inst
                            mm = G.op("tensor", f,
                                      deps=[last_ql, ldqu[h], qc_user.get(pp)] +
                                           (ps1_guard if h == 0 and cg == 0 else []),
                                      tag=f"qcmm.{h}.{cg}")
                            ev = G.op("scalar",
                                      lambda sc, cg=cg, pp=pp, pi=pi: sc.activation(
                                          qtc_st[:, pi * NQ + 512 * cg:pi * NQ + 512 * cg + 512],
                                          qcps[pp][:, :], AF.Copy, bias=0.0,
                                          scale=consts[:, CSC:CSC + 1]),
                                      deps=[mm, o_consts, stq.get((h - 2, 0))],
                                      tag=f"qcev.{h}.{cg}")
                            qc_user[pp] = ev
                            evs.append(ev)
                        qc_lastmm[h] = mm

                        def f_qr(t, h=h, pi=pi):
                            inst = None
                            for cg in range(4):
                                for qq in range(6):
                                    inst = t.matmul(
                                        qrps[:, 512 * cg:512 * cg + 512],
                                        wqrh[:, pi * 384 + 64 * qq:pi * 384 + 64 * qq + 64],
                                        qlat[:, 2048 * qq + 512 * cg:2048 * qq + 512 * cg + 512],
                                        start=(qq == 0), stop=(qq == 5))
                            return inst
                        mmqr = G.op("tensor", f_qr,
                                    deps=[last_ql, ldqr[h], rope_user[0]] +
                                         (ps1_guard if h == 0 else []),
                                    tag=f"qrmm.{h}")
                        oA_ = G.op("vector",
                                   lambda v: v.tensor_copy(rtmp[:, 0:2048], qrps[:, :]),
                                   deps=[mmqr, rope_user[0]] +
                                        (g1ab_last if h == 0 else []),
                                   tag=f"qrA.{h}")

                        def f_rot(v):
                            v.tensor_copy(rtmp[0:32, 2048:4096], qrps[32:64, :])
                            return v.tensor_copy(rtmp[32:64, 2048:4096], qrps[0:32, :])
                        oR_ = G.op("vector", f_rot, deps=[mmqr, rope_user[0]],
                                   tag=f"qrR.{h}")
                        oB_ = G.op("vector",
                                   lambda v: v.tensor_mul(rtmp[:, 4096:6144],
                                                          rtmp[:, 0:2048], cq_sb[:, :]),
                                   deps=[oA_, o_g1cd], tag=f"qrB.{h}")

                        oC_ = G.op("vector",
                                   lambda v: v.tensor_mul(rtmp[:, 2048:4096],
                                                          rtmp[:, 2048:4096], sq_sb[:, :]),
                                   deps=[oB_, oR_, o_g1cd], tag=f"qrC.{h}")
                        oD_ = G.op("vector",
                                   lambda v, pi=pi: v.tensor_add(
                                       qtr_st[:, pi * NQ:pi * NQ + NQ],
                                       rtmp[:, 4096:6144], rtmp[:, 2048:4096]),
                                   deps=[oC_, stq.get((h - 2, 1))], tag=f"qrD.{h}")
                        rope_user[0] = oD_
                        qr_lastmm[h] = mmqr
                        stq[(h, 0)] = ld(qt_hbm[h, 0:128, :],
                                         qtc_st[:, pi * NQ:(pi + 1) * NQ],
                                         eng="sync", deps=evs, dkey=f"stqc{h % 2}", tag=f"stqc.{h}")
                        stq[(h, 1)] = ld(qt_hbm[h, 128:192, :],
                                         qtr_st[:, pi * NQ:(pi + 1) * NQ],
                                         eng="sync", deps=[oD_], dkey=f"stqr{h % 2}", tag=f"stqr.{h}")
                    ps1c_guard = [qc_user[0], qc_user[1], rope_user[0]]

                ps1dk = contextlib.ExitStack()
                with ps1dk:
                    kcps = [ps1dk.enter_context(
                        nc.psum_tensor(f"kcps{i}", [128, 512], F32)) for i in range(2)]
                    stkc = {}
                    ldku = {}
                    kc_user = {}
                    kc_lastmm = {}
                    for h in range(NH):
                        ldku[h] = ld(_r3(wkuh, (h % 2) * 4 * 128, 4, 128),
                                     wk_up[:, 128 * h:128 * h + 128].rearrange(
                                         "(n p) c -> p n c", p=128),
                                     deps=[kc_lastmm.get(h - 2)] +
                                          (g1ab_last if h < 2 else []),
                                     dkey=f"ldku{h % 2}", tag=f"ldku.{h}")
                        for qq4 in range(4):
                            qg = 4 * h + qq4
                            sl = qg % 2
                            evs = []
                            for g2 in range(4):
                                g8 = 4 * qq4 + g2
                                pp = g8 % 2

                                def f(t, h=h, g8=g8, pp=pp):
                                    inst = None
                                    for kk in range(4):
                                        inst = t.matmul(
                                            kcps[pp][:, :],
                                            wkuh[:, (h % 2) * 512 + 128 * kk:
                                                 (h % 2) * 512 + 128 * kk + 128],
                                            kvlat[:, 8192 * kk + 512 * g8:
                                                  8192 * kk + 512 * g8 + 512],
                                            start=(kk == 0), stop=(kk == 3))
                                    return inst
                                mm = G.op("tensor", f,
                                          deps=[last_kv, ldku[h], kc_user.get(pp)] +
                                               (ps1c_guard if h == 0 and g8 == 0 else []),
                                          tag=f"kcmm.{h}.{g8}")
                                ev = G.op("vector",
                                          lambda v, g2=g2, pp=pp, sl=sl: v.tensor_copy(
                                              ktc_st[:, sl * 2048 + 512 * g2:
                                                     sl * 2048 + 512 * g2 + 512],
                                              kcps[pp][:, :]),
                                          deps=[mm, stkc.get(qg - 2)],
                                          tag=f"kcev.{h}.{g8}")
                                kc_user[pp] = ev
                                evs.append(ev)
                            stkc[qg] = ld(kt_hbm[h, 0:128, 2048 * qq4:2048 * qq4 + 2048]
                                          if False else
                                          kt_hbm[h, :, 2048 * qq4:2048 * qq4 + 2048],
                                          ktc_st[:, sl * 2048:(sl + 1) * 2048],
                                          eng="sync", deps=evs, dkey=f"stkc{qg % 2}", tag=f"stkc.{qg}")
                        kc_lastmm[h] = mm
                    ps1dk_guard = [kc_user[0], kc_user[1]]

                ps1dv = contextlib.ExitStack()
                with ps1dv:
                    vps = [ps1dv.enter_context(
                        nc.psum_tensor(f"vps{i}", [128, 2048], F32)) for i in range(2)]

                    def f_ones(gp):
                        for pi2 in range(2):
                            base = pi2 * NH * VS
                            inst = gp.memset(
                                _r3(vstage, base, NH, VS)[:, :, 128:130], 1.0)
                        return inst
                    o_ones = G.op("gpsimd", f_ones, deps=g1ab_last, tag="vones")

                    stv, evv, mmv = {}, {}, {}
                    for tt in range(NKC):
                        pi = tt % 2

                        def f(t, tt=tt, pi=pi):
                            inst = None
                            for kk in range(4):
                                for hg in range(4):
                                    inst = t.matmul(
                                        vps[pi][:, 512 * hg:512 * hg + 512],
                                        kvlat[:, 8192 * kk + 128 * tt:8192 * kk + 128 * tt + 128],
                                        wvu_sb[:, kk * 2048 + 512 * hg:kk * 2048 + 512 * hg + 512],
                                        start=(kk == 0), stop=(kk == 3))
                            return inst
                        mmv[tt] = G.op("tensor", f,
                                       deps=[last_kv, o_g1cd, evv.get(tt - 2)] +
                                            (ps1dk_guard if tt == 0 else []),
                                       tag=f"vmm.{tt}")

                        def f_ev(v, tt=tt, pi=pi):
                            inst = None
                            for hg in range(4):
                                dst = _r3(vstage, pi * NH * VS + VS * 4 * hg, 4, VS)[:, :, 0:128]
                                inst = v.tensor_copy(
                                    dst,
                                    vps[pi].rearrange("p (a b) -> p a b", b=128)[:, 4 * hg:4 * hg + 4, :])
                            return inst
                        evv[tt] = G.op("vector", f_ev,
                                       deps=[mmv[tt], o_ones, stv.get(tt - 2)],
                                       tag=f"vev.{tt}")
                        stv[tt] = ld(v_hbm[tt], vstage[:, pi * NH * VS:(pi + 1) * NH * VS],
                                     eng="sync", deps=[evv[tt]], dkey=f"stv{tt % 2}", tag=f"stv.{tt}")
                    ps1dv_guard = [evv[NKC - 1], evv[NKC - 2]]

                ph1_last = [stv[NKC - 1], stv[NKC - 2], stq[(NH - 1, 1)], stq[(NH - 2, 1)],
                            stq[(NH - 1, 0)], stq[(NH - 2, 0)],
                            stkc[4 * NH - 1], stkc[4 * NH - 2],
                            w2pp_done, w2pp_done2, stkr[15], stkr[14]] + ps1dv_guard

        if upto < 2:
            _finish_stub(G, nc, es, out, ph1_last)
            G.emit(nc, es)
            return nc

        p2 = contextlib.ExitStack()
        with p2:
            P = p2.enter_context
            qtc_b = P(nc.sbuf_tensor("qtc_b", [128, 2 * NQ], BF16))
            qtr_b = P(nc.sbuf_tensor("qtr_b", [128, 2 * NQ], BF16))
            ktc_b = P(nc.sbuf_tensor("ktc_b", [128, 2 * L], BF16))
            ktr_b = P(nc.sbuf_tensor("ktr_b", [128, L], BF16))
            v_b = P(nc.sbuf_tensor("v_b", [128, 2 * NKC * VS], BF16))
            pt_b = P(nc.sbuf_tensor("pt_b", [128, 3 * 1024], BF16))
            mask_sb = P(nc.sbuf_tensor("mask_sb", [128, 16 * 512], BF16))
            obf = P(nc.sbuf_tensor("obf", [128, 2 * NQ], BF16))
            rs_sb = P(nc.sbuf_tensor("rs_sb", [128, 8], F32))
            ssum_sb = P(nc.sbuf_tensor("ssum_sb", [128, 8], F32))
            aot_st = P(nc.sbuf_tensor("aot_st", [128, 2 * NQ], BF16))

            stp = [P(nc.psum_tensor(f"st{i}", [128, 1024], F32)) for i in range(2)]
            oA = P(nc.psum_tensor("oA", [128, 512], F32))
            oB = P(nc.psum_tensor("oB", [128, 512], F32))
            tp2 = [P(nc.psum_tensor(f"tp2{i}", [128, 512], BF16)) for i in range(2)]
            OFF = [0, 144]

            o_mask = ld(mask_sb[:, :], maskp, deps=ph1_last, dkey="g2", tag="maskld")
            o_ktr = ld(ktr_b[0:64, :], kr_hbm, deps=ph1_last, dkey="g2", tag="ktrld")
            o_g2 = o_ktr

            def f_zr(gp):
                gp.memset(ktr_b[64:128, :], 0.0)
                return gp.memset(qtr_b[64:128, :], 0.0)
            o_zr = G.op("gpsimd", f_zr, deps=ph1_last, tag="zrope")

            lq, lk, lv = {}, {}, {}
            sc_last, pv_last = {}, {}
            exop, pvop, scop, mkop = {}, {}, {}, {}
            sfin, trh, evh, sta = {}, {}, {}, {}
            tasks = []
            for h in range(NH):
                for j in range(4):
                    nch = 16 * (j + 1)
                    for p in range(nch // 2):
                        tasks.append((h, j, p, nch))
            NT = len(tasks)
            DEPTH = 2

            def make_loads(h):
                pi = h % 2
                dep_prev = [sc_last.get(h - 2), pv_last.get(h - 2),
                            trh.get((h - 2, 3))] + (ph1_last if h < 2 else [])
                lq[h] = ld(qtc_b[:, pi * NQ:(pi + 1) * NQ], qt_hbm[h, 0:128, :],
                           eng="sync", deps=dep_prev, dkey=f"l2h{h % 2}", tag=f"lqc.{h}")
                lq[(h, 1)] = ld(qtr_b[0:64, pi * NQ:(pi + 1) * NQ], qt_hbm[h, 128:192, :],
                                eng="sync", deps=dep_prev, dkey=f"l2h{h % 2}", tag=f"lqr.{h}")
                lk[h] = ld(ktc_b[:, pi * L:(pi + 1) * L], kt_hbm[h],
                           eng="sync", deps=dep_prev, dkey=f"l2h{h % 2}", tag=f"lkc.{h}")
                for part in range(8):
                    src = v_hbm[8 * part:8 * part + 8, :, VS * h:VS * h + VS]\
                        .rearrange("a p c -> p a c")
                    dst = _r3(v_b, pi * NKC * VS + 8 * part * VS, 8, VS)
                    lv[(h, part)] = ld(dst, src, eng="sync", deps=dep_prev,
                                       dkey=f"l2h{h % 2}", tag=f"lv.{h}.{part}")

            def make_sc(Pg):
                h, j, p, nch = tasks[Pg]
                pi = h % 2
                pp = Pg % 2
                pp3 = Pg % 3
                masked = [c for c in (2 * p, 2 * p + 1) if c >= 16 * j]

                def f_sc(t, p=p, j=j, pp=pp, pi=pi):
                    inst = None
                    for half in range(2):
                        c = 2 * p + half
                        cs = slice(512 * half, 512 * half + 512)
                        t.matmul(stp[pp][:, cs],
                                 ktc_b[:, pi * L + 128 * c:pi * L + 128 * c + 128],
                                 qtc_b[:, pi * NQ + 512 * j:pi * NQ + 512 * j + 512],
                                 start=True, stop=False)
                        inst = t.matmul(
                            stp[pp][:, cs],
                            ktr_b[:, 128 * c:128 * c + 128],
                            qtr_b[:, pi * NQ + 512 * j:pi * NQ + 512 * j + 512],
                            start=False, stop=True)
                    return inst
                scop[Pg] = G.op("tensor", f_sc,
                                deps=[lv[(h, 7)], o_g2, o_zr,
                                      exop.get(Pg - 2)] +
                                     ([ph1_last[0]] if Pg == 0 else []),
                                tag=f"sc.{Pg}")

            def make_ex(Pg):
                h, j, p, nch = tasks[Pg]
                pp = Pg % 2
                pp3 = Pg % 3
                masked = [c for c in (2 * p, 2 * p + 1) if c >= 16 * j]
                exop[Pg] = G.op("scalar",
                                lambda s, pp=pp, pp3=pp3: s.activation(
                                    pt_b[:, pp3 * 1024:pp3 * 1024 + 1024],
                                    stp[pp][:, :], AF.Exp,
                                    bias=consts[:, CB:CB + 1], scale=1.0),
                                deps=[scop[Pg], o_consts, pvop.get(Pg - 3)],
                                tag=f"ex.{Pg}")
                mkop[Pg] = None
                if masked:
                    def f_mk(v, masked=tuple(masked), pp3=pp3, j=j, p=p):
                        inst = None
                        for c in masked:
                            half = c - 2 * p
                            inst = v.tensor_mul(
                                pt_b[:, pp3 * 1024 + 512 * half:
                                     pp3 * 1024 + 512 * half + 512],
                                pt_b[:, pp3 * 1024 + 512 * half:
                                     pp3 * 1024 + 512 * half + 512],
                                mask_sb[:, 512 * (c - 16 * j):512 * (c - 16 * j) + 512])
                        return inst
                    mkop[Pg] = G.op("vector", f_mk, deps=[exop[Pg], o_g2],
                                    tag=f"mk.{Pg}")

            def make_pv(Pg):
                h, j, p, nch = tasks[Pg]
                pi = h % 2
                pp3 = Pg % 3
                jg = 4 * h + j
                jp = jg % 2

                def f_pv(t, p=p, pp3=pp3, pi=pi, nch=nch):
                    inst = None
                    for half in range(2):
                        c = 2 * p + half
                        for u in range(4):
                            inst = t.matmul(
                                (oA if u < 2 else oB)[:, OFF[u % 2]:OFF[u % 2] + 129],
                                pt_b[:, pp3 * 1024 + 512 * half + 128 * u:
                                     pp3 * 1024 + 512 * half + 128 * u + 128],
                                v_b[:, pi * NKC * VS + VS * c:
                                    pi * NKC * VS + VS * c + 129],
                                start=(c == 0 and u % 2 == 0),
                                stop=(c == nch - 1 and u % 2 == 1),
                                skip_group_check=True)
                    return inst
                pvdeps = [exop[Pg], mkop[Pg], lv[(h, 7)]]
                if p == 0:
                    pvdeps.append(sfin.get(jg - 1))
                pvop[Pg] = G.op("tensor", f_pv, deps=pvdeps, tag=f"pv.{Pg}")
                if p != nch // 2 - 1:
                    return
                vs1 = G.op("vector",
                           lambda v, jp=jp: v.tensor_copy(
                               ssum_sb[:, 4 * jp:4 * jp + 2], oA[:, 128:273:144]),
                           deps=[pvop[Pg], sfin.get(jg - 1)], tag=f"vs1.{jg}")
                vs2 = G.op("vector",
                           lambda v, jp=jp: v.tensor_copy(
                               ssum_sb[:, 4 * jp + 2:4 * jp + 4], oB[:, 128:273:144]),
                           deps=[pvop[Pg], sfin.get(jg - 1)], tag=f"vs2.{jg}")
                vrec = G.op("vector",
                            lambda v, jp=jp: v.reciprocal(
                                rs_sb[:, 4 * jp:4 * jp + 4],
                                ssum_sb[:, 4 * jp:4 * jp + 4]),
                            deps=[vs1, vs2], tag=f"vrec.{jg}")

                def f_fin(s, j=j, pi=pi, jp=jp):
                    inst = None
                    for u in range(4):
                        ob = oA if u < 2 else oB
                        ui = (0 if u < 2 else 2) + u % 2
                        inst = s.activation(
                            obf[:, pi * NQ + 512 * j + 128 * u:
                                pi * NQ + 512 * j + 128 * u + 128],
                            ob[:, OFF[u % 2]:OFF[u % 2] + 128],
                            AF.Copy, bias=0.0,
                            scale=rs_sb[:, 4 * jp + ui:4 * jp + ui + 1])
                    return inst
                sfin[jg] = G.op("scalar", f_fin,
                                deps=[vrec, trh.get((h - 2, 3))], tag=f"sfin.{jg}")
                if j != 3:
                    return
                for r in range(4):
                    def f_tr(t, r=r, pi=pi):
                        inst = None
                        for k in range(4):
                            u = 4 * r + k
                            inst = t.transpose(
                                tp2[r % 2][:, 128 * k:128 * k + 128],
                                obf[:, pi * NQ + 128 * u:pi * NQ + 128 * u + 128],
                                ident[:, :])
                        return inst
                    trh[(h, r)] = G.op("tensor", f_tr,
                                       deps=[sfin[4 * h + 3], evh.get((h, r - 2)), o_id],
                                       tag=f"trh.{h}.{r}")
                    evh[(h, r)] = G.op("vector",
                                       lambda v, r=r, pi=pi: v.tensor_copy(
                                           aot_st[:, pi * NQ + 512 * r:pi * NQ + 512 * r + 512],
                                           tp2[r % 2][:, :]),
                                       deps=[trh[(h, r)], sta.get(h - 2)],
                                       tag=f"evh.{h}.{r}")
                sta[h] = ld(aot_hbm[h], aot_st[:, pi * NQ:(pi + 1) * NQ], eng="scalar",
                            deps=[evh[(h, 3)]], dkey=f"sta{h % 2}", tag=f"sta.{h}")

            for Pg in range(NT):
                h = tasks[Pg][0]
                if Pg == 0 or tasks[Pg - 1][0] != h:
                    make_loads(h)
                make_sc(Pg)
                if Pg >= DEPTH:
                    make_pv(Pg - DEPTH)
                make_ex(Pg)
                if tasks[Pg][1] == 3 and tasks[Pg][2] == tasks[Pg][3] // 2 - 1:
                    sc_last[h] = scop[Pg]
                if Pg >= DEPTH:
                    hd = tasks[Pg - DEPTH][0]
                    td = tasks[Pg - DEPTH]
                    if td[1] == 3 and td[2] == td[3] // 2 - 1:
                        pv_last[hd] = pvop[Pg - DEPTH]
            for Pg in range(NT - DEPTH, NT):
                make_pv(Pg)
            pv_last[NH - 1] = pvop[NT - 1]
            ph2_last = [sta[NH - 1], sta[NH - 2], pvop[NT - 1], sfin[4 * NH - 1],
                        trh[(NH - 1, 3)], evh[(NH - 1, 3)]]
            ph2_guard = [exop[NT - 1], exop[NT - 2], sfin[4 * NH - 1],
                         evh[(NH - 1, 3)], evh[(NH - 1, 2)]]

        if upto < 3:
            _finish_stub(G, nc, es, out, ph2_last)
            G.emit(nc, es)
            return nc

        p34 = contextlib.ExitStack()
        y1t_sb = p34.enter_context(nc.sbuf_tensor("y1t_sb", [128, 16 * NQ], BF16))
        p3 = contextlib.ExitStack()
        with p3:
            P = p3.enter_context
            wo_sb = P(nc.sbuf_tensor("wo_sb", [128, 16 * D], BF16))
            aotm = P(nc.sbuf_tensor("aotm", [128, 2 * D], BF16))
            xq_sb = P(nc.sbuf_tensor("xq_sb", [128, 2 * D], F32))
            ypre = P(nc.sbuf_tensor("ypre", [128, D], F32))
            sqd = P(nc.sbuf_tensor("sqd", [128, D], F32))
            y1f = P(nc.sbuf_tensor("y1f", [128, 2 * D], F32))
            y1b = P(nc.sbuf_tensor("y1b", [128, D], BF16))
            y1ps = P(nc.psum_tensor("y1ps", [128, 2048], F32))
            tp3 = [P(nc.psum_tensor(f"tp3{i}", [128, 512], BF16)) for i in range(2)]

            o_wo = ld(_r3(wo_sb, 0, 16, D),
                      wo.rearrange("(n p) c -> p n c", p=128),
                      deps=ph2_last, dkey="g3", tag="wold")
            xqv = xq.rearrange("(n p) d -> n p d", p=128)
            y1v = y1_hbm.rearrange("(n p) d -> n p d", p=128)
            lnx = {}
            sty1 = {}
            evt3 = {}
            prev_mm = {}

            def make_tr3(m):
                pi = m % 2
                l9 = lnx[(m, "l9")]
                for r in range(4):
                    def f_tr(t, r=r):
                        inst = None
                        for k in range(4):
                            dd = 4 * r + k
                            inst = t.transpose(tp3[r % 2][:, 128 * k:128 * k + 128],
                                               y1b[:, 128 * dd:128 * dd + 128],
                                               ident[:, :])
                        return inst
                    tr = G.op("tensor", f_tr,
                              deps=[l9, evt3.get((m, r - 2)), evt3.get((m - 1, r + 2)),
                                    o_id] + (ph2_guard if m == 0 and r == 0 else []),
                              tag=f"tr3.{m}.{r}")

                    def f_ev(v, r=r, m=m):
                        dst = y1t_sb.rearrange("p (a c) -> p a c", c=2048)[
                            :, 4 * r:4 * r + 4, 128 * m:128 * m + 128]
                        return v.tensor_copy(
                            dst, tp3[r % 2].rearrange("p (a b) -> p a b", b=128))
                    evt3[(m, r)] = G.op("vector", f_ev, deps=[tr], tag=f"evt3.{m}.{r}")

            for m in range(16):
                pi = m % 2
                lda = []
                for q4 in range(4):
                    src = aot_hbm[4 * q4:4 * q4 + 4, :, 128 * m:128 * m + 128]\
                        .rearrange("a p c -> p a c")
                    dst = _r3(aotm, pi * D + 512 * q4, 4, 128)
                    lda.append(ld(dst, src, eng="sync",
                                  deps=[prev_mm.get(m - 2)] + (ph2_last if m < 2 else []),
                                  dkey=f"lda{m % 2}", tag=f"lda.{m}.{q4}"))
                ldx3 = ld(xq_sb[:, pi * D:(pi + 1) * D], xqv[m],
                          eng="scalar", deps=[lnx.get((m - 2, "evy"))] +
                          (ph2_last if m < 2 else []), dkey=f"ldx3{m % 2}",
                          tag=f"ldx3.{m}")

                def f_mm(t, pi=pi):
                    inst = None
                    for hh in range(16):
                        for dg in range(4):
                            inst = t.matmul(
                                y1ps[:, 512 * dg:512 * dg + 512],
                                aotm[:, pi * D + 128 * hh:pi * D + 128 * hh + 128],
                                wo_sb[:, hh * D + 512 * dg:hh * D + 512 * dg + 512],
                                start=(hh == 0), stop=(hh == 15))
                    return inst
                mm = G.op("tensor", f_mm,
                          deps=[lda[3], o_wo, lnx.get((m - 1, "evy"))] +
                               (ph2_guard if m == 0 else []),
                          tag=f"womm.{m}")
                prev_mm[m] = mm
                if m >= 1:
                    make_tr3(m - 1)

                def f_evy(v, pi=pi):
                    inst = None
                    for dg in range(4):
                        inst = v.tensor_add(
                            ypre[:, 512 * dg:512 * dg + 512],
                            y1ps[:, 512 * dg:512 * dg + 512],
                            xq_sb[:, pi * D + 512 * dg:pi * D + 512 * dg + 512])
                    return inst
                evy = G.op("vector", f_evy,
                           deps=[mm, ldx3, lnx.get((m - 1, "l9")),
                                 lnx.get((m - 1, "l3"))],
                           tag=f"evy.{m}")
                lnx[(m, "evy")] = evy
                _ln_chain(G, "p3", m, evy, ypre, sqd, stat, consts,
                          (CID, CEPS, CNH), y1f, pi, lnx, AXP=AX,
                          war=sty1.get(m - 2))
                l9 = G.op("vector",
                          lambda v, pi=pi: v.tensor_copy(
                              y1b[:, :], y1f[:, pi * D:(pi + 1) * D]),
                          deps=[lnx[(m, "l8")], evt3.get((m - 1, 3))],
                          tag=f"p3ln9.{m}")
                lnx[(m, "l9")] = l9
                sty1[m] = ld(y1v[m], y1f[:, pi * D:(pi + 1) * D], eng="scalar",
                             deps=[lnx[(m, "l8")]], dkey=f"sty1{m % 2}",
                             tag=f"sty1.{m}")
            make_tr3(15)
            ph3_last = [sty1[15], sty1[14]] + [evt3[(15, r)] for r in range(4)] + \
                [lnx[(15, "evy")], lnx[(15, "l9")]]
            ph3_guard = [lnx[(15, "evy")], lnx[(15, "l3")],
                         evt3[(15, 2)], evt3[(15, 3)]]

        if upto < 4:
            _finish_stub(G, nc, es, out, ph3_last)
            G.emit(nc, es)
            return nc

        hmuls, sth = {}, {}
        p4 = contextlib.ExitStack()
        with p4:
            P = p4.enter_context
            w1c = P(nc.sbuf_tensor("w1c", [128, 2 * D], BF16))
            w3c = P(nc.sbuf_tensor("w3c", [128, 2 * D], BF16))
            silu_sb = P(nc.sbuf_tensor("silu_sb", [128, 2 * 1024], F32))
            hidst = P(nc.sbuf_tensor("hidst", [128, 2 * D], BF16))
            gps = [P(nc.psum_tensor(f"gps{i}", [128, 1024], F32)) for i in range(2)]
            lps = [P(nc.psum_tensor(f"lps{i}", [128, 1024], F32)) for i in range(2)]

            for hh in range(64):
                pi = hh % 2
                wdep = [hmuls.get((hh - 2, 1))] + (ph3_last if hh < 2 else [])
                ldw1 = ld(_r3(w1c, pi * D, 16, 128),
                          w1til[hh].rearrange("n p c -> p n c"), deps=wdep, dkey=f"ldw1{hh % 2}", tag=f"ldw1.{hh}")
                ldw3 = ld(_r3(w3c, pi * D, 16, 128),
                          w3til[hh].rearrange("n p c -> p n c"), deps=wdep, dkey=f"ldw3{hh % 2}", tag=f"ldw3.{hh}")
                for half in range(2):
                    def f_g(t, pi=pi, half=half):
                        inst = None
                        for dd in range(16):
                            for tg in range(2):
                                inst = t.matmul(
                                    gps[half][:, 512 * tg:512 * tg + 512],
                                    w1c[:, pi * D + 128 * dd:pi * D + 128 * dd + 128],
                                    y1t_sb[:, 2048 * dd + 1024 * half + 512 * tg:
                                           2048 * dd + 1024 * half + 512 * tg + 512],
                                    start=(dd == 0), stop=(dd == 15))
                        return inst
                    depg = [ldw1, hmuls.get((hh - 1, half))]
                    if hh == 0 and half == 0:
                        depg += ph3_guard
                    mmg = G.op("tensor", f_g, deps=depg, tag=f"mmg.{hh}.{half}")

                    def f_l(t, pi=pi, half=half):
                        inst = None
                        for dd in range(16):
                            for tg in range(2):
                                inst = t.matmul(
                                    lps[half][:, 512 * tg:512 * tg + 512],
                                    w3c[:, pi * D + 128 * dd:pi * D + 128 * dd + 128],
                                    y1t_sb[:, 2048 * dd + 1024 * half + 512 * tg:
                                           2048 * dd + 1024 * half + 512 * tg + 512],
                                    start=(dd == 0), stop=(dd == 15))
                        return inst
                    mml = G.op("tensor", f_l,
                               deps=[ldw3, hmuls.get((hh - 1, half))],
                               tag=f"mml.{hh}.{half}")
                    sil = G.op("scalar",
                               lambda s, half=half: s.activation(
                                   silu_sb[:, 1024 * half:1024 * half + 1024],
                                   gps[half][:, :], AF.Silu),
                               deps=[mmg, hmuls.get((hh - 1, half))],
                               tag=f"sil.{hh}.{half}")
                    hmuls[(hh, half)] = G.op(
                        "vector",
                        lambda v, half=half, pi=pi: v.tensor_mul(
                            hidst[:, pi * D + 1024 * half:pi * D + 1024 * half + 1024],
                            lps[half][:, :],
                            silu_sb[:, 1024 * half:1024 * half + 1024]),
                        deps=[mml, sil, sth.get((hh - 2, 3))],
                        tag=f"hmul.{hh}.{half}")
                for r in range(4):
                    src = _r3(hidst, pi * D + 512 * r, 4, 128)
                    sth[(hh, r)] = ld(hid_hbm[4 * r:4 * r + 4, hh].rearrange(
                        "a p c -> p a c"), src, eng="sync",
                                      deps=[hmuls[(hh, r // 2)]],
                                      dkey=f"sth{hh % 2}", tag=f"sth.{hh}.{r}")
            up_last = [sth[(63, r)] for r in range(4)] + [sth[(62, 3)]]
            up_guard = [hmuls[(63, 0)], hmuls[(63, 1)], hmuls[(62, 0)], hmuls[(62, 1)]]

        p34.close()
        p4b = contextlib.ExitStack()
        with p4b:
            P = p4b.enter_context
            hidt = P(nc.sbuf_tensor("hidt", [128, 4 * 64 * 128], BF16))
            w2c = P(nc.sbuf_tensor("w2c", [128, 2 * 64 * 256], BF16))
            dstg = P(nc.sbuf_tensor("dstg", [128, 2 * 256], F32))
            y2ld = P(nc.sbuf_tensor("y2ld", [128, 2 * D], F32))
            y1ld = P(nc.sbuf_tensor("y1ld", [128, 2 * D], F32))
            ypre4 = P(nc.sbuf_tensor("ypre4", [128, D], F32))
            sqd4 = P(nc.sbuf_tensor("sqd4", [128, D], F32))
            outf = P(nc.sbuf_tensor("outf", [128, 2 * D], F32))
            dps = [P(nc.psum_tensor(f"dps{i}", [128, 256], F32)) for i in range(2)]

            ldh, ldw2, mmd, evd, std = {}, {}, {}, {}, {}
            outv = out.rearrange("(n p) d -> n p d", p=128)
            ln4 = {}
            sto = {}

            def emit_ln4_group(g):
                for tt in range(4 * g, 4 * g + 4):
                    pi = tt % 2
                    l2d = ld(y2ld[:, pi * D:(pi + 1) * D], y2f_hbm[tt],
                             eng="gpsimd", deps=[std[(g, 7, tt)],
                                                 ln4.get((tt - 2, "evy"))],
                             dkey=f"ly2{tt % 2}", tag=f"ly2.{tt}")
                    l1d = ld(y1ld[:, pi * D:(pi + 1) * D], y1v[tt],
                             eng="gpsimd", deps=[sty1[15], sty1[14],
                                                 ln4.get((tt - 2, "evy"))],
                             dkey=f"ly1{tt % 2}", tag=f"ly1.{tt}")

                    def f_add(v, pi=pi):
                        inst = None
                        for dg in range(2):
                            inst = v.tensor_add(
                                ypre4[:, 1024 * dg:1024 * dg + 1024],
                                y2ld[:, pi * D + 1024 * dg:pi * D + 1024 * dg + 1024],
                                y1ld[:, pi * D + 1024 * dg:pi * D + 1024 * dg + 1024])
                        return inst
                    evy4 = G.op("vector", f_add,
                                deps=[l2d, l1d, ln4.get((tt - 1, "l9")),
                                      ln4.get((tt - 1, "l3"))],
                                tag=f"evy4.{tt}")
                    ln4[(tt, "evy")] = evy4
                    _ln_chain(G, "p4", tt, evy4, ypre4, sqd4, stat, consts,
                              (CID, CEPS, CNH), outf, pi, ln4, AXP=AX,
                              war=sto.get(tt - 2))
                    ln4[(tt, "l9")] = ln4[(tt, "l8")]
                    sto[tt] = ld(outv[tt], outf[:, pi * D:(pi + 1) * D],
                                 eng="gpsimd", deps=[ln4[(tt, "l8")]],
                                 dkey=f"sto{tt % 2}", tag=f"sto.{tt}")

            for tg4 in range(4):
                if tg4 >= 2:
                    emit_ln4_group(tg4 - 2)
                for i in range(4):
                    tt = 4 * tg4 + i
                    ldh[tt] = ld(_r3(hidt, i * 64 * 128, 64, 128),
                                 hid_hbm[tt].rearrange("n p c -> p n c"), eng="sync",
                                 deps=[up_last[3], up_last[4],
                                       mmd.get((tg4 - 1, 7, 4 * (tg4 - 1) + i))],
                                 dkey=f"ldh{tg4 % 2}", tag=f"ldh.{tt}")
                for e8 in range(8):
                    pi = e8 % 2
                    ldw2[(tg4, e8)] = ld(
                        _r3(w2c, pi * 64 * 256, 64, 256),
                        w2pp[e8].rearrange("n p c -> p n c"), eng="scalar",
                        deps=[w2pp_done, w2pp_done2,
                              mmd.get((tg4, e8 - 2, 4 * tg4 + 3)),
                              mmd.get((tg4 - 1, e8 + 6, 4 * (tg4 - 1) + 3))] +
                             (up_guard + up_last if tg4 == 0 and e8 < 2 else []),
                        dkey=f"ldw2{e8 % 2}", tag=f"ldw2.{tg4}.{e8}")
                    for i in range(4):
                        tt = 4 * tg4 + i

                        def f_d(t, i=i, pi=pi, tt=tt):
                            inst = None
                            for hh in range(64):
                                inst = t.matmul(
                                    dps[tt % 2][:, :],
                                    hidt[:, i * 64 * 128 + 128 * hh:
                                         i * 64 * 128 + 128 * hh + 128],
                                    w2c[:, pi * 64 * 256 + 256 * hh:
                                        pi * 64 * 256 + 256 * hh + 256],
                                    start=(hh == 0), stop=(hh == 63))
                            return inst
                        depd = [ldh[tt], ldw2[(tg4, e8)],
                                evd.get((tg4, e8, tt - 2)),
                                evd.get((tg4, e8 - 1, tt + 2))]
                        if tg4 == 0 and e8 == 0 and i == 0:
                            depd += up_guard + [lnx[(15, "l3")]]
                        mmd[(tg4, e8, tt)] = G.op("tensor", f_d, deps=depd,
                                                  tag=f"mmd.{tg4}.{e8}.{tt}")
                        evd[(tg4, e8, tt)] = G.op(
                            "vector",
                            lambda v, tt=tt: v.tensor_copy(
                                dstg[:, 256 * (tt % 2):256 * (tt % 2) + 256],
                                dps[tt % 2][:, :]),
                            deps=[mmd[(tg4, e8, tt)], std.get((tg4, e8, tt - 2)),
                                  std.get((tg4, e8 - 1, tt + 2)),
                                  std.get((tg4 - 1, 7, 4 * (tg4 - 1) + 2 + tt % 2))
                                  if e8 == 0 else None],
                            tag=f"evd.{tg4}.{e8}.{tt}")
                        std[(tg4, e8, tt)] = ld(
                            y2f_hbm[tt, :, 256 * e8:256 * e8 + 256],
                            dstg[:, 256 * (tt % 2):256 * (tt % 2) + 256],
                            eng="sync", deps=[evd[(tg4, e8, tt)]],
                            dkey=f"std{tt % 2}", tag=f"std.{tg4}.{e8}.{tt}")

            emit_ln4_group(2)
            emit_ln4_group(3)

            G.op("sync", lambda sy: sy.nop(), deps=[sto[i] for i in range(16)],
                 tag="fence")

        G.emit(nc, es)
    return nc


def _ln_chain(G, pfx, m, evy, ypre, sqd, stat, consts, cc, yout, pi, store, AXP,
              war=None):
    CID, CEPS, CNH = cc
    l1 = G.op("vector",
              lambda v: v.reduce_sum(stat[:, 0:1], ypre[:, :], axis=AXP, negate=True),
              deps=[evy, store.get((m - 1, "l8"))], tag=f"{pfx}ln1.{m}")
    l2 = G.op("vector",
              lambda v: v.tensor_scalar_mul(stat[:, 1:2], stat[:, 0:1],
                                            consts[:, CID:CID + 1]),
              deps=[l1], tag=f"{pfx}ln2.{m}")
    l3 = G.op("scalar",
              lambda s: s.activation(sqd[:, :], ypre[:, :], AF.Square,
                                     bias=stat[:, 1:2], scale=1.0,
                                     accum_out=stat[:, 2:3]),
              deps=[l2, store.get((m - 1, "l3"))], tag=f"{pfx}ln3.{m}")
    store[(m, "l3")] = l3
    l4 = G.op("vector",
              lambda v: v.tensor_scalar(stat[:, 3:4], stat[:, 2:3],
                                        consts[:, CID:CID + 1],
                                        consts[:, CEPS:CEPS + 1],
                                        ALU.mult, ALU.add),
              deps=[l3], tag=f"{pfx}ln4.{m}")
    l5 = G.op("scalar",
              lambda s: s.activation(stat[:, 4:5], stat[:, 3:4], AF.Ln),
              deps=[l4], tag=f"{pfx}ln5.{m}")
    l6 = G.op("scalar",
              lambda s: s.activation(stat[:, 5:6], stat[:, 4:5], AF.Exp,
                                     scale=consts[:, CNH:CNH + 1]),
              deps=[l5], tag=f"{pfx}ln6.{m}")
    l7 = G.op("vector",
              lambda v: v.tensor_mul(stat[:, 6:7], stat[:, 1:2], stat[:, 5:6]),
              deps=[l6, l2], tag=f"{pfx}ln7.{m}")
    l8 = G.op("vector",
              lambda v: v.tensor_scalar(yout[:, pi * 2048:(pi + 1) * 2048], ypre[:, :],
                                        stat[:, 5:6], stat[:, 6:7],
                                        ALU.mult, ALU.add),
              deps=[l7, l3, war], tag=f"{pfx}ln8.{m}")
    store[(m, "l8")] = l8
    return l8


def _finish_stub(G, nc, es, out, deps):
    z = es.enter_context(nc.sbuf_tensor("zstub", [128, D], F32))
    zo = G.op("gpsimd", lambda g: g.memset(z[:, :], 0.0), tag="zstub")
    outv = out.rearrange("(n p) d -> n p d", p=128)
    last = []
    for m in range(16):
        last.append(G.op("sync", lambda sy, m=m: sy.dma_start(outv[m], z[:, :]),
                         deps=[zo] + list(deps), dma=True, tag=f"zo.{m}"))
    G.op("sync", lambda sy: sy.nop(), deps=[last[-1]], tag="fence")


def _sylvester(n):
    H = np.array([[1.0]])
    while H.shape[0] < n:
        H = np.block([[H, H], [H, -H]])
    return H.astype(np.float32)


def make_core_inputs(x, W, core):
    b, g = core // 4, core % 4
    ins = dict(W)
    ins["xb"] = np.ascontiguousarray(x[b])
    ins["xq"] = np.ascontiguousarray(x[b, g::4, :])
    inv = (1.0 / 10000.0 ** (np.arange(0, RD, 2, dtype=np.float64) / RD))
    posq = (g + 4 * np.arange(NQ, dtype=np.float64))
    ang_q = np.outer(inv, posq)
    cq, sq = np.cos(ang_q), np.sin(ang_q)
    ins["cosq"] = (np.concatenate([cq, cq], 0) * SCALE).astype(np.float32)
    ins["sinq"] = (np.concatenate([-sq, sq], 0) * SCALE).astype(np.float32)
    posk = np.arange(L, dtype=np.float64)
    ang_k = np.outer(inv, posk)
    ck, sk = np.cos(ang_k), np.sin(ang_k)
    ins["cosk"] = np.concatenate([ck, ck], 0).astype(np.float32)
    ins["sink"] = np.concatenate([-sk, sk], 0).astype(np.float32)
    p_ = np.arange(128)[:, None]
    i_ = np.arange(512)[None, :]
    tiles = [(128 * cc + p_ <= g + 4 * i_).astype(np.float32) for cc in range(16)]
    ins["maskp"] = np.ascontiguousarray(np.concatenate(tiles, axis=1))
    return ins


def make_shared_weights(inputs):
    W = {}
    for k in ("wq_down", "wq_up", "wq_rope", "wkv_down", "wk_up", "wv_up",
              "wk_rope", "wo"):
        W[k] = np.ascontiguousarray(np.asarray(inputs["L0_attn_" + k], np.float32))
    W1 = np.asarray(inputs["L0_ffn_W1"], np.float32)
    W3 = np.asarray(inputs["L0_ffn_W3"], np.float32)
    W2 = np.asarray(inputs["L0_ffn_W2"], np.float32)
    W["w1til"] = np.ascontiguousarray(W1.reshape(64, 128, 16, 128).transpose(0, 2, 3, 1))
    W["w3til"] = np.ascontiguousarray(W3.reshape(64, 128, 16, 128).transpose(0, 2, 3, 1))
    W["w2r"] = np.ascontiguousarray(W2.T).reshape(128, 64 * D)
    H64 = _sylvester(64)
    W["h64i2"] = np.ascontiguousarray(np.kron(H64, np.eye(2, dtype=np.float32)))
    W["h128"] = _sylvester(128)
    W["idin"] = np.eye(128, dtype=np.float32)
    return W


def kernel(**inputs):
    x = np.asarray(inputs["L0_attn_x"], np.float32)
    W = make_shared_weights(inputs)
    nc = build_program(upto=4)
    in_maps = [make_core_inputs(x, W, c) for c in range(8)]
    res = run_bass_kernel_spmd(nc, in_maps, core_ids=list(range(8)))
    y = np.empty((B, L, D), np.float32)
    for c in range(8):
        y[c // 4, c % 4::4, :] = res.results[c]["out"]
    return y


# version 08
# speedup vs baseline: 1.6117x; 1.0805x over previous
import contextlib
import math

import numpy as np

import concourse.bass as bass
import concourse.mybir as mybir
from concourse.bass_utils import run_bass_kernel_spmd

F32 = mybir.dt.float32
BF16 = mybir.dt.bfloat16
AF = mybir.ActivationFunctionType
AX = mybir.AxisListType.X
ALU = mybir.AluOpType

B, L, D = 2, 8192, 2048
NH, HD, RD, QR, KVR = 16, 128, 64, 768, 512
HF = 8192
NQ = 2048
SCALE = 1.0 / math.sqrt(HD + RD)
BSHIFT = 32.0
EPS = 1e-5
VS = 130
NKC = L // 128

ENGINES = ("tensor", "vector", "scalar", "sync", "gpsimd")


class Op:
    __slots__ = ("eng", "fn", "deps", "dma", "dkey", "needed", "ctr", "val", "tag")

    def __init__(self, eng, fn, deps, dma, dkey, tag):
        self.eng, self.fn, self.dma, self.tag = eng, fn, dma, tag
        self.dkey = dkey
        self.deps = [d for d in deps if d is not None]
        self.needed = False
        self.ctr = None
        self.val = None


class Graph:
    def __init__(self):
        self.streams = {e: [] for e in ENGINES}
        self.all = []

    def op(self, eng, fn, deps=(), dma=False, dkey=None, tag=""):
        if dma and dkey is None:
            dkey = "bulk"
        o = Op(eng, fn, deps, dma, dkey, tag)
        self.streams[eng].append(o)
        self.all.append(o)
        return o

    def check(self):
        done = set()
        idx = {e: 0 for e in ENGINES}
        while len(done) < len(self.all):
            progress = False
            for e in ENGINES:
                while idx[e] < len(self.streams[e]):
                    o = self.streams[e][idx[e]]
                    if all(d in done for d in o.deps):
                        done.add(o)
                        idx[e] += 1
                        progress = True
                    else:
                        break
            if not progress:
                stuck = []
                for e in ENGINES:
                    if idx[e] < len(self.streams[e]):
                        o = self.streams[e][idx[e]]
                        missing = [d.tag for d in o.deps if d not in done]
                        stuck.append(f"{e}[{idx[e]}] {o.tag} waits {missing[:3]}")
                raise RuntimeError("op graph deadlock: " + " | ".join(stuck))

    def emit(self, nc, es):
        cc, dc = {}, {}
        for e in ENGINES:
            cc[e] = [es.enter_context(nc.semaphore(f"c_{e}")), 0]
        for o in self.all:
            if o.dma:
                o.needed = True
            for d in o.deps:
                d.needed = True
        for e in ENGINES:
            for o in self.streams[e]:
                if o.needed:
                    if o.dma:
                        k = (e, o.dkey)
                        if k not in dc:
                            dc[k] = [es.enter_context(nc.semaphore(f"d_{e}_{o.dkey}")), 0]
                        c = dc[k]
                        c[1] += 16
                    else:
                        c = cc[e]
                        c[1] += 1
                    o.ctr = c
                    o.val = c[1]
        self.check()
        blk = es.enter_context(nc.Block("main"))
        graph = self

        def make_cb(e):
            def cb(eng):
                covered = {}
                for o in graph.streams[e]:
                    want = {}
                    for d in o.deps:
                        assert d.val is not None, f"dep {d.tag} of {o.tag} not inc'd"
                        key = id(d.ctr)
                        if want.get(key, (None, -1))[1] < d.val:
                            want[key] = (d.ctr, d.val)
                    for key, (ctr, val) in want.items():
                        if covered.get(key, -1) < val:
                            eng.wait_ge(ctr[0], val)
                            covered[key] = val
                    inst = o.fn(eng)
                    if o.needed:
                        assert inst is not None, f"op {o.tag} returned no instruction"
                        inst.then_inc(o.ctr[0], 16 if o.dma else 1)
            return cb

        blk.tensor(make_cb("tensor"))
        blk.vector(make_cb("vector"))
        blk.scalar(make_cb("scalar"))
        blk.sync(make_cb("sync"))
        blk.gpsimd(make_cb("gpsimd"))


def _r3(sb, lo, n, c):
    return sb[:, lo:lo + n * c].rearrange("p (a c) -> p a c", c=c)


def build_program(upto=4, dump=()):
    nc = bass.Bass("TRN2", target_bir_lowering=False)
    G = Graph()

    def dram(name, shape, dt, kind="Internal"):
        if name in dump and kind == "Internal":
            kind = "ExternalOutput"
        return nc.dram_tensor(name, shape, dt, kind=kind).ap()

    xb = dram("xb", [L, D], F32, "ExternalInput")
    xq = dram("xq", [NQ, D], F32, "ExternalInput")
    wq_down = dram("wq_down", [D, QR], F32, "ExternalInput")
    wq_up = dram("wq_up", [QR, NH * HD], F32, "ExternalInput")
    wq_rope = dram("wq_rope", [QR, NH * RD], F32, "ExternalInput")
    wkv_down = dram("wkv_down", [D, KVR], F32, "ExternalInput")
    wk_up = dram("wk_up", [KVR, NH * HD], F32, "ExternalInput")
    wv_up = dram("wv_up", [KVR, NH * HD], F32, "ExternalInput")
    wk_rope = dram("wk_rope", [D, RD], F32, "ExternalInput")
    wo = dram("wo", [NH * HD, D], F32, "ExternalInput")
    w1til = dram("w1til", [64, 16, 128, 128], F32, "ExternalInput")
    w3til = dram("w3til", [64, 16, 128, 128], F32, "ExternalInput")
    w2r = dram("w2r", [128, 64 * D], F32, "ExternalInput")
    h64i2 = dram("h64i2", [128, 128], F32, "ExternalInput")
    h128 = dram("h128", [128, 128], F32, "ExternalInput")
    idin = dram("idin", [128, 128], F32, "ExternalInput")
    cosq = dram("cosq", [64, NQ], F32, "ExternalInput")
    sinq = dram("sinq", [64, NQ], F32, "ExternalInput")
    cosk = dram("cosk", [64, L], F32, "ExternalInput")
    sink = dram("sink", [64, L], F32, "ExternalInput")
    maskp = dram("maskp", [128, 16 * 512], F32, "ExternalInput")

    out = dram("out", [NQ, D], F32, "ExternalOutput")

    w2i = dram("w2i", [HF, D], BF16)
    w2pp = dram("w2pp", [8, 64, 128, 256], BF16)
    qt_hbm = dram("qt_hbm", [NH, 192, NQ], BF16)
    kt_hbm = dram("kt_hbm", [NH, 128, L], BF16)
    kr_hbm = dram("kr_hbm", [64, L], BF16)
    v_hbm = dram("v_hbm", [NKC, 128, NH * VS], BF16)
    aot_hbm = dram("aot_hbm", [NH, 128, NQ], BF16)
    y1_hbm = dram("y1_hbm", [NQ, D], F32)
    hid_hbm = dram("hid_hbm", [16, 64, 128, 128], BF16)
    y2f_hbm = dram("y2f_hbm", [16, 128, D], F32)

    es = contextlib.ExitStack()
    with es:
        E = es.enter_context
        ident = E(nc.sbuf_tensor("ident", [128, 128], BF16))
        consts = E(nc.sbuf_tensor("consts", [128, 8], F32))
        CB, CID, CEPS, CSC, CNH, CHS = 0, 1, 2, 3, 4, 5
        stat = E(nc.sbuf_tensor("stat", [128, 8], F32))

        def ld(dst_ap, src_ap, eng="gpsimd", deps=(), dkey="bulk", tag="ld"):
            return G.op(eng, lambda g, d=dst_ap, s=src_ap: g.dma_start(d, s),
                        deps=deps, dma=True, dkey=dkey, tag=tag)

        p1 = contextlib.ExitStack()
        with p1:
            P = p1.enter_context
            qlat = P(nc.sbuf_tensor("qlat", [128, 6 * NQ], BF16))
            kvlat = P(nc.sbuf_tensor("kvlat", [128, 4 * L], BF16))

            def memset_consts(gp):
                gp.memset(consts[:, CB:CB + 1], -BSHIFT)
                gp.memset(consts[:, CID:CID + 1], 1.0 / 2048.0)
                gp.memset(consts[:, CEPS:CEPS + 1], EPS)
                gp.memset(consts[:, CSC:CSC + 1], SCALE)
                gp.memset(consts[:, CNH:CNH + 1], -0.5)
                return gp.memset(consts[:, CHS:CHS + 1], 1.0 / math.sqrt(8192.0))
            o_consts = G.op("gpsimd", memset_consts, tag="consts")

            s1ab = contextlib.ExitStack()
            with s1ab:
                Q = s1ab.enter_context
                w2in = Q(nc.sbuf_tensor("w2in", [128, 2 * 1024], BF16))
                w2st = Q(nc.sbuf_tensor("w2st", [128, 2 * 1024], BF16))
                h64_sb = Q(nc.sbuf_tensor("h64_sb", [128, 128], BF16))
                h128_sb = Q(nc.sbuf_tensor("h128_sb", [128, 128], BF16))
                xin = Q(nc.sbuf_tensor("xin", [128, 2 * D], BF16))
                xt = Q(nc.sbuf_tensor("xt", [128, 2 * 16 * 512], BF16))
                wqd_sb = Q(nc.sbuf_tensor("wqd_sb", [128, 16 * QR], BF16))
                wkvd_sb = Q(nc.sbuf_tensor("wkvd_sb", [128, 16 * KVR], BF16))
                wkr_sb = Q(nc.sbuf_tensor("wkr_sb", [128, 16 * RD], BF16))
                ckb = Q(nc.sbuf_tensor("ckb", [64, 2 * 512], F32))
                skb = Q(nc.sbuf_tensor("skb", [64, 2 * 512], F32))
                rtmpK = Q(nc.sbuf_tensor("rtmpK", [64, 3 * 512], F32))
                krst = Q(nc.sbuf_tensor("krst", [64, 2 * 512], BF16))

                o_h64 = ld(h64_sb[:, :], h64i2, dkey="setup", tag="h64")
                o_h128 = ld(h128_sb[:, :], h128, dkey="setup", tag="h128")
                o_id = ld(ident[:, :], idin, dkey="setup", tag="id")
                o_wqd = ld(_r3(wqd_sb, 0, 16, QR),
                           wq_down.rearrange("(n p) c -> p n c", p=128), dkey="setup", tag="wqd")
                o_wkvd = ld(_r3(wkvd_sb, 0, 16, KVR),
                            wkv_down.rearrange("(n p) c -> p n c", p=128), dkey="setup", tag="wkvd")
                o_wkr = ld(_r3(wkr_sb, 0, 16, RD),
                           wk_rope.rearrange("(n p) c -> p n c", p=128), dkey="setup", tag="wkr")
                o_setup = o_wkr

                ps0 = contextlib.ExitStack()
                with ps0:
                    w2ps = [ps0.enter_context(nc.psum_tensor(f"w2ps{i}", [128, 1024], F32))
                            for i in range(2)]
                    mm1, ev1, st1 = {}, {}, {}
                    st2 = {}
                    for it in range(128):
                        b1, hf = it // 2, it % 2
                        pi = it % 2
                        ld1 = ld(w2in[:, pi * 1024:(pi + 1) * 1024],
                                 w2r[:, b1 * D + 1024 * hf:b1 * D + 1024 * hf + 1024],
                                 deps=[mm1.get(it - 2)], dkey=f"w2ld1{it % 2}", tag=f"w2ld1.{it}")

                        def f_mm1(t, pi=pi):
                            inst = None
                            for dcn in range(2):
                                inst = t.matmul(
                                    w2ps[pi][:, 512 * dcn:512 * dcn + 512], h64_sb[:, :],
                                    w2in[:, pi * 1024 + 512 * dcn:pi * 1024 + 512 * dcn + 512],
                                    start=True, stop=True)
                            return inst
                        mm1[it] = G.op("tensor", f_mm1,
                                       deps=[ld1, o_setup, ev1.get(it - 2)], tag=f"w2mm1.{it}")
                        ev1[it] = G.op("vector",
                                       lambda v, pi=pi: v.tensor_scalar_mul(
                                           w2st[:, pi * 1024:(pi + 1) * 1024],
                                           w2ps[pi][:, :], consts[:, CHS:CHS + 1]),
                                       deps=[mm1[it], st1.get(it - 2), o_consts],
                                       tag=f"w2ev1.{it}")
                        st1[it] = ld(
                            w2i.rearrange("(p b) d -> p b d", b=64)[:, b1,
                                                                   1024 * hf:1024 * hf + 1024],
                            w2st[:, pi * 1024:(pi + 1) * 1024], eng="sync",
                            deps=[ev1[it]], dkey=f"w2st1{it % 2}", tag=f"w2st1.{it}")
                    mm2, ev2 = {}, {}
                    for it in range(128):
                        a, hf = it // 2, it % 2
                        pi = it % 2
                        ld2 = ld(w2in[:, pi * 1024:(pi + 1) * 1024],
                                 w2i[a * 128:(a + 1) * 128, 1024 * hf:1024 * hf + 1024],
                                 eng="sync", deps=[st1[127], st1[126], mm2.get(it - 2)],
                                 dkey=f"w2ld2{it % 2}", tag=f"w2ld2.{it}")

                        def f_mm2(t, pi=pi):
                            inst = None
                            for dcn in range(2):
                                inst = t.matmul(
                                    w2ps[pi][:, 512 * dcn:512 * dcn + 512], h128_sb[:, :],
                                    w2in[:, pi * 1024 + 512 * dcn:pi * 1024 + 512 * dcn + 512],
                                    start=True, stop=True)
                            return inst
                        mm2[it] = G.op("tensor", f_mm2,
                                       deps=[ld2, o_setup, ev2.get(it - 2)], tag=f"w2mm2.{it}")
                        ev2[it] = G.op("vector",
                                       lambda v, pi=pi: v.tensor_copy(
                                           w2st[:, pi * 1024:(pi + 1) * 1024], w2ps[pi][:, :]),
                                       deps=[mm2[it], st2.get((it - 2, 3))], tag=f"w2ev2.{it}")
                        for e4 in range(4):
                            e8 = 4 * hf + e4
                            st2[(it, e4)] = ld(
                                w2pp[e8, a, :, :],
                                w2st[:, pi * 1024 + 256 * e4:pi * 1024 + 256 * e4 + 256],
                                eng="scalar", deps=[ev2[it]], dkey=f"w2st2{it % 2}", tag=f"w2st2.{it}.{e4}")
                ph0_guard = ev2[127]
                w2pp_done = st2[(127, 3)]
                w2pp_done2 = st2[(126, 3)]

                ps1 = contextlib.ExitStack()
                with ps1:
                    tpsX = [ps1.enter_context(nc.psum_tensor(f"tps{i}", [128, 512], BF16))
                            for i in range(2)]
                    prjX = [ps1.enter_context(nc.psum_tensor(f"prj{i}", [128, 512], F32))
                            for i in range(2)]
                    prjK = ps1.enter_context(nc.psum_tensor("prjK", [64, 512], F32))

                    trops, evops, ldx = {}, {}, {}
                    ldck, ldsk = {}, {}
                    prj_users = {}
                    nprj = [0]
                    evql, evkv, ropeK, stkr = {}, {}, {}, {}

                    def qlat_consumer(ci, evdep):
                        n = ci
                        for mt in range(6):
                            pp = nprj[0] % 2
                            nprj[0] += 1

                            def f(t, mt=mt, pp=pp):
                                inst = None
                                for dd in range(16):
                                    inst = t.matmul(
                                        prjX[pp][:, :],
                                        wqd_sb[:, dd * QR + 128 * mt:dd * QR + 128 * mt + 128],
                                        xt[:, (ci % 2) * 8192 + 512 * dd:
                                           (ci % 2) * 8192 + 512 * dd + 512],
                                        start=(dd == 0), stop=(dd == 15))
                                return inst
                            mm = G.op("tensor", f,
                                      deps=[evdep, o_setup, prj_users.get(pp), ph0_guard],
                                      tag=f"qlmm.{n}.{mt}")
                            evql[(n, mt)] = G.op(
                                "vector",
                                lambda v, pp=pp, mt=mt, n=n: v.tensor_copy(
                                    qlat[:, 2048 * mt + 512 * n:2048 * mt + 512 * n + 512],
                                    prjX[pp][:, :]),
                                deps=[mm], tag=f"qlev.{n}.{mt}")
                            prj_users[pp] = evql[(n, mt)]

                    def kv_consumer(ci, evdep):
                        nb = ci - 4
                        for mt in range(4):
                            pp = nprj[0] % 2
                            nprj[0] += 1

                            def f(t, mt=mt, pp=pp, ci=ci):
                                inst = None
                                for dd in range(16):
                                    inst = t.matmul(
                                        prjX[pp][:, :],
                                        wkvd_sb[:, dd * KVR + 128 * mt:dd * KVR + 128 * mt + 128],
                                        xt[:, (ci % 2) * 8192 + 512 * dd:
                                           (ci % 2) * 8192 + 512 * dd + 512],
                                        start=(dd == 0), stop=(dd == 15))
                                return inst
                            mm = G.op("tensor", f,
                                      deps=[evdep, o_setup, prj_users.get(pp), ph0_guard],
                                      tag=f"kvmm.{nb}.{mt}")
                            evkv[(nb, mt)] = G.op(
                                "vector",
                                lambda v, pp=pp, mt=mt, nb=nb: v.tensor_copy(
                                    kvlat[:, 8192 * mt + 512 * nb:8192 * mt + 512 * nb + 512],
                                    prjX[pp][:, :]),
                                deps=[mm], tag=f"kvev.{nb}.{mt}")
                            prj_users[pp] = evkv[(nb, mt)]

                        ldck[nb] = ld(ckb[:, (nb % 2) * 512:(nb % 2) * 512 + 512],
                                      cosk[:, 512 * nb:512 * nb + 512],
                                      eng="sync", deps=[ropeK.get(("B", nb - 2))],
                                      dkey=f"ldck{nb % 2}", tag=f"ldck.{nb}")
                        ldsk[nb] = ld(skb[:, (nb % 2) * 512:(nb % 2) * 512 + 512],
                                      sink[:, 512 * nb:512 * nb + 512],
                                      eng="sync", deps=[ropeK.get(("D", nb - 2))],
                                      dkey=f"ldsk{nb % 2}", tag=f"ldsk.{nb}")

                        def f_kr(t, ci=ci):
                            inst = None
                            for dd in range(16):
                                inst = t.matmul(
                                    prjK[:, :],
                                    wkr_sb[:, dd * RD:dd * RD + RD],
                                    xt[:, (ci % 2) * 8192 + 512 * dd:
                                       (ci % 2) * 8192 + 512 * dd + 512],
                                    start=(dd == 0), stop=(dd == 15))
                            return inst
                        mmkr = G.op("tensor", f_kr,
                                    deps=[evdep, o_setup, ropeK.get(("D", nb - 1)), ph0_guard],
                                    tag=f"krmm.{nb}")
                        ropeK[("A", nb)] = G.op(
                            "vector",
                            lambda v: v.tensor_copy(rtmpK[:, 0:512], prjK[:, :]),
                            deps=[mmkr, ropeK.get(("D", nb - 1))], tag=f"krA.{nb}")

                        def f_rot(v):
                            v.tensor_copy(rtmpK[0:32, 512:1024], prjK[32:64, :])
                            return v.tensor_copy(rtmpK[32:64, 512:1024], prjK[0:32, :])
                        ropeK[("R", nb)] = G.op("vector", f_rot,
                                                deps=[mmkr, ropeK.get(("D", nb - 1))],
                                                tag=f"krR.{nb}")
                        ropeK[("B", nb)] = G.op(
                            "vector",
                            lambda v, nb=nb: v.tensor_mul(
                                rtmpK[:, 1024:1536], rtmpK[:, 0:512],
                                ckb[:, (nb % 2) * 512:(nb % 2) * 512 + 512]),
                            deps=[ropeK[("A", nb)], ldck[nb]], tag=f"krB.{nb}")

                        ropeK[("C", nb)] = G.op(
                            "vector",
                            lambda v, nb=nb: v.tensor_mul(
                                rtmpK[:, 512:1024], rtmpK[:, 512:1024],
                                skb[:, (nb % 2) * 512:(nb % 2) * 512 + 512]),
                            deps=[ropeK[("B", nb)], ropeK[("R", nb)], ldsk[nb]],
                            tag=f"krC.{nb}")
                        ropeK[("D", nb)] = G.op(
                            "vector",
                            lambda v, nb=nb: v.tensor_add(
                                krst[:, (nb % 2) * 512:(nb % 2) * 512 + 512],
                                rtmpK[:, 1024:1536], rtmpK[:, 512:1024]),
                            deps=[ropeK[("C", nb)], stkr.get(nb - 2)],
                            tag=f"krD.{nb}")
                        stkr[nb] = ld(kr_hbm[:, 512 * nb:512 * nb + 512],
                                      krst[:, (nb % 2) * 512:(nb % 2) * 512 + 512],
                                      eng="sync", deps=[ropeK[("D", nb)]],
                                      dkey=f"stkr{nb % 2}", tag=f"stkr.{nb}")

                    def x_chunk(ci, srcrows, consumers):
                        for mtl in range(4):
                            gt = 4 * ci + mtl
                            pig = gt % 2
                            ldx[gt] = ld(
                                xin[:, pig * D:(pig + 1) * D],
                                srcrows[128 * mtl:128 * mtl + 128, :],
                                deps=[trops.get(4 * (gt - 2) + 3)], dkey=f"ldx{gt % 2}", tag=f"ldx.{gt}")
                            for r in range(4):
                                R = 4 * gt + r
                                dd0 = 4 * r

                                def f_tr(t, pig=pig, dd0=dd0, R=R):
                                    inst = None
                                    for k in range(4):
                                        inst = t.transpose(
                                            tpsX[R % 2][:, 128 * k:128 * k + 128],
                                            xin[:, pig * D + 128 * (dd0 + k):
                                                pig * D + 128 * (dd0 + k) + 128],
                                            ident[:, :])
                                    return inst
                                trops[R] = G.op("tensor", f_tr,
                                                deps=[ldx[gt], o_setup, evops.get(R - 2),
                                                      ph0_guard],
                                                tag=f"tr.{R}")

                                def f_ev(v, ci=ci, mtl=mtl, dd0=dd0, R=R):
                                    dst = xt[:, (ci % 2) * 8192:(ci % 2) * 8192 + 8192]\
                                        .rearrange("p (a c) -> p a c", c=512)[
                                        :, dd0:dd0 + 4, 128 * mtl:128 * mtl + 128]
                                    return v.tensor_copy(
                                        dst, tpsX[R % 2].rearrange("p (a b) -> p a b", b=128))
                                evops[R] = G.op("vector", f_ev, deps=[trops[R]],
                                                tag=f"ev.{R}")
                        for c in consumers:
                            c(ci, evops[4 * (4 * ci + 3) + 3])

                    for n in range(4):
                        x_chunk(n, xq[512 * n:512 * (n + 1), :], [qlat_consumer])
                    for n in range(16):
                        x_chunk(4 + n, xb[512 * n:512 * (n + 1), :], [kv_consumer])

                    last_ql = evql[(3, 5)]
                    last_kv = evkv[(15, 3)]
                    last_kr = ropeK[("D", 15)]
                    ps1_guard = [last_ql, last_kv, last_kr, prj_users[0], prj_users[1],
                                 evops[4 * (4 * 19 + 3) + 3]]
                g1ab_last = [last_ql, last_kv, stkr[15], stkr[14], st1[127], st1[126],
                             w2pp_done, w2pp_done2,
                             trops[4 * (4 * 19 + 3) + 3], ps1_guard[-1]]

            s1cd = contextlib.ExitStack()
            with s1cd:
                Q = s1cd.enter_context
                wquh = Q(nc.sbuf_tensor("wquh", [128, 2 * 6 * 128], BF16))
                wqrh = Q(nc.sbuf_tensor("wqrh", [128, 2 * 6 * 64], BF16))
                wkuh = Q(nc.sbuf_tensor("wkuh", [128, 2 * 4 * 128], BF16))
                wvu_sb = Q(nc.sbuf_tensor("wvu_sb", [128, 4 * 2048], BF16))
                cq_sb = Q(nc.sbuf_tensor("cq_sb", [64, NQ], F32))
                sq_sb = Q(nc.sbuf_tensor("sq_sb", [64, NQ], F32))
                qtc_st = Q(nc.sbuf_tensor("qtc_st", [128, 2 * NQ], BF16))
                qtr_st = Q(nc.sbuf_tensor("qtr_st", [64, 2 * NQ], BF16))
                ktc_st = Q(nc.sbuf_tensor("ktc_st", [128, 2 * 2048], BF16))
                vstage = Q(nc.sbuf_tensor("vstage", [128, 2 * NH * VS], BF16))
                rtmp = Q(nc.sbuf_tensor("rtmp", [64, 3 * 2048], F32))

                o_cq = ld(cq_sb[:, :], cosq, deps=g1ab_last, dkey="g1cd", tag="cq")
                o_sq = ld(sq_sb[:, :], sinq, deps=g1ab_last, dkey="g1cd", tag="sq")
                o_wvu = ld(_r3(wvu_sb, 0, 4, 2048),
                           wv_up.rearrange("(n p) c -> p n c", p=128),
                           deps=g1ab_last, dkey="g1cd", tag="wvu")
                o_g1cd = o_wvu

                ps1c = contextlib.ExitStack()
                with ps1c:
                    qcps = [ps1c.enter_context(nc.psum_tensor(f"qcps{i}", [128, 512], F32))
                            for i in range(2)]
                    qrps = ps1c.enter_context(nc.psum_tensor("qrps", [64, 2048], F32))
                    stq = {}
                    ldqu, ldqr = {}, {}
                    qc_user = {}
                    rope_user = [None]
                    qc_lastmm, qr_lastmm = {}, {}
                    for h in range(NH):
                        pi = h % 2
                        ldqu[h] = ld(_r3(wquh, pi * 6 * 128, 6, 128),
                                     wq_up[:, 128 * h:128 * h + 128].rearrange(
                                         "(n p) c -> p n c", p=128),
                                     deps=[qc_lastmm.get(h - 2)] +
                                          (g1ab_last if h < 2 else []),
                                     dkey=f"ldqu{h % 2}", tag=f"ldqu.{h}")
                        ldqr[h] = ld(_r3(wqrh, pi * 6 * 64, 6, 64),
                                     wq_rope[:, 64 * h:64 * h + 64].rearrange(
                                         "(n p) c -> p n c", p=128),
                                     deps=[qr_lastmm.get(h - 2)] +
                                          (g1ab_last if h < 2 else []),
                                     dkey=f"ldqr{h % 2}", tag=f"ldqr.{h}")
                        evs = []
                        for cg in range(4):
                            pp = cg % 2

                            def f(t, h=h, cg=cg, pp=pp, pi=pi):
                                inst = None
                                for qq in range(6):
                                    inst = t.matmul(
                                        qcps[pp][:, :],
                                        wquh[:, pi * 768 + 128 * qq:pi * 768 + 128 * qq + 128],
                                        qlat[:, 2048 * qq + 512 * cg:2048 * qq + 512 * cg + 512],
                                        start=(qq == 0), stop=(qq == 5))
                                return inst
                            mm = G.op("tensor", f,
                                      deps=[last_ql, ldqu[h], qc_user.get(pp)] +
                                           (ps1_guard if h == 0 and cg == 0 else []),
                                      tag=f"qcmm.{h}.{cg}")
                            ev = G.op("scalar",
                                      lambda sc, cg=cg, pp=pp, pi=pi: sc.activation(
                                          qtc_st[:, pi * NQ + 512 * cg:pi * NQ + 512 * cg + 512],
                                          qcps[pp][:, :], AF.Copy, bias=0.0,
                                          scale=consts[:, CSC:CSC + 1]),
                                      deps=[mm, o_consts, stq.get((h - 2, 0))],
                                      tag=f"qcev.{h}.{cg}")
                            qc_user[pp] = ev
                            evs.append(ev)
                        qc_lastmm[h] = mm

                        def f_qr(t, h=h, pi=pi):
                            inst = None
                            for cg in range(4):
                                for qq in range(6):
                                    inst = t.matmul(
                                        qrps[:, 512 * cg:512 * cg + 512],
                                        wqrh[:, pi * 384 + 64 * qq:pi * 384 + 64 * qq + 64],
                                        qlat[:, 2048 * qq + 512 * cg:2048 * qq + 512 * cg + 512],
                                        start=(qq == 0), stop=(qq == 5))
                            return inst
                        mmqr = G.op("tensor", f_qr,
                                    deps=[last_ql, ldqr[h], rope_user[0]] +
                                         (ps1_guard if h == 0 else []),
                                    tag=f"qrmm.{h}")
                        oA_ = G.op("vector",
                                   lambda v: v.tensor_copy(rtmp[:, 0:2048], qrps[:, :]),
                                   deps=[mmqr, rope_user[0]] +
                                        (g1ab_last if h == 0 else []),
                                   tag=f"qrA.{h}")

                        def f_rot(v):
                            v.tensor_copy(rtmp[0:32, 2048:4096], qrps[32:64, :])
                            return v.tensor_copy(rtmp[32:64, 2048:4096], qrps[0:32, :])
                        oR_ = G.op("vector", f_rot, deps=[mmqr, rope_user[0]],
                                   tag=f"qrR.{h}")
                        oB_ = G.op("vector",
                                   lambda v: v.tensor_mul(rtmp[:, 4096:6144],
                                                          rtmp[:, 0:2048], cq_sb[:, :]),
                                   deps=[oA_, o_g1cd], tag=f"qrB.{h}")

                        oC_ = G.op("vector",
                                   lambda v: v.tensor_mul(rtmp[:, 2048:4096],
                                                          rtmp[:, 2048:4096], sq_sb[:, :]),
                                   deps=[oB_, oR_, o_g1cd], tag=f"qrC.{h}")
                        oD_ = G.op("vector",
                                   lambda v, pi=pi: v.tensor_add(
                                       qtr_st[:, pi * NQ:pi * NQ + NQ],
                                       rtmp[:, 4096:6144], rtmp[:, 2048:4096]),
                                   deps=[oC_, stq.get((h - 2, 1))], tag=f"qrD.{h}")
                        rope_user[0] = oD_
                        qr_lastmm[h] = mmqr
                        stq[(h, 0)] = ld(qt_hbm[h, 0:128, :],
                                         qtc_st[:, pi * NQ:(pi + 1) * NQ],
                                         eng="sync", deps=evs, dkey=f"stqc{h % 2}", tag=f"stqc.{h}")
                        stq[(h, 1)] = ld(qt_hbm[h, 128:192, :],
                                         qtr_st[:, pi * NQ:(pi + 1) * NQ],
                                         eng="sync", deps=[oD_], dkey=f"stqr{h % 2}", tag=f"stqr.{h}")
                    ps1c_guard = [qc_user[0], qc_user[1], rope_user[0]]

                ps1dk = contextlib.ExitStack()
                with ps1dk:
                    kcps = [ps1dk.enter_context(
                        nc.psum_tensor(f"kcps{i}", [128, 512], F32)) for i in range(2)]
                    stkc = {}
                    ldku = {}
                    kc_user = {}
                    kc_lastmm = {}
                    for h in range(NH):
                        ldku[h] = ld(_r3(wkuh, (h % 2) * 4 * 128, 4, 128),
                                     wk_up[:, 128 * h:128 * h + 128].rearrange(
                                         "(n p) c -> p n c", p=128),
                                     deps=[kc_lastmm.get(h - 2)] +
                                          (g1ab_last if h < 2 else []),
                                     dkey=f"ldku{h % 2}", tag=f"ldku.{h}")
                        for qq4 in range(4):
                            qg = 4 * h + qq4
                            sl = qg % 2
                            evs = []
                            for g2 in range(4):
                                g8 = 4 * qq4 + g2
                                pp = g8 % 2

                                def f(t, h=h, g8=g8, pp=pp):
                                    inst = None
                                    for kk in range(4):
                                        inst = t.matmul(
                                            kcps[pp][:, :],
                                            wkuh[:, (h % 2) * 512 + 128 * kk:
                                                 (h % 2) * 512 + 128 * kk + 128],
                                            kvlat[:, 8192 * kk + 512 * g8:
                                                  8192 * kk + 512 * g8 + 512],
                                            start=(kk == 0), stop=(kk == 3))
                                    return inst
                                mm = G.op("tensor", f,
                                          deps=[last_kv, ldku[h], kc_user.get(pp)] +
                                               (ps1c_guard if h == 0 and g8 == 0 else []),
                                          tag=f"kcmm.{h}.{g8}")
                                ev = G.op("vector",
                                          lambda v, g2=g2, pp=pp, sl=sl: v.tensor_copy(
                                              ktc_st[:, sl * 2048 + 512 * g2:
                                                     sl * 2048 + 512 * g2 + 512],
                                              kcps[pp][:, :]),
                                          deps=[mm, stkc.get(qg - 2)],
                                          tag=f"kcev.{h}.{g8}")
                                kc_user[pp] = ev
                                evs.append(ev)
                            stkc[qg] = ld(kt_hbm[h, 0:128, 2048 * qq4:2048 * qq4 + 2048]
                                          if False else
                                          kt_hbm[h, :, 2048 * qq4:2048 * qq4 + 2048],
                                          ktc_st[:, sl * 2048:(sl + 1) * 2048],
                                          eng="sync", deps=evs, dkey=f"stkc{qg % 2}", tag=f"stkc.{qg}")
                        kc_lastmm[h] = mm
                    ps1dk_guard = [kc_user[0], kc_user[1]]

                ps1dv = contextlib.ExitStack()
                with ps1dv:
                    vps = [ps1dv.enter_context(
                        nc.psum_tensor(f"vps{i}", [128, 2048], F32)) for i in range(2)]

                    def f_ones(gp):
                        for pi2 in range(2):
                            base = pi2 * NH * VS
                            inst = gp.memset(
                                _r3(vstage, base, NH, VS)[:, :, 128:130], 1.0)
                        return inst
                    o_ones = G.op("gpsimd", f_ones, deps=g1ab_last, tag="vones")

                    stv, evv, mmv = {}, {}, {}
                    for tt in range(NKC):
                        pi = tt % 2

                        def f(t, tt=tt, pi=pi):
                            inst = None
                            for kk in range(4):
                                for hg in range(4):
                                    inst = t.matmul(
                                        vps[pi][:, 512 * hg:512 * hg + 512],
                                        kvlat[:, 8192 * kk + 128 * tt:8192 * kk + 128 * tt + 128],
                                        wvu_sb[:, kk * 2048 + 512 * hg:kk * 2048 + 512 * hg + 512],
                                        start=(kk == 0), stop=(kk == 3))
                            return inst
                        mmv[tt] = G.op("tensor", f,
                                       deps=[last_kv, o_g1cd, evv.get(tt - 2)] +
                                            (ps1dk_guard if tt == 0 else []),
                                       tag=f"vmm.{tt}")

                        def f_ev(v, tt=tt, pi=pi):
                            inst = None
                            for hg in range(4):
                                dst = _r3(vstage, pi * NH * VS + VS * 4 * hg, 4, VS)[:, :, 0:128]
                                inst = v.tensor_copy(
                                    dst,
                                    vps[pi].rearrange("p (a b) -> p a b", b=128)[:, 4 * hg:4 * hg + 4, :])
                            return inst
                        evv[tt] = G.op("vector", f_ev,
                                       deps=[mmv[tt], o_ones, stv.get(tt - 2)],
                                       tag=f"vev.{tt}")
                        stv[tt] = ld(v_hbm[tt], vstage[:, pi * NH * VS:(pi + 1) * NH * VS],
                                     eng="sync", deps=[evv[tt]], dkey=f"stv{tt % 2}", tag=f"stv.{tt}")
                    ps1dv_guard = [evv[NKC - 1], evv[NKC - 2]]

                ph1_last = [stv[NKC - 1], stv[NKC - 2], stq[(NH - 1, 1)], stq[(NH - 2, 1)],
                            stq[(NH - 1, 0)], stq[(NH - 2, 0)],
                            stkc[4 * NH - 1], stkc[4 * NH - 2],
                            w2pp_done, w2pp_done2, stkr[15], stkr[14]] + ps1dv_guard

        if upto < 2:
            _finish_stub(G, nc, es, out, ph1_last)
            G.emit(nc, es)
            return nc

        p2 = contextlib.ExitStack()
        with p2:
            P = p2.enter_context
            qtc_b = P(nc.sbuf_tensor("qtc_b", [128, 2 * NQ], BF16))
            qtr_b = P(nc.sbuf_tensor("qtr_b", [128, 2 * NQ], BF16))
            ktc_b = P(nc.sbuf_tensor("ktc_b", [128, 2 * L], BF16))
            ktr_b = P(nc.sbuf_tensor("ktr_b", [128, L], BF16))
            v_b = P(nc.sbuf_tensor("v_b", [128, 2 * NKC * VS], BF16))
            pt_b = P(nc.sbuf_tensor("pt_b", [128, 3 * 1024], BF16))
            mask_sb = P(nc.sbuf_tensor("mask_sb", [128, 16 * 512], BF16))
            obf = P(nc.sbuf_tensor("obf", [128, 2 * NQ], BF16))
            rs_sb = P(nc.sbuf_tensor("rs_sb", [128, 8], F32))
            ssum_sb = P(nc.sbuf_tensor("ssum_sb", [128, 8], F32))
            aot_st = P(nc.sbuf_tensor("aot_st", [128, 2 * NQ], BF16))

            stp = [P(nc.psum_tensor(f"st{i}", [128, 1024], F32)) for i in range(2)]
            oA = P(nc.psum_tensor("oA", [128, 512], F32))
            oB = P(nc.psum_tensor("oB", [128, 512], F32))
            tp2 = [P(nc.psum_tensor(f"tp2{i}", [128, 512], BF16)) for i in range(2)]
            OFF = [0, 144]

            o_mask = ld(mask_sb[:, :], maskp, deps=ph1_last, dkey="g2", tag="maskld")
            o_ktr = ld(ktr_b[0:64, :], kr_hbm, deps=ph1_last, dkey="g2", tag="ktrld")
            o_g2 = o_ktr

            def f_zr(gp):
                gp.memset(ktr_b[64:128, :], 0.0)
                return gp.memset(qtr_b[64:128, :], 0.0)
            o_zr = G.op("gpsimd", f_zr, deps=ph1_last, tag="zrope")

            lq, lk, lv = {}, {}, {}
            sc_last, pv_last = {}, {}
            exop, pvop, scop, mkop = {}, {}, {}, {}
            sfin, trh, evh, sta = {}, {}, {}, {}
            tasks = []
            for h in range(NH):
                for j in range(4):
                    nch = 16 * (j + 1)
                    for p in range(nch // 2):
                        tasks.append((h, j, p, nch))
            NT = len(tasks)
            DEPTH = 2

            def make_loads(h):
                pi = h % 2
                dep_prev = [sc_last.get(h - 2), pv_last.get(h - 2),
                            trh.get((h - 2, 3))] + (ph1_last if h < 2 else [])
                lq[h] = ld(qtc_b[:, pi * NQ:(pi + 1) * NQ], qt_hbm[h, 0:128, :],
                           eng="sync", deps=dep_prev, dkey=f"l2h{h % 2}", tag=f"lqc.{h}")
                lq[(h, 1)] = ld(qtr_b[0:64, pi * NQ:(pi + 1) * NQ], qt_hbm[h, 128:192, :],
                                eng="sync", deps=dep_prev, dkey=f"l2h{h % 2}", tag=f"lqr.{h}")
                lk[h] = ld(ktc_b[:, pi * L:(pi + 1) * L], kt_hbm[h],
                           eng="sync", deps=dep_prev, dkey=f"l2h{h % 2}", tag=f"lkc.{h}")
                for part in range(8):
                    src = v_hbm[8 * part:8 * part + 8, :, VS * h:VS * h + VS]\
                        .rearrange("a p c -> p a c")
                    dst = _r3(v_b, pi * NKC * VS + 8 * part * VS, 8, VS)
                    lv[(h, part)] = ld(dst, src, eng="sync", deps=dep_prev,
                                       dkey=f"l2h{h % 2}", tag=f"lv.{h}.{part}")

            def make_sc(Pg):
                h, j, p, nch = tasks[Pg]
                pi = h % 2
                pp = Pg % 2
                pp3 = Pg % 3
                masked = [c for c in (2 * p, 2 * p + 1) if c >= 16 * j]

                def f_sc(t, p=p, j=j, pp=pp, pi=pi):
                    inst = None
                    for half in range(2):
                        c = 2 * p + half
                        cs = slice(512 * half, 512 * half + 512)
                        t.matmul(stp[pp][:, cs],
                                 ktc_b[:, pi * L + 128 * c:pi * L + 128 * c + 128],
                                 qtc_b[:, pi * NQ + 512 * j:pi * NQ + 512 * j + 512],
                                 start=True, stop=False)
                        inst = t.matmul(
                            stp[pp][:, cs],
                            ktr_b[:, 128 * c:128 * c + 128],
                            qtr_b[:, pi * NQ + 512 * j:pi * NQ + 512 * j + 512],
                            start=False, stop=True)
                    return inst
                scop[Pg] = G.op("tensor", f_sc,
                                deps=[lv[(h, 7)], o_g2, o_zr,
                                      exop.get(Pg - 2)] +
                                     ([ph1_last[0]] if Pg == 0 else []),
                                tag=f"sc.{Pg}")

            def make_ex(Pg):
                h, j, p, nch = tasks[Pg]
                pp = Pg % 2
                pp3 = Pg % 3
                masked = [c for c in (2 * p, 2 * p + 1) if c >= 16 * j]
                exop[Pg] = G.op("scalar",
                                lambda s, pp=pp, pp3=pp3: s.activation(
                                    pt_b[:, pp3 * 1024:pp3 * 1024 + 1024],
                                    stp[pp][:, :], AF.Exp,
                                    bias=consts[:, CB:CB + 1], scale=1.0),
                                deps=[scop[Pg], o_consts, pvop.get(Pg - 3)],
                                tag=f"ex.{Pg}")
                mkop[Pg] = None
                if masked:
                    def f_mk(v, masked=tuple(masked), pp3=pp3, j=j, p=p):
                        inst = None
                        for c in masked:
                            half = c - 2 * p
                            inst = v.tensor_mul(
                                pt_b[:, pp3 * 1024 + 512 * half:
                                     pp3 * 1024 + 512 * half + 512],
                                pt_b[:, pp3 * 1024 + 512 * half:
                                     pp3 * 1024 + 512 * half + 512],
                                mask_sb[:, 512 * (c - 16 * j):512 * (c - 16 * j) + 512])
                        return inst
                    mkop[Pg] = G.op("vector", f_mk, deps=[exop[Pg], o_g2],
                                    tag=f"mk.{Pg}")

            def make_pv(Pg):
                h, j, p, nch = tasks[Pg]
                pi = h % 2
                pp3 = Pg % 3
                jg = 4 * h + j
                jp = jg % 2

                def f_pv(t, p=p, pp3=pp3, pi=pi, nch=nch):
                    inst = None
                    for half in range(2):
                        c = 2 * p + half
                        for u in range(4):
                            inst = t.matmul(
                                (oA if u < 2 else oB)[:, OFF[u % 2]:OFF[u % 2] + 129],
                                pt_b[:, pp3 * 1024 + 512 * half + 128 * u:
                                     pp3 * 1024 + 512 * half + 128 * u + 128],
                                v_b[:, pi * NKC * VS + VS * c:
                                    pi * NKC * VS + VS * c + 129],
                                start=(c == 0 and u % 2 == 0),
                                stop=(c == nch - 1 and u % 2 == 1),
                                skip_group_check=True)
                    return inst
                pvdeps = [exop[Pg], mkop[Pg], lv[(h, 7)]]
                if p == 0:
                    pvdeps.append(sfin.get(jg - 1))
                pvop[Pg] = G.op("tensor", f_pv, deps=pvdeps, tag=f"pv.{Pg}")
                if p != nch // 2 - 1:
                    return
                vs1 = G.op("vector",
                           lambda v, jp=jp: v.tensor_copy(
                               ssum_sb[:, 4 * jp:4 * jp + 2], oA[:, 128:273:144]),
                           deps=[pvop[Pg], sfin.get(jg - 1)], tag=f"vs1.{jg}")
                vs2 = G.op("vector",
                           lambda v, jp=jp: v.tensor_copy(
                               ssum_sb[:, 4 * jp + 2:4 * jp + 4], oB[:, 128:273:144]),
                           deps=[pvop[Pg], sfin.get(jg - 1)], tag=f"vs2.{jg}")
                vrec = G.op("vector",
                            lambda v, jp=jp: v.reciprocal(
                                rs_sb[:, 4 * jp:4 * jp + 4],
                                ssum_sb[:, 4 * jp:4 * jp + 4]),
                            deps=[vs1, vs2], tag=f"vrec.{jg}")

                def f_fin(s, j=j, pi=pi, jp=jp):
                    inst = None
                    for u in range(4):
                        ob = oA if u < 2 else oB
                        ui = (0 if u < 2 else 2) + u % 2
                        inst = s.activation(
                            obf[:, pi * NQ + 512 * j + 128 * u:
                                pi * NQ + 512 * j + 128 * u + 128],
                            ob[:, OFF[u % 2]:OFF[u % 2] + 128],
                            AF.Copy, bias=0.0,
                            scale=rs_sb[:, 4 * jp + ui:4 * jp + ui + 1])
                    return inst
                sfin[jg] = G.op("scalar", f_fin,
                                deps=[vrec, trh.get((h - 2, 3))], tag=f"sfin.{jg}")
                if j != 3:
                    return
                for r in range(4):
                    def f_tr(t, r=r, pi=pi):
                        inst = None
                        for k in range(4):
                            u = 4 * r + k
                            inst = t.transpose(
                                tp2[r % 2][:, 128 * k:128 * k + 128],
                                obf[:, pi * NQ + 128 * u:pi * NQ + 128 * u + 128],
                                ident[:, :])
                        return inst
                    trh[(h, r)] = G.op("tensor", f_tr,
                                       deps=[sfin[4 * h + 3], evh.get((h, r - 2)), o_id],
                                       tag=f"trh.{h}.{r}")
                    evh[(h, r)] = G.op("vector",
                                       lambda v, r=r, pi=pi: v.tensor_copy(
                                           aot_st[:, pi * NQ + 512 * r:pi * NQ + 512 * r + 512],
                                           tp2[r % 2][:, :]),
                                       deps=[trh[(h, r)], sta.get(h - 2)],
                                       tag=f"evh.{h}.{r}")
                sta[h] = ld(aot_hbm[h], aot_st[:, pi * NQ:(pi + 1) * NQ], eng="scalar",
                            deps=[evh[(h, 3)]], dkey=f"sta{h % 2}", tag=f"sta.{h}")

            for Pg in range(NT):
                h = tasks[Pg][0]
                if Pg == 0 or tasks[Pg - 1][0] != h:
                    make_loads(h)
                make_sc(Pg)
                if Pg >= DEPTH:
                    make_pv(Pg - DEPTH)
                make_ex(Pg)
                if tasks[Pg][1] == 3 and tasks[Pg][2] == tasks[Pg][3] // 2 - 1:
                    sc_last[h] = scop[Pg]
                if Pg >= DEPTH:
                    hd = tasks[Pg - DEPTH][0]
                    td = tasks[Pg - DEPTH]
                    if td[1] == 3 and td[2] == td[3] // 2 - 1:
                        pv_last[hd] = pvop[Pg - DEPTH]
            for Pg in range(NT - DEPTH, NT):
                make_pv(Pg)
            pv_last[NH - 1] = pvop[NT - 1]
            ph2_last = [sta[NH - 1], sta[NH - 2], pvop[NT - 1], sfin[4 * NH - 1],
                        trh[(NH - 1, 3)], evh[(NH - 1, 3)]]
            ph2_guard = [exop[NT - 1], exop[NT - 2], sfin[4 * NH - 1],
                         evh[(NH - 1, 3)], evh[(NH - 1, 2)]]

        if upto < 3:
            _finish_stub(G, nc, es, out, ph2_last)
            G.emit(nc, es)
            return nc

        p34 = contextlib.ExitStack()
        y1t_sb = p34.enter_context(nc.sbuf_tensor("y1t_sb", [128, 16 * NQ], BF16))
        p3 = contextlib.ExitStack()
        with p3:
            P = p3.enter_context
            wo_sb = P(nc.sbuf_tensor("wo_sb", [128, 16 * D], BF16))
            aotm = P(nc.sbuf_tensor("aotm", [128, 2 * D], BF16))
            xq_sb = P(nc.sbuf_tensor("xq_sb", [128, 2 * D], F32))
            ypre = P(nc.sbuf_tensor("ypre", [128, D], F32))
            sqd = P(nc.sbuf_tensor("sqd", [128, D], F32))
            y1f = P(nc.sbuf_tensor("y1f", [128, 2 * D], F32))
            y1b = P(nc.sbuf_tensor("y1b", [128, D], BF16))
            y1ps = P(nc.psum_tensor("y1ps", [128, 2048], F32))
            tp3 = [P(nc.psum_tensor(f"tp3{i}", [128, 512], BF16)) for i in range(2)]

            o_wo = ld(_r3(wo_sb, 0, 16, D),
                      wo.rearrange("(n p) c -> p n c", p=128),
                      deps=ph2_last, dkey="g3", tag="wold")
            xqv = xq.rearrange("(n p) d -> n p d", p=128)
            y1v = y1_hbm.rearrange("(n p) d -> n p d", p=128)
            lnx = {}
            sty1 = {}
            evt3 = {}
            prev_mm = {}

            def make_tr3(m):
                pi = m % 2
                l9 = lnx[(m, "l9")]
                for r in range(4):
                    def f_tr(t, r=r):
                        inst = None
                        for k in range(4):
                            dd = 4 * r + k
                            inst = t.transpose(tp3[r % 2][:, 128 * k:128 * k + 128],
                                               y1b[:, 128 * dd:128 * dd + 128],
                                               ident[:, :])
                        return inst
                    tr = G.op("tensor", f_tr,
                              deps=[l9, evt3.get((m, r - 2)), evt3.get((m - 1, r + 2)),
                                    o_id] + (ph2_guard if m == 0 and r == 0 else []),
                              tag=f"tr3.{m}.{r}")

                    def f_ev(v, r=r, m=m):
                        dst = y1t_sb.rearrange("p (a c) -> p a c", c=2048)[
                            :, 4 * r:4 * r + 4, 128 * m:128 * m + 128]
                        return v.tensor_copy(
                            dst, tp3[r % 2].rearrange("p (a b) -> p a b", b=128))
                    evt3[(m, r)] = G.op("vector", f_ev, deps=[tr], tag=f"evt3.{m}.{r}")

            for m in range(16):
                pi = m % 2
                lda = []
                for q4 in range(4):
                    src = aot_hbm[4 * q4:4 * q4 + 4, :, 128 * m:128 * m + 128]\
                        .rearrange("a p c -> p a c")
                    dst = _r3(aotm, pi * D + 512 * q4, 4, 128)
                    lda.append(ld(dst, src, eng="sync",
                                  deps=[prev_mm.get(m - 2)] + (ph2_last if m < 2 else []),
                                  dkey=f"lda{m % 2}", tag=f"lda.{m}.{q4}"))
                ldx3 = ld(xq_sb[:, pi * D:(pi + 1) * D], xqv[m],
                          eng="scalar", deps=[lnx.get((m - 2, "evy"))] +
                          (ph2_last if m < 2 else []), dkey=f"ldx3{m % 2}",
                          tag=f"ldx3.{m}")

                def f_mm(t, pi=pi):
                    inst = None
                    for hh in range(16):
                        for dg in range(4):
                            inst = t.matmul(
                                y1ps[:, 512 * dg:512 * dg + 512],
                                aotm[:, pi * D + 128 * hh:pi * D + 128 * hh + 128],
                                wo_sb[:, hh * D + 512 * dg:hh * D + 512 * dg + 512],
                                start=(hh == 0), stop=(hh == 15))
                    return inst
                mm = G.op("tensor", f_mm,
                          deps=[lda[3], o_wo, lnx.get((m - 1, "evy"))] +
                               (ph2_guard if m == 0 else []),
                          tag=f"womm.{m}")
                prev_mm[m] = mm
                if m >= 1:
                    make_tr3(m - 1)

                def f_evy(v, pi=pi):
                    inst = None
                    for dg in range(4):
                        inst = v.tensor_add(
                            ypre[:, 512 * dg:512 * dg + 512],
                            y1ps[:, 512 * dg:512 * dg + 512],
                            xq_sb[:, pi * D + 512 * dg:pi * D + 512 * dg + 512])
                    return inst
                evy = G.op("vector", f_evy,
                           deps=[mm, ldx3, lnx.get((m - 1, "l9")),
                                 lnx.get((m - 1, "l3"))],
                           tag=f"evy.{m}")
                lnx[(m, "evy")] = evy
                _ln_chain(G, "p3", m, evy, ypre, sqd, stat, consts,
                          (CID, CEPS, CNH), y1f, pi, lnx, AXP=AX,
                          war=sty1.get(m - 2))
                l9 = G.op("vector",
                          lambda v, pi=pi: v.tensor_copy(
                              y1b[:, :], y1f[:, pi * D:(pi + 1) * D]),
                          deps=[lnx[(m, "l8")], evt3.get((m - 1, 3))],
                          tag=f"p3ln9.{m}")
                lnx[(m, "l9")] = l9
                sty1[m] = ld(y1v[m], y1f[:, pi * D:(pi + 1) * D], eng="scalar",
                             deps=[lnx[(m, "l8")]], dkey=f"sty1{m % 2}",
                             tag=f"sty1.{m}")
            make_tr3(15)
            ph3_last = [sty1[15], sty1[14]] + [evt3[(15, r)] for r in range(4)] + \
                [lnx[(15, "evy")], lnx[(15, "l9")]]
            ph3_guard = [lnx[(15, "evy")], lnx[(15, "l3")],
                         evt3[(15, 2)], evt3[(15, 3)]]

        if upto < 4:
            _finish_stub(G, nc, es, out, ph3_last)
            G.emit(nc, es)
            return nc

        hmuls, sth = {}, {}
        p4 = contextlib.ExitStack()
        with p4:
            P = p4.enter_context
            w1c = P(nc.sbuf_tensor("w1c", [128, 2 * D], BF16))
            w3c = P(nc.sbuf_tensor("w3c", [128, 2 * D], BF16))
            silu_sb = P(nc.sbuf_tensor("silu_sb", [128, 2 * 1024], F32))
            hidst = P(nc.sbuf_tensor("hidst", [128, 2 * D], BF16))
            gps = [P(nc.psum_tensor(f"gps{i}", [128, 1024], F32)) for i in range(2)]
            lps = [P(nc.psum_tensor(f"lps{i}", [128, 1024], F32)) for i in range(2)]

            for hh in range(64):
                pi = hh % 2
                wdep = [hmuls.get((hh - 2, 1))] + (ph3_last if hh < 2 else [])
                ldw1 = ld(_r3(w1c, pi * D, 16, 128),
                          w1til[hh].rearrange("n p c -> p n c"), deps=wdep, dkey=f"ldw1{hh % 2}", tag=f"ldw1.{hh}")
                ldw3 = ld(_r3(w3c, pi * D, 16, 128),
                          w3til[hh].rearrange("n p c -> p n c"), deps=wdep, dkey=f"ldw3{hh % 2}", tag=f"ldw3.{hh}")
                for half in range(2):
                    def f_g(t, pi=pi, half=half):
                        inst = None
                        for dd in range(16):
                            for tg in range(2):
                                inst = t.matmul(
                                    gps[half][:, 512 * tg:512 * tg + 512],
                                    w1c[:, pi * D + 128 * dd:pi * D + 128 * dd + 128],
                                    y1t_sb[:, 2048 * dd + 1024 * half + 512 * tg:
                                           2048 * dd + 1024 * half + 512 * tg + 512],
                                    start=(dd == 0), stop=(dd == 15))
                        return inst
                    depg = [ldw1, hmuls.get((hh - 1, half))]
                    if hh == 0 and half == 0:
                        depg += ph3_guard
                    mmg = G.op("tensor", f_g, deps=depg, tag=f"mmg.{hh}.{half}")

                    def f_l(t, pi=pi, half=half):
                        inst = None
                        for dd in range(16):
                            for tg in range(2):
                                inst = t.matmul(
                                    lps[half][:, 512 * tg:512 * tg + 512],
                                    w3c[:, pi * D + 128 * dd:pi * D + 128 * dd + 128],
                                    y1t_sb[:, 2048 * dd + 1024 * half + 512 * tg:
                                           2048 * dd + 1024 * half + 512 * tg + 512],
                                    start=(dd == 0), stop=(dd == 15))
                        return inst
                    mml = G.op("tensor", f_l,
                               deps=[ldw3, hmuls.get((hh - 1, half))],
                               tag=f"mml.{hh}.{half}")
                    sil = G.op("scalar",
                               lambda s, half=half: s.activation(
                                   silu_sb[:, 1024 * half:1024 * half + 1024],
                                   gps[half][:, :], AF.Silu),
                               deps=[mmg, hmuls.get((hh - 1, half))],
                               tag=f"sil.{hh}.{half}")
                    hmuls[(hh, half)] = G.op(
                        "vector",
                        lambda v, half=half, pi=pi: v.tensor_mul(
                            hidst[:, pi * D + 1024 * half:pi * D + 1024 * half + 1024],
                            lps[half][:, :],
                            silu_sb[:, 1024 * half:1024 * half + 1024]),
                        deps=[mml, sil, sth.get((hh - 2, 3))],
                        tag=f"hmul.{hh}.{half}")
                for r in range(4):
                    src = _r3(hidst, pi * D + 512 * r, 4, 128)
                    sth[(hh, r)] = ld(hid_hbm[4 * r:4 * r + 4, hh].rearrange(
                        "a p c -> p a c"), src, eng="sync",
                                      deps=[hmuls[(hh, r // 2)]],
                                      dkey=f"sth{hh % 2}", tag=f"sth.{hh}.{r}")
            up_last = [sth[(63, r)] for r in range(4)] + [sth[(62, 3)]]
            up_guard = [hmuls[(63, 0)], hmuls[(63, 1)], hmuls[(62, 0)], hmuls[(62, 1)]]

        p34.close()
        p4b = contextlib.ExitStack()
        with p4b:
            P = p4b.enter_context
            hidt = P(nc.sbuf_tensor("hidt", [128, 4 * 64 * 128], BF16))
            w2c = P(nc.sbuf_tensor("w2c", [128, 2 * 64 * 256], BF16))
            dstg = P(nc.sbuf_tensor("dstg", [128, 2 * 256], F32))
            y2ld = P(nc.sbuf_tensor("y2ld", [128, 2 * D], F32))
            y1ld = P(nc.sbuf_tensor("y1ld", [128, 2 * D], F32))
            ypre4 = P(nc.sbuf_tensor("ypre4", [128, D], F32))
            sqd4 = P(nc.sbuf_tensor("sqd4", [128, D], F32))
            outf = P(nc.sbuf_tensor("outf", [128, 2 * D], F32))
            dps = [P(nc.psum_tensor(f"dps{i}", [128, 256], F32)) for i in range(2)]

            ldh, ldw2, mmd, evd, std = {}, {}, {}, {}, {}
            outv = out.rearrange("(n p) d -> n p d", p=128)
            ln4 = {}
            sto = {}

            def emit_ln4_group(g):
                for tt in range(4 * g, 4 * g + 4):
                    pi = tt % 2
                    l2d = ld(y2ld[:, pi * D:(pi + 1) * D], y2f_hbm[tt],
                             eng="gpsimd", deps=[std[(g, 7, tt)],
                                                 ln4.get((tt - 2, "evy"))],
                             dkey=f"ly2{tt % 2}", tag=f"ly2.{tt}")
                    l1d = ld(y1ld[:, pi * D:(pi + 1) * D], y1v[tt],
                             eng="gpsimd", deps=[sty1[15], sty1[14],
                                                 ln4.get((tt - 2, "evy"))],
                             dkey=f"ly1{tt % 2}", tag=f"ly1.{tt}")

                    def f_add(v, pi=pi):
                        inst = None
                        for dg in range(2):
                            inst = v.tensor_add(
                                ypre4[:, 1024 * dg:1024 * dg + 1024],
                                y2ld[:, pi * D + 1024 * dg:pi * D + 1024 * dg + 1024],
                                y1ld[:, pi * D + 1024 * dg:pi * D + 1024 * dg + 1024])
                        return inst
                    evy4 = G.op("vector", f_add,
                                deps=[l2d, l1d, ln4.get((tt - 1, "l9")),
                                      ln4.get((tt - 1, "l3"))],
                                tag=f"evy4.{tt}")
                    ln4[(tt, "evy")] = evy4
                    _ln_chain(G, "p4", tt, evy4, ypre4, sqd4, stat, consts,
                              (CID, CEPS, CNH), outf, pi, ln4, AXP=AX,
                              war=sto.get(tt - 2))
                    ln4[(tt, "l9")] = ln4[(tt, "l8")]
                    sto[tt] = ld(outv[tt], outf[:, pi * D:(pi + 1) * D],
                                 eng="gpsimd", deps=[ln4[(tt, "l8")]],
                                 dkey=f"sto{tt % 2}", tag=f"sto.{tt}")

            for tg4 in range(4):
                for e8 in range(2):
                    ldw2[(tg4, e8)] = ld(
                        _r3(w2c, (e8 % 2) * 64 * 256, 64, 256),
                        w2pp[e8].rearrange("n p c -> p n c"),
                        eng=("scalar", "gpsimd")[e8 % 2],
                        deps=[w2pp_done, w2pp_done2,
                              mmd.get((tg4 - 1, e8 + 6, 4 * (tg4 - 1) + 3))] +
                             (up_guard + up_last if tg4 == 0 else []),
                        dkey=f"ldw2{e8 % 2}", tag=f"ldw2.{tg4}.{e8}")
                if tg4 >= 2:
                    emit_ln4_group(tg4 - 2)
                for i in range(4):
                    tt = 4 * tg4 + i
                    ldh[tt] = ld(_r3(hidt, i * 64 * 128, 64, 128),
                                 hid_hbm[tt].rearrange("n p c -> p n c"), eng="sync",
                                 deps=[up_last[3], up_last[4],
                                       mmd.get((tg4 - 1, 7, 4 * (tg4 - 1) + i))],
                                 dkey=f"ldh{tg4 % 2}", tag=f"ldh.{tt}")
                for e8 in range(8):
                    pi = e8 % 2
                    if e8 >= 2:
                        ldw2[(tg4, e8)] = ld(
                            _r3(w2c, pi * 64 * 256, 64, 256),
                            w2pp[e8].rearrange("n p c -> p n c"),
                            eng=("scalar", "gpsimd")[e8 % 2],
                            deps=[w2pp_done, w2pp_done2,
                                  mmd.get((tg4, e8 - 2, 4 * tg4 + 3)),
                                  mmd.get((tg4 - 1, e8 + 6, 4 * (tg4 - 1) + 3))],
                            dkey=f"ldw2{e8 % 2}", tag=f"ldw2.{tg4}.{e8}")
                    for i in range(4):
                        tt = 4 * tg4 + i

                        def f_d(t, i=i, pi=pi, tt=tt):
                            inst = None
                            for hh in range(64):
                                inst = t.matmul(
                                    dps[tt % 2][:, :],
                                    hidt[:, i * 64 * 128 + 128 * hh:
                                         i * 64 * 128 + 128 * hh + 128],
                                    w2c[:, pi * 64 * 256 + 256 * hh:
                                        pi * 64 * 256 + 256 * hh + 256],
                                    start=(hh == 0), stop=(hh == 63))
                            return inst
                        depd = [ldh[tt], ldw2[(tg4, e8)],
                                evd.get((tg4, e8, tt - 2)),
                                evd.get((tg4, e8 - 1, tt + 2))]
                        if tg4 == 0 and e8 == 0 and i == 0:
                            depd += up_guard + [lnx[(15, "l3")]]
                        mmd[(tg4, e8, tt)] = G.op("tensor", f_d, deps=depd,
                                                  tag=f"mmd.{tg4}.{e8}.{tt}")
                        evd[(tg4, e8, tt)] = G.op(
                            "vector",
                            lambda v, tt=tt: v.tensor_copy(
                                dstg[:, 256 * (tt % 2):256 * (tt % 2) + 256],
                                dps[tt % 2][:, :]),
                            deps=[mmd[(tg4, e8, tt)], std.get((tg4, e8, tt - 2)),
                                  std.get((tg4, e8 - 1, tt + 2)),
                                  std.get((tg4 - 1, 7, 4 * (tg4 - 1) + 2 + tt % 2))
                                  if e8 == 0 else None],
                            tag=f"evd.{tg4}.{e8}.{tt}")
                        std[(tg4, e8, tt)] = ld(
                            y2f_hbm[tt, :, 256 * e8:256 * e8 + 256],
                            dstg[:, 256 * (tt % 2):256 * (tt % 2) + 256],
                            eng="sync", deps=[evd[(tg4, e8, tt)]],
                            dkey=f"std{tt % 2}", tag=f"std.{tg4}.{e8}.{tt}")

            emit_ln4_group(2)
            emit_ln4_group(3)

            G.op("sync", lambda sy: sy.nop(), deps=[sto[i] for i in range(16)],
                 tag="fence")

        G.emit(nc, es)
    return nc


def _ln_chain(G, pfx, m, evy, ypre, sqd, stat, consts, cc, yout, pi, store, AXP,
              war=None):
    CID, CEPS, CNH = cc
    l1 = G.op("vector",
              lambda v: v.reduce_sum(stat[:, 0:1], ypre[:, :], axis=AXP, negate=True),
              deps=[evy, store.get((m - 1, "l8"))], tag=f"{pfx}ln1.{m}")
    l2 = G.op("vector",
              lambda v: v.tensor_scalar_mul(stat[:, 1:2], stat[:, 0:1],
                                            consts[:, CID:CID + 1]),
              deps=[l1], tag=f"{pfx}ln2.{m}")
    l3 = G.op("scalar",
              lambda s: s.activation(sqd[:, :], ypre[:, :], AF.Square,
                                     bias=stat[:, 1:2], scale=1.0,
                                     accum_out=stat[:, 2:3]),
              deps=[l2, store.get((m - 1, "l3"))], tag=f"{pfx}ln3.{m}")
    store[(m, "l3")] = l3
    l4 = G.op("vector",
              lambda v: v.tensor_scalar(stat[:, 3:4], stat[:, 2:3],
                                        consts[:, CID:CID + 1],
                                        consts[:, CEPS:CEPS + 1],
                                        ALU.mult, ALU.add),
              deps=[l3], tag=f"{pfx}ln4.{m}")
    l5 = G.op("scalar",
              lambda s: s.activation(stat[:, 4:5], stat[:, 3:4], AF.Ln),
              deps=[l4], tag=f"{pfx}ln5.{m}")
    l6 = G.op("scalar",
              lambda s: s.activation(stat[:, 5:6], stat[:, 4:5], AF.Exp,
                                     scale=consts[:, CNH:CNH + 1]),
              deps=[l5], tag=f"{pfx}ln6.{m}")
    l7 = G.op("vector",
              lambda v: v.tensor_mul(stat[:, 6:7], stat[:, 1:2], stat[:, 5:6]),
              deps=[l6, l2], tag=f"{pfx}ln7.{m}")
    l8 = G.op("vector",
              lambda v: v.tensor_scalar(yout[:, pi * 2048:(pi + 1) * 2048], ypre[:, :],
                                        stat[:, 5:6], stat[:, 6:7],
                                        ALU.mult, ALU.add),
              deps=[l7, l3, war], tag=f"{pfx}ln8.{m}")
    store[(m, "l8")] = l8
    return l8


def _finish_stub(G, nc, es, out, deps):
    z = es.enter_context(nc.sbuf_tensor("zstub", [128, D], F32))
    zo = G.op("gpsimd", lambda g: g.memset(z[:, :], 0.0), tag="zstub")
    outv = out.rearrange("(n p) d -> n p d", p=128)
    last = []
    for m in range(16):
        last.append(G.op("sync", lambda sy, m=m: sy.dma_start(outv[m], z[:, :]),
                         deps=[zo] + list(deps), dma=True, tag=f"zo.{m}"))
    G.op("sync", lambda sy: sy.nop(), deps=[last[-1]], tag="fence")


def _sylvester(n):
    H = np.array([[1.0]])
    while H.shape[0] < n:
        H = np.block([[H, H], [H, -H]])
    return H.astype(np.float32)


def make_core_inputs(x, W, core):
    b, g = core // 4, core % 4
    ins = dict(W)
    ins["xb"] = np.ascontiguousarray(x[b])
    ins["xq"] = np.ascontiguousarray(x[b, g::4, :])
    inv = (1.0 / 10000.0 ** (np.arange(0, RD, 2, dtype=np.float64) / RD))
    posq = (g + 4 * np.arange(NQ, dtype=np.float64))
    ang_q = np.outer(inv, posq)
    cq, sq = np.cos(ang_q), np.sin(ang_q)
    ins["cosq"] = (np.concatenate([cq, cq], 0) * SCALE).astype(np.float32)
    ins["sinq"] = (np.concatenate([-sq, sq], 0) * SCALE).astype(np.float32)
    posk = np.arange(L, dtype=np.float64)
    ang_k = np.outer(inv, posk)
    ck, sk = np.cos(ang_k), np.sin(ang_k)
    ins["cosk"] = np.concatenate([ck, ck], 0).astype(np.float32)
    ins["sink"] = np.concatenate([-sk, sk], 0).astype(np.float32)
    p_ = np.arange(128)[:, None]
    i_ = np.arange(512)[None, :]
    tiles = [(128 * cc + p_ <= g + 4 * i_).astype(np.float32) for cc in range(16)]
    ins["maskp"] = np.ascontiguousarray(np.concatenate(tiles, axis=1))
    return ins


def make_shared_weights(inputs):
    W = {}
    for k in ("wq_down", "wq_up", "wq_rope", "wkv_down", "wk_up", "wv_up",
              "wk_rope", "wo"):
        W[k] = np.ascontiguousarray(np.asarray(inputs["L0_attn_" + k], np.float32))
    W1 = np.asarray(inputs["L0_ffn_W1"], np.float32)
    W3 = np.asarray(inputs["L0_ffn_W3"], np.float32)
    W2 = np.asarray(inputs["L0_ffn_W2"], np.float32)
    W["w1til"] = np.ascontiguousarray(W1.reshape(64, 128, 16, 128).transpose(0, 2, 3, 1))
    W["w3til"] = np.ascontiguousarray(W3.reshape(64, 128, 16, 128).transpose(0, 2, 3, 1))
    W["w2r"] = np.ascontiguousarray(W2.T).reshape(128, 64 * D)
    H64 = _sylvester(64)
    W["h64i2"] = np.ascontiguousarray(np.kron(H64, np.eye(2, dtype=np.float32)))
    W["h128"] = _sylvester(128)
    W["idin"] = np.eye(128, dtype=np.float32)
    return W


def kernel(**inputs):
    x = np.asarray(inputs["L0_attn_x"], np.float32)
    W = make_shared_weights(inputs)
    nc = build_program(upto=4)
    in_maps = [make_core_inputs(x, W, c) for c in range(8)]
    res = run_bass_kernel_spmd(nc, in_maps, core_ids=list(range(8)))
    y = np.empty((B, L, D), np.float32)
    for c in range(8):
        y[c // 4, c % 4::4, :] = res.results[c]["out"]
    return y
